# version 24
# baseline (speedup 1.0000x reference)
import contextlib

import numpy as np
import ml_dtypes

import concourse.bass as bass
import concourse.mybir as mybir

F32 = mybir.dt.float32
BF16 = mybir.dt.bfloat16
U8 = mybir.dt.uint8
AF = mybir.ActivationFunctionType
ALU = mybir.AluOpType
AX = mybir.AxisListType

BF16NP = ml_dtypes.bfloat16

ENGINES = ("sync", "gp", "ve", "sc", "pe")

NEG_BIG = -1.0e30



class _Op:
    __slots__ = ("idx", "eng", "emit", "dma", "deps", "inc", "cum", "waits")

    def __init__(self, idx, eng, emit, dma):
        self.idx = idx
        self.eng = eng
        self.emit = emit
        self.dma = dma
        self.deps = set()
        self.inc = False
        self.cum = 0
        self.waits = None


def _key(item):
    if isinstance(item, str):
        return item
    if isinstance(item, tuple):
        return item[0]
    try:
        return item.name
    except Exception:
        return item.tensor.name


NDMA_LANES = 8


class Prog:
    def __init__(self, nc):
        self.nc = nc
        self.ops = []
        self.last_writer = {}
        self.readers = {}
        self._lane_ctr = {}
        self._lane_last = {}

    def op(self, eng, emit, reads=(), writes=(), dma=False):
        idx = len(self.ops)
        if dma:
            c = self._lane_ctr.get(eng, 0)
            self._lane_ctr[eng] = c + 1
            lane = c % NDMA_LANES
            dma = f"d{lane}"
        o = _Op(idx, eng, emit, dma)
        if dma:
            prev = self._lane_last.get((eng, dma))
            if prev is not None:
                o.deps.add(prev)
            self._lane_last[(eng, dma)] = idx
        rk = [_key(r) for r in reads]
        wk = [_key(w) for w in writes]
        for r in rk:
            lw = self.last_writer.get(r)
            if lw is not None:
                o.deps.add(lw)
        for w in wk:
            lw = self.last_writer.get(w)
            if lw is not None:
                o.deps.add(lw)
            for rd in self.readers.get(w, ()):
                o.deps.add(rd)
        for r in rk:
            self.readers.setdefault(r, []).append(idx)
        for w in wk:
            self.last_writer[w] = idx
            self.readers[w] = []
        o.deps.discard(idx)
        self.ops.append(o)
        return idx

    def dma(self, queue, out, in_, reads=(), writes=(), **kw):
        self.op(queue, lambda e: e.dma_start(out=out, in_=in_, **kw),
                reads=list(reads), writes=list(writes), dma=True)

    def fence(self, keys):
        self.op("ve", lambda v: v.engine_nop(), writes=list(keys))

    def fence_all(self):
        keys = set(self.last_writer) | set(self.readers)
        self.op("ve", lambda v: v.engine_nop(), writes=sorted(keys))

    def emit(self, stack, tail="sync"):
        nc = self.nc
        ops = self.ops
        for o in ops:
            if o.dma:
                o.inc = True
        for o in ops:
            for d in o.deps:
                p = ops[d]
                if p.eng == "pe" and o.eng == "pe":
                    continue
                p.inc = True
        cum = {}
        for o in ops:
            if o.inc:
                key = (o.eng, o.dma)
                cum[key] = cum.get(key, 0) + (16 if o.dma else 1)
                o.cum = cum[key]
        stream_vc = {e: {} for e in ENGINES}
        op_vc = [None] * len(ops)
        for o in ops:
            vce = stream_vc[o.eng]
            waits = {}
            for d in sorted(o.deps):
                p = ops[d]
                if p.eng == "pe" and o.eng == "pe":
                    continue
                key = (p.eng, p.dma)
                need = p.cum
                if vce.get(key, 0) >= need or waits.get(key, 0) >= need:
                    continue
                waits[key] = need
                pv = op_vc[d]
                if pv:
                    for k2, v2 in pv.items():
                        if vce.get(k2, 0) < v2:
                            vce[k2] = v2
            for k2, v2 in waits.items():
                if vce.get(k2, 0) < v2:
                    vce[k2] = v2
            o.waits = waits
            snap = dict(vce)
            if o.inc:
                k = (o.eng, o.dma)
                snap[k] = max(snap.get(k, 0), o.cum)
            op_vc[o.idx] = snap

        sems = {}
        for key in cum:
            sems[key] = stack.enter_context(
                nc.semaphore(f"s_{key[0]}_{key[1] or 'c'}"))
        streams = {e: [] for e in ENGINES}
        for o in ops:
            streams[o.eng].append(o)

        block = stack.enter_context(nc.Block())

        def mk(engname):
            def body(eng):
                for o in streams[engname]:
                    for key, val in sorted(o.waits.items(), key=lambda kv: (kv[0][0], str(kv[0][1]))):
                        eng.wait_ge(sems[key], val)
                    inst = o.emit(eng)
                    if o.inc:
                        inst.then_inc(sems[(o.eng, o.dma)], 16 if o.dma else 1)
                if engname == tail:
                    for key, val in sorted(cum.items(), key=lambda kv: (kv[0][0], str(kv[0][1]))):
                        if key[0] == engname and not key[1]:
                            continue
                        eng.wait_ge(sems[key], val)
            return body

        block.sync(mk("sync"))
        block.vector(mk("ve"))
        block.scalar(mk("sc"))
        block.tensor(mk("pe"))
        block.gpsimd(mk("gp"))


class Ctx:
    def __init__(self, nc, stack):
        self.nc = nc
        self.stack = stack
        self.p = Prog(nc)
        self.n = 0
        self.ident = None
        self.prefix = ""
        self.pre = {}
        self.garena = None
        self.gcur = 0
        self.gpsum = None
        self.gpsum_i = 0

    def enable_shared(self, stack):
        nc = self.nc
        self.gpsum = [
            stack.enter_context(nc.psum_tensor("Gpsm", [128, 2048], F32)),
            stack.enter_context(nc.psum_tensor("Gpso", [128, 512], F32)),
            stack.enter_context(nc.psum_tensor("Gpsb", [128, 512], F32)),
            stack.enter_context(nc.psum_tensor("Gpst0", [128, 512], F32)),
            stack.enter_context(nc.psum_tensor("Gpst1", [128, 512], F32)),
        ]
        avail = (nc.sbuf_bytes_remaining - 512) // 32 * 32
        self.garena = stack.enter_context(
            nc.sbuf_tensor("GARENA", [128, avail], U8))
        self.gbytes = avail

    def block_reset(self):
        self.gcur = 0
        self.gpsum_i = 0

    def avail_bytes(self):
        if self.garena is not None:
            return self.gbytes - ((self.gcur + 31) // 32 * 32)
        return self.nc.sbuf_bytes_remaining

    def sb(self, shape, dtype, name=None):
        self.n += 1
        name = self.prefix + (name or f"sb{self.n}")
        if self.garena is None:
            return self.stack.enter_context(
                self.nc.sbuf_tensor(name, shape, dtype))
        assert len(shape) == 2 and shape[0] <= 128, shape
        nb = shape[1] * mybir.dt.size(dtype)
        off = (self.gcur + 31) // 32 * 32
        self.gcur = off + nb
        assert self.gcur <= self.gbytes, (name, self.gcur, self.gbytes)
        return carve(self.garena[:, :], off, shape[1], dtype)[0:shape[0], :]

    def ps(self, shape, dtype, name=None):
        self.n += 1
        name = self.prefix + (name or f"ps{self.n}")
        if self.gpsum is None:
            return self.stack.enter_context(
                self.nc.psum_tensor(name, shape, dtype))
        t = self.gpsum[self.gpsum_i]
        self.gpsum_i += 1
        assert list(t.shape) == list(shape) and dtype == F32, (shape, dtype)
        return t


def carve(arena_ap, byte_off, cols, dtype):
    size = mybir.dt.size(dtype)
    return arena_ap[:, byte_off:byte_off + cols * size].bitcast(dtype)



def emit_transposes(ctx, srcs, dsts, pst_bf, pst_keys):
    p = ctx.p
    nb = len(srcs)
    bi = 0
    i = 0
    while i < nb:
        g = min(8, nb - i)
        pt = pst_bf[bi % 2]
        ptk = pst_keys[bi % 2]
        for j in range(g):
            sap, sk = srcs[i + j]
            p.op("pe",
                 (lambda t, o=pt, jj=j, a=sap: t.transpose(
                     o[:, jj * 128:(jj + 1) * 128], a, ctx.ident[:, :])),
                 reads=[sk, "c_ident"], writes=[ptk])
        if len(dsts[i]) == 3 and g > 1 and all(
                d[0] is dsts[i][0] and d[2] == dsts[i][2]
                and d[1] == dsts[i][1] + 128 * jj
                for jj, d in enumerate(dsts[i:i + g])):
            par, c0, dk = dsts[i]
            p.op("ve",
                 (lambda v, o=pt, g=g, par=par, c0=c0: v.tensor_copy(
                     par[:, c0:c0 + g * 128], o[:, 0:g * 128])),
                 reads=[ptk], writes=[dk])
        else:
            for j in range(g):
                d = dsts[i + j]
                if len(d) == 3:
                    dap, dk = d[0][:, d[1]:d[1] + 128], d[2]
                else:
                    dap, dk = d
                p.op("ve",
                     (lambda v, o=pt, jj=j, a=dap: v.tensor_copy(
                         a, o[:, jj * 128:(jj + 1) * 128])),
                     reads=[ptk], writes=[dk])
        bi += 1
        i += g


def emit_ln(ctx, yr_ap, yr_key, out_ap, out_key, scr, width):
    p = ctx.p
    s1, ss, mean, ms, var, rstd, sqd = (
        scr["s1"], scr["ss"], scr["mean"], scr["ms"], scr["var"], scr["rstd"],
        scr["sqd"])
    p.op("ve", lambda v: v.reduce_sum(s1[:, :], yr_ap, AX.X),
         reads=[yr_key], writes=["ln_s1"])
    p.op("sc", lambda s: s.activation(sqd[:, :width], yr_ap, AF.Square,
                                      accum_out=ss[:, 0:1]),
         reads=[yr_key], writes=["ln_sqd", "ln_ss"])
    p.op("ve", lambda v: v.tensor_scalar(mean[:, :], s1[:, :], 1.0 / width, None,
                                         ALU.mult),
         reads=["ln_s1"], writes=["ln_mean"])
    p.op("ve", lambda v: v.tensor_mul(ms[:, :], mean[:, :], mean[:, :]),
         reads=["ln_mean"], writes=["ln_ms"])
    p.op("ve", lambda v: v.scalar_tensor_tensor(var[:, :], ss[:, :], 1.0 / width,
                                                ms[:, :], ALU.mult, ALU.subtract),
         reads=["ln_ss", "ln_ms"], writes=["ln_var"])
    p.op("sc", lambda s: s.activation(rstd[:, :], var[:, :], AF.Ln,
                                      bias=ctx.eps_col[:, 0:1]),
         reads=["ln_var", "c_eps"], writes=["ln_rstd"])
    p.op("sc", lambda s: s.activation(rstd[:, :], rstd[:, :], AF.Exp,
                                      scale=-0.5),
         reads=["ln_rstd"], writes=["ln_rstd"])
    p.op("ve", lambda v: v.tensor_scalar(out_ap, yr_ap, mean[:, 0:1],
                                         rstd[:, 0:1], ALU.subtract, ALU.mult),
         reads=[yr_key, "ln_mean", "ln_rstd"], writes=[out_key])



def emit_l0_attn(ctx, cfg, io):
    nc, p = ctx.nc, ctx.p
    xin_key = io.get("xin_key") or (lambda s, n: None)
    xout_key = io.get("xout_key") or (lambda s, n: f"x_out@{s}_{n}")
    T, D, H, B = cfg["T"], cfg["D"], cfg["H"], cfg["B"]
    DH = 64
    assert D // H == DH
    NT = T // 128
    DC = D // 128
    half = D // 2
    QB = min(512, T)
    NQB = T // QB
    NQT = QB // 128
    KB = min(512, T)
    NK5 = T // KB
    NKC = T // 128
    G64 = D // 64
    W5 = min(512, D)
    NH5 = D // W5

    ctx.ident = ctx.sb([128, 128], BF16, "identsb")
    p.dma("sync", ctx.ident[:, :], io["ident"], writes=[("c_ident", None)])
    ctx.eps_col = ctx.sb([128, 1], F32, "eps_col")
    p.op("ve", lambda v: v.memset(ctx.eps_col[:, :], 1e-5), writes=["c_eps"])
    ones1 = ctx.sb([1, 128], BF16, "ones1")
    p.op("ve", lambda v: v.memset(ones1[:, :], 1.0), writes=["c_ones1"])

    psm = ctx.ps([128, 2048], F32, "psm")
    pso = ctx.ps([128, 512], F32, "pso")
    psb = ctx.ps([128, 512], F32, "psb")
    pst = [ctx.ps([128, 512], F32, "pst0"), ctx.ps([128, 512], F32, "pst1")]
    pst_bf = [t[:, :].bitcast(BF16) for t in pst]
    PSTK = ["pst0", "pst1"]

    def psm_keys(c0, c1):
        return [f"psm{b}" for b in range(c0 // 512, (c1 + 511) // 512)]

    arena = ctx.sb([128, 40960], U8, "arenaA")
    ar = arena[:, :]
    xbf4 = [carve(ar, 0, 4 * D, BF16), carve(ar, 8192, 4 * D, BF16)]
    wb4 = carve(ar, 16384, 4 * D, BF16)
    r4 = carve(ar, 24576, 4 * D, BF16)
    cosb = [carve(ar, 32768, half, F32), carve(ar, 32768 + 4096, half, F32)]
    sinb = [carve(ar, 32768 + 2048, half, F32),
            carve(ar, 32768 + 6144, half, F32)]
    QKV_KEYS = ["xbf4_0", "xbf4_1", "wb4", "r4", "cos0", "cos1", "sin0", "sin1"]
    xr = [carve(ar, 0, D, F32), carve(ar, 4096, D, F32)]
    yrt = carve(ar, 8192, D, F32)
    ot = [carve(ar, 12288, D, F32), carve(ar, 16384, D, F32)]
    sqd = carve(ar, 20480, D, F32)
    buB = carve(ar, 24576, D, F32)
    bu_row = carve(ar, 28672, D, F32)[0:1, :]
    bu_bf = carve(ar, 32768, D, BF16)[0:1, :]
    EPI_KEYS = ["xr0", "xr1", "yr", "ot0", "ot1", "ln_sqd", "buB", "bu_row",
                "bu_bf"]

    XT = [ctx.sb([128, T], BF16, f"XTt{c}") for c in range(DC)]
    XTK = [f"XT{c}" for c in range(DC)]
    OTb = [XT[c][:, 0:T] for c in range(DC)]
    PTt = [carve(ar, 34816, 1024, BF16), carve(ar, 36864, 1024, BF16),
           carve(ar, 0, 1024, BF16), carve(ar, 2048, 1024, BF16)]
    rec2 = [carve(ar, 38912, 512, F32)[0:64, :],
            ctx.sb([64, 512], F32, "recb")[:, :]]
    OTK = [f"OT{c}" for c in range(DC)]
    PTK = ["PT0", "PT1", "PT2", "PT3"]
    ATT_KEYS = OTK + PTK + ["rec0", "rec1"]

    nmax = ctx.sb([128, 12], F32, "nmax")
    hsum = ctx.sb([128, 16], F32, "hsum")
    qn2 = ctx.sb([128, 16], F32, "qn2")
    kn2 = ctx.sb([128, 16], F32, "kn2")
    nrmsc = ctx.sb([128, 16], F32, "nrmsc")
    negcB = ctx.sb([128, 16], F32, "negcB")
    ctx.crow = ctx.sb([1, 64], F32, "crow")
    pf1 = ctx.sb([128, D], F32, "projf0")
    rt1_1 = ctx.sb([128, half], F32, "ropet1_0")
    rt2_1 = ctx.sb([128, half], F32, "ropet2_0")
    pf = [pf1, pf1]
    rt1 = [rt1_1, rt1_1]
    rt2 = [rt2_1, rt2_1]
    scr = {k: ctx.sb([128, 1], F32, f"lns_{k}") for k in
           ("s1", "ss", "mean", "ms", "var", "rstd")}
    scr["sqd"] = sqd

    QT = [ctx.sb([128, T], BF16, f"QTt{c}") for c in range(DC)]
    KTt = [ctx.sb([128, T], BF16, f"KTt{c}") for c in range(DC)]
    V = [ctx.sb([128, D], BF16, f"Vt{n}") for n in range(NT)]
    wt = [ctx.sb([128, D], BF16, f"wtA{c}") for c in range(DC)]
    Vst = [ctx.sb([128, T], BF16, f"Vst{i}") for i in range(2)]
    for i in range(2):
        p.op("ve", (lambda v, i=i: v.memset(Vst[i][:, :], 1.0)),
             writes=[f"Vst{i}"])

    def prep_wt(wname, dram_ap):
        src_r = dram_ap.rearrange("(n p) i -> p n i", p=128)
        for g in range(0, DC, 4):
            gl = min(4, DC - g)
            p.dma("gp", wb4[:, 0:gl * D].rearrange("p (n i) -> p n i", n=gl),
                  src_r[:, g:g + gl, :], writes=["wb4"])
            for c in range(DC):
                srcs = [(wb4[:, j * D + c * 128: j * D + (c + 1) * 128], "wb4")
                        for j in range(gl)]
                par = wt[c][:, :]
                dsts = [(par, (g + j) * 128, f"wt{c}")
                        for j in range(gl)]
                emit_transposes(ctx, srcs, dsts, pst_bf, PSTK)

    for s in range(B):
        x_seq = io["x_in"][s]
        xo_seq = io["x_out"][s]
        x_r = x_seq.rearrange("(m p) d -> p m d", p=128)
        xo_r = xo_seq.rearrange("(m p) d -> p m d", p=128)

        p.fence(QKV_KEYS + EPI_KEYS + XTK + ATT_KEYS)

        for g in range(0, NT, 4):
            gl = min(4, NT - g)
            xb = xbf4[(g // 4) % 2]
            xbk = f"xbf4_{(g // 4) % 2}"
            p.dma("gp", xb[:, 0:gl * D].rearrange("p (n d) -> p n d", n=gl),
                  x_r[:, g:g + gl, :],
                  reads=[k for k in (xin_key(s, g + j) for j in range(gl))
                         if k], writes=[xbk])
            for c in range(DC):
                srcs = [(xb[:, j * D + c * 128: j * D + (c + 1) * 128], xbk)
                        for j in range(gl)]
                par = XT[c][:, :]
                dsts = [(par, (g + j) * 128, XTK[c])
                        for j in range(gl)]
                emit_transposes(ctx, srcs, dsts, pst_bf, PSTK)

        def proj_pass(wname, kind):
            dst_feat = QT if kind == "q" else KTt if kind == "k" else None
            dfk = "QT" if kind == "q" else "KT"
            for g in range(0, NT, 4):
                gl = min(4, NT - g)
                for j in range(gl):
                    n = g + j
                    if kind in ("q", "k"):
                        cb, sbn = cosb[n % 2], sinb[n % 2]
                        ck, sk = f"cos{n % 2}", f"sin{n % 2}"
                        p.dma("sync", cb[:, :],
                              io["rope_cos"].rearrange("(m p) h -> p m h",
                                                       p=128)[:, n, :],
                              writes=[ck])
                        p.dma("sync", sbn[:, :],
                              io["rope_sin"].rearrange("(m p) h -> p m h",
                                                       p=128)[:, n, :],
                              writes=[sk])
                    pbase = (n % 2) * D if 2 * D <= 2048 else 0
                    for hf in range(NH5):
                        for ic in range(DC):
                            p.op("pe",
                                 (lambda t, n=n, hf=hf, ic=ic, pb=pbase:
                                  t.matmul(psm[:, pb + hf * W5:
                                               pb + (hf + 1) * W5],
                                           XT[ic][:, n * 128:(n + 1) * 128],
                                           wt[ic][:, hf * W5:(hf + 1) * W5],
                                           start=(ic == 0),
                                           stop=(ic == DC - 1))),
                                 reads=[XTK[ic], f"wt{ic}"],
                                 writes=psm_keys(pbase + hf * W5,
                                                 pbase + (hf + 1) * W5))
                    pskeys = psm_keys(pbase, pbase + D)
                    if kind == "v":
                        p.op("ve", (lambda v, n=n, pb=pbase: v.tensor_copy(
                            V[n][:, 0:D], psm[:, pb:pb + D])),
                            reads=pskeys, writes=[f"V{n}"])
                        continue
                    pfn, pfk = pf[n % 2], "projf0"
                    r1n, r1k = rt1[n % 2], "ropet1_0"
                    r2n, r2k = rt2[n % 2], "ropet2_0"
                    if kind == "q":
                        p.op("ve", (lambda v, pfn=pfn, pb=pbase:
                                    v.tensor_scalar(
                            pfn[:, :], psm[:, pb:pb + D], 0.125, None,
                            ALU.mult)), reads=pskeys, writes=[pfk])
                    else:
                        p.op("ve", (lambda v, pfn=pfn, pb=pbase: v.tensor_copy(
                            pfn[:, :], psm[:, pb:pb + D])),
                            reads=pskeys, writes=[pfk])
                    pe_ = pfn[:, :].rearrange("p (h two) -> p two h", two=2)
                    ev, od = pe_[:, 0, :], pe_[:, 1, :]
                    dst = r4[:, j * D:(j + 1) * D].rearrange(
                        "p (h two) -> p two h", two=2)
                    dev, dod = dst[:, 0, :], dst[:, 1, :]
                    p.op("ve", (lambda v, ev=ev, cb=cb, r1n=r1n: v.tensor_mul(
                        r1n[:, :], ev, cb[:, :])),
                        reads=[pfk, ck], writes=[r1k])
                    p.op("ve", (lambda v, od=od, sbn=sbn, r2n=r2n:
                                v.tensor_mul(r2n[:, :], od, sbn[:, :])),
                        reads=[pfk, sk], writes=[r2k])
                    p.op("ve", (lambda v, dev=dev, r1n=r1n, r2n=r2n:
                                v.tensor_sub(dev, r1n[:, :], r2n[:, :])),
                        reads=[r1k, r2k], writes=["r4"])
                    p.op("ve", (lambda v, ev=ev, sbn=sbn, r1n=r1n:
                                v.tensor_mul(r1n[:, :], ev, sbn[:, :])),
                        reads=[pfk, sk], writes=[r1k])
                    p.op("ve", (lambda v, od=od, cb=cb, r2n=r2n: v.tensor_mul(
                        r2n[:, :], od, cb[:, :])),
                        reads=[pfk, ck], writes=[r2k])
                    p.op("ve", (lambda v, dod=dod, r1n=r1n, r2n=r2n:
                                v.tensor_add(dod, r1n[:, :], r2n[:, :])),
                        reads=[r1k, r2k], writes=["r4"])
                    nt2, nt2k = (qn2, "qn2") if kind == "q" else (kn2, "kn2")
                    p.op("sc", (lambda s_, j=j, pfn=pfn: s_.activation(
                        pfn[:, :], r4[:, j * D:(j + 1) * D], AF.Square)),
                        reads=["r4"], writes=[pfk])
                    p.op("ve", (lambda v, pfn=pfn: v.tensor_reduce(
                        hsum[:, 0:H], pfn[:, :].rearrange(
                            "p (h c) -> p h c", c=64), AX.X, ALU.add)),
                        reads=[pfk], writes=["hsum"])
                    p.op("ve", (lambda v, nt2=nt2: v.tensor_max(
                        nt2[:, 0:H], nt2[:, 0:H], hsum[:, 0:H])),
                        reads=["hsum", nt2k], writes=[nt2k])
                if kind in ("q", "k"):
                    for c in range(DC):
                        srcs = [(r4[:, j * D + c * 128: j * D + (c + 1) * 128],
                                 "r4") for j in range(gl)]
                        par = dst_feat[c][:, :]
                        dsts = [(par, (g + j) * 128,
                                 f"{dfk}{c}") for j in range(gl)]
                        emit_transposes(ctx, srcs, dsts, pst_bf, PSTK)

        p.op("ve", lambda v: v.memset(qn2[:, 0:H], 0.0), writes=["qn2"])
        p.op("ve", lambda v: v.memset(kn2[:, 0:H], 0.0), writes=["kn2"])
        prep_wt("wq", io["wq"])
        proj_pass("wq", "q")
        prep_wt("wk", io["wk"])
        proj_pass("wk", "k")
        prep_wt("wv", io["wv"])
        proj_pass("wv", "v")
        prep_wt("wu", io["wu"])

        hsbf = hsum[:, 0:H].bitcast(BF16)
        p.op("ve", lambda v: v.tensor_copy(hsbf[:, 0:H], qn2[:, 0:H]),
             reads=["qn2"], writes=["hsum"])
        p.op("ve", lambda v: v.tensor_copy(hsbf[:, H:2 * H], kn2[:, 0:H]),
             reads=["kn2"], writes=["hsum"])
        p.op("pe", lambda t: t.transpose(
            pst_bf[0][0:2 * H, 0:128], hsbf[:, 0:2 * H], ctx.ident[:, :]),
            reads=["hsum", "c_ident"], writes=["pst0"])
        p.op("ve", lambda v: v.tensor_reduce(
            nrmsc[0:2 * H, 0:1], pst_bf[0][0:2 * H, 0:128], AX.X, ALU.max),
            reads=["pst0"], writes=["nrmsc"])
        nrbf = nrmsc[:, 1:2].bitcast(BF16)
        p.op("ve", lambda v: v.tensor_copy(nrbf[0:2 * H, 0:1],
                                           nrmsc[0:2 * H, 0:1]),
             reads=["nrmsc"], writes=["nrmsc"])
        p.op("pe", lambda t: t.transpose(
            pst_bf[1][0:1, 0:2 * H], nrbf[0:2 * H, 0:1],
            ctx.ident[0:2 * H, 0:2 * H]),
            reads=["nrmsc", "c_ident"], writes=["pst1"])
        crow = ctx.crow
        p.op("ve", lambda v: v.tensor_copy(crow[0:1, 0:2 * H],
                                           pst_bf[1][0:1, 0:2 * H]),
             reads=["pst1"], writes=["crow"])
        p.op("ve", lambda v: v.tensor_mul(
            crow[0:1, 32:32 + H], crow[0:1, 0:H], crow[0:1, H:2 * H]),
            reads=["crow"], writes=["crow"])
        p.op("ve", lambda v: v.tensor_scalar_max(
            crow[0:1, 32:32 + H], crow[0:1, 32:32 + H], 1e-12),
            reads=["crow"], writes=["crow"])
        p.op("sc", lambda s_: s_.activation(
            crow[0:1, 32:32 + H], crow[0:1, 32:32 + H], AF.Ln),
            reads=["crow"], writes=["crow"])
        p.op("sc", lambda s_: s_.activation(
            crow[0:1, 32:32 + H], crow[0:1, 32:32 + H], AF.Exp, scale=0.5),
            reads=["crow"], writes=["crow"])
        crbf = crow[:, 48:56].bitcast(BF16)
        p.op("ve", (lambda v: v.tensor_scalar(
            crbf[0:1, 0:H], crow[0:1, 32:32 + H], -1.0, None, ALU.mult)),
            reads=["crow"], writes=["crow"])
        p.op("pe", lambda t: t.matmul(
            psb[:, 0:H], ones1[0:1, :], crbf[0:1, 0:H],
            start=True, stop=True),
            reads=["crow", "c_ones1"], writes=["psb"])
        p.op("ve", lambda v: v.tensor_copy(negcB[:, 0:H], psb[:, 0:H]),
             reads=["psb"], writes=["negcB"])

        p.fence(QKV_KEYS + EPI_KEYS + XTK + ATT_KEYS)

        p.dma("sync", bu_row[0:1, :], io["bu"].unsqueeze(0), writes=["bu_row"])
        p.op("ve", lambda v: v.tensor_copy(bu_bf[0:1, :], bu_row[0:1, :]),
             reads=["bu_row"], writes=["bu_bf"])
        for i in range(NH5):
            p.op("pe", (lambda t, i=i: t.matmul(
                psb[:, 0:W5], ones1[0:1, :], bu_bf[0:1, i * W5:(i + 1) * W5],
                start=True, stop=True)),
                reads=["bu_bf", "c_ones1"], writes=["psb"])
            p.op("ve", (lambda v, i=i: v.tensor_copy(
                buB[:, i * W5:(i + 1) * W5], psb[:, 0:W5])),
                reads=["psb"], writes=["buB"])

        for ch in range(H // 2):
            heads = [2 * ch, 2 * ch + 1]
            for h in heads:
                vs, vsk = Vst[h % 2], f"Vst{h % 2}"
                for kc in range(NKC):
                    p.op("gp", (lambda g, kc=kc, h=h, vs=vs: g.tensor_copy(
                        vs[:, kc * 128:kc * 128 + 64],
                        V[kc][:, h * 64:(h + 1) * 64])),
                        reads=[f"V{kc}"], writes=[vsk])
            for qb in range(NQB):
                pend = []
                for kc in range(NKC):
                    if QB == 512 and kc % 2 == 1:
                        continue
                    kcs = (kc, kc + 1) if QB == 512 else (kc,)
                    pp = (kc // len(kcs)) % 2
                    for h in heads:
                        hb = (h % 2) * 64
                        bb = (h % 2) * 2
                        for jj, kcj in enumerate(kcs):
                            bank = psm[:, (bb + jj) * 512:
                                       (bb + jj) * 512 + QB]
                            p.op("pe", (lambda t, kcj=kcj, ch=ch, hb=hb,
                                        qb=qb, bank=bank: t.matmul(
                                            bank,
                                            KTt[ch][hb:hb + 64,
                                                    kcj * 128:
                                                    (kcj + 1) * 128],
                                            QT[ch][hb:hb + 64,
                                                   qb * QB:(qb + 1) * QB],
                                            start=True, stop=True)),
                                reads=[f"QT{ch}", f"KT{ch}"],
                                writes=[f"psm{bb + jj}"])
                    for h in heads:
                        hb = (h % 2) * 64
                        bb = (h % 2) * 2
                        vs, vsk = Vst[h % 2], f"Vst{h % 2}"
                        pvo = pso if h % 2 == 0 else psb
                        pvk = "pso" if h % 2 == 0 else "psb"
                        pt = (h % 2) * 2 + pp
                        pb, ptk = PTt[pt], PTK[pt]
                        w = len(kcs) * QB
                        p.op("sc", (lambda s_, h=h, bb=bb, pb=pb, w=w:
                                    s_.activation(
                                        pb[:, 0:w],
                                        psm[:, bb * 512:bb * 512 + w],
                                        AF.Exp, bias=negcB[:, h:h + 1])),
                            reads=[f"psm{bb + j}" for j in
                                   range(len(kcs))] + ["negcB"],
                            writes=[ptk])
                        for j, kcj in enumerate(kcs):
                            pend.append((lambda t, kcj=kcj, j=j, pb=pb,
                                         vs=vs, pvo=pvo: t.matmul(
                                pvo[:, 0:NQT * 128],
                                vs[:, kcj * 128:(kcj + 1) * 128],
                                pb[:, j * QB:(j + 1) * QB],
                                start=(kcj == 0), stop=(kcj == NKC - 1)),
                                [vsk, ptk], [pvk]))
                    nkeep = 2 * len(kcs)
                    if len(pend) > nkeep:
                        for em, rd, wr in pend[:-nkeep]:
                            p.op("pe", em, reads=rd, writes=wr)
                        pend = pend[-nkeep:]
                for em, rd, wr in pend:
                    p.op("pe", em, reads=rd, writes=wr)
                for h in heads:
                    hb = (h % 2) * 64
                    pvo = pso if h % 2 == 0 else psb
                    pvk = "pso" if h % 2 == 0 else "psb"
                    rc, rck = rec2[h % 2], f"rec{h % 2}"
                    p.op("sc", (lambda s_, pvo=pvo, rc=rc: s_.activation(
                        rc[0:64, 0:NQT * 128], pvo[64:128, 0:NQT * 128],
                        AF.Ln)),
                        reads=[pvk], writes=[rck])
                    p.op("sc", (lambda s_, rc=rc: s_.activation(
                        rc[0:64, 0:NQT * 128], rc[0:64, 0:NQT * 128], AF.Exp,
                        scale=-1.0)),
                        reads=[rck], writes=[rck])
                    p.op("ve", (lambda v, ch=ch, hb=hb, pvo=pvo, rc=rc,
                                qb=qb: v.tensor_tensor(
                        OTb[ch][hb:hb + 64, qb * QB:(qb + 1) * QB],
                        pvo[0:64, 0:NQT * 128],
                        rc[0:64, 0:NQT * 128], ALU.mult)),
                        reads=[pvk, rck], writes=[OTK[ch]])

        p.fence(["PT2", "PT3", "xr0", "xr1"])
        if True:
            for n in range(NT):
                xk = f"xr{n % 2}"
                p.dma("sync", xr[n % 2][:, :], x_r[:, n, :],
                      reads=[k for k in [xin_key(s, n)] if k], writes=[xk])
                pbase = (n % 2) * D if 2 * D <= 2048 else 0
                for hf in range(NH5):
                    for ic in range(DC):
                        p.op("pe", (lambda t, n=n, hf=hf, ic=ic, pb=pbase:
                                    t.matmul(
                            psm[:, pb + hf * W5:pb + (hf + 1) * W5],
                            OTb[ic][:, n * 128:(n + 1) * 128],
                            wt[ic][:, hf * W5:(hf + 1) * W5],
                            start=(ic == 0), stop=(ic == DC - 1))),
                            reads=[OTK[ic], f"wt{ic}"],
                            writes=psm_keys(pbase + hf * W5,
                                            pbase + (hf + 1) * W5))
                pskeys = psm_keys(pbase, pbase + D)
                p.op("ve", (lambda v, pb=pbase: v.tensor_tensor(
                    yrt[:, :], psm[:, pb:pb + D], buB[:, :], ALU.add)),
                    reads=pskeys + ["buB"], writes=["yr"])
                p.op("ve", (lambda v, n=n: v.tensor_add(
                    yrt[:, :], yrt[:, :], xr[n % 2][:, :])),
                    reads=["yr", xk], writes=["yr"])
                ob, obk = ot[n % 2], f"ot{n % 2}"
                emit_ln(ctx, yrt[:, :], "yr", ob[:, :], obk, scr, D)
                p.dma("gp", xo_r[:, n, :], ob[:, :], reads=[obk],
                      writes=[xout_key(s, n)])



def emit_l0_ffn(ctx, cfg, io):
    nc, p = ctx.nc, ctx.p
    _NTs = cfg["T"] // 128
    xin_key = io.get("xin_key") or (lambda s, n: None)
    xout_key = io.get("xout_key") or (lambda s, n: f"x_out@{s}_{n}")
    xin_g = lambda m: xin_key(m // _NTs, m % _NTs)
    xout_g = lambda m: xout_key(m // _NTs, m % _NTs)
    T, D, B = cfg["T"], cfg["D"], cfg["B"]
    F1, F2 = cfg["F1"], cfg["F2"]
    TT = B * T
    TB = min(512, TT)
    NTB = TT // TB
    NTT = TB // 128
    DC = D // 128
    NO1 = F1 // 128
    NO2 = F2 // 128
    assert DC <= 8
    SWK = min(8, NO2)
    SW2 = (NO2 + SWK - 1) // SWK

    ctx.ident = ctx.sb([128, 128], BF16, "identsb")
    p.dma("sync", ctx.ident[:, :], io["ident"], writes=[("c_ident", None)])
    ctx.eps_col = ctx.sb([128, 1], F32, "eps_col")
    p.op("ve", lambda v: v.memset(ctx.eps_col[:, :], 1e-5), writes=["c_eps"])
    onescol = ctx.sb([128, 1], BF16, "onescol")
    p.op("ve", lambda v: v.memset(onescol[:, :], 1.0), writes=["c_onescol"])
    neghalf = ctx.sb([1, 128], BF16, "neghalf")
    p.op("ve", lambda v: v.memset(neghalf[:, :], -0.5), writes=["c_neghalf"])

    scol = {}
    acol = {}
    for nm, mul in (("s0", 2.0), ("s1", 4.0), ("s2", 4.0)):
        scol[nm] = ctx.sb([128, 1], F32, f"{nm}col")
        acol[nm] = ctx.sb([128, 1], F32, f"{nm}acol")
        p.dma("sync", scol[nm][:, :], io[nm + "c"], writes=[f"{nm}c"])
        p.op("ve", (lambda v, nm=nm, mul=mul: v.tensor_scalar(
            acol[nm][:, :], scol[nm][:, :], mul, None, ALU.mult)),
            reads=[f"{nm}c"], writes=[f"{nm}a"])

    psm = ctx.ps([128, 2048], F32, "psm")
    pso = ctx.ps([128, 512], F32, "pso")
    psb = ctx.ps([128, 512], F32, "psb")
    pst = [ctx.ps([128, 512], F32, "pst0"), ctx.ps([128, 512], F32, "pst1")]
    pst_bf = [t[:, :].bitcast(BF16) for t in pst]
    PSTK = ["pst0", "pst1"]
    banks = [(psm[:, i * 512:(i + 1) * 512], f"psm{i}") for i in range(4)] + \
            [(pso[:, :], "pso"), (psb[:, :], "psb"),
             (pst[0][:, :], "pst0"), (pst[1][:, :], "pst1")]

    K0d = [ctx.sb([128, F1], BF16, f"K0d{c}") for c in range(DC)]
    h1T = [ctx.sb([128, TB], BF16, f"h1T{c}") for c in range(NO1)]
    h2T = [ctx.sb([128, TB], BF16, f"h2T{c}") for c in range(NO2)]
    XT2 = [[ctx.sb([128, TB], BF16, f"XTt{b}_{c}") for c in range(DC)]
           for b in range(2)]
    ySB = [ctx.sb([128, TB], BF16, f"ySB{c}") for c in range(DC)]
    k1sb = [ctx.sb([128, SWK * 128], BF16, f"k1sb{i}") for i in range(2)]
    k2sb = [ctx.sb([128, D], BF16, f"k2sb{i}") for i in range(2)]
    _xb1 = ctx.sb([128, 4 * D], BF16, "xbf4_0")
    xbf4 = [_xb1, _xb1]
    xr = [ctx.sb([128, D], F32, f"xr{i}") for i in range(2)]
    yrt = ctx.sb([128, D], F32, "yrt")
    ot = [ctx.sb([128, D], F32, f"ott{i}") for i in range(2)]
    scr = {k: ctx.sb([128, 1], F32, f"lns_{k}") for k in
           ("s1", "ss", "mean", "ms", "var", "rstd")}
    scr["sqd"] = ctx.sb([128, D], F32, "lnsqd")
    sg32 = [ctx.sb([1, TB], F32, "sg32a"), ctx.sb([1, TB], F32, "sg32b")]
    sghi = [ctx.sb([1, TB], BF16, "sghia"), ctx.sb([1, TB], BF16, "sghib")]
    sgh32 = ctx.sb([1, TB], F32, "sgh32")
    sglo = [ctx.sb([1, TB], BF16, "sgloa"), ctx.sb([1, TB], BF16, "sglob")]

    if "k1bf" in ctx.pre:
        k1bf, k2bf = ctx.pre["k1bf"], ctx.pre["k2bf"]
    else:
        k1bf = nc.dram_tensor(ctx.prefix + "k1bf", [F1, F2], BF16,
                              kind="Internal").ap()
        k2bf = nc.dram_tensor(ctx.prefix + "k2bf", [F2, D], BF16,
                              kind="Internal").ap()
        for q in range(4):
            r0, r1 = q * F1 // 4, (q + 1) * F1 // 4
            p.dma("gp", k1bf[r0:r1, :], io["k1"][r0:r1, :],
                  writes=[f"k1bf{q}"])
            r0, r1 = q * F2 // 4, (q + 1) * F2 // 4
            p.dma("gp", k2bf[r0:r1, :], io["k2"][r0:r1, :],
                  writes=[f"k2bf{q}"])

    def k1bf_key(ic):
        return f"k1bf{(ic * 128) * 4 // F1}"

    def k2bf_key(ic):
        return f"k2bf{(ic * 128) * 4 // F2}"

    KW = SWK * 128
    for ic in range(DC):
        for q0 in range(0, F1, KW):
            qw = min(KW, F1 - q0)
            p.dma("gp", k1sb[0][:, 0:qw],
                  io["k0"][ic * 128:(ic + 1) * 128, q0:q0 + qw],
                  writes=["k1sb0"])
            p.dma("gp", k1sb[1][:, 0:qw],
                  io["k0"][D + ic * 128:D + (ic + 1) * 128, q0:q0 + qw],
                  writes=["k1sb1"])
            p.op("ve", (lambda v, ic=ic, q0=q0, qw=qw: v.tensor_sub(
                K0d[ic][:, q0:q0 + qw], k1sb[0][:, 0:qw], k1sb[1][:, 0:qw])),
                reads=["k1sb0", "k1sb1"], writes=[f"K0d{ic}"])

    x_flat = io["x_in"].rearrange("b t d -> (b t) d")
    xo_flat = io["x_out"].rearrange("b t d -> (b t) d")
    x_r = x_flat.rearrange("(m p) d -> p m d", p=128)
    xo_r = xo_flat.rearrange("(m p) d -> p m d", p=128)

    def sigma_rows(src_tiles, src_keys, idx):
        n = len(src_tiles)
        for i in range(n):
            p.op("pe", (lambda t, i=i: t.matmul(
                pso[0:1, 0:TB], onescol[:, 0:1], src_tiles[i][:, 0:TB],
                start=(i == 0), stop=(i == n - 1))),
                reads=[src_keys[i], "c_onescol"], writes=["pso"])
        g32, ghi, glo = sg32[idx], sghi[idx], sglo[idx]
        kk = ("sg1", "sghi1", "sglo1") if idx == 0 else ("sg2", "sghi2", "sglo2")
        p.op("ve", lambda v: v.tensor_copy(g32[0:1, :], pso[0:1, 0:TB]),
             reads=["pso"], writes=[kk[0]])
        p.op("ve", lambda v: v.tensor_copy(ghi[0:1, :], g32[0:1, :]),
             reads=[kk[0]], writes=[kk[1]])
        p.op("ve", lambda v: v.tensor_copy(sgh32[0:1, :], ghi[0:1, :]),
             reads=[kk[1]], writes=["sgh32"])
        p.op("ve", lambda v: v.tensor_sub(glo[0:1, :], g32[0:1, :],
                                          sgh32[0:1, :]),
             reads=[kk[0], "sgh32"], writes=[kk[2]])
        return ghi, glo, kk[1], kk[2]

    for tb in range(NTB):
        t0 = tb * NTT
        XT = XT2[tb % 2]
        XK = [f"XT{tb % 2}_{c}" for c in range(DC)]
        for g in range(0, NTT, 4):
            gl = min(4, NTT - g)
            xb = xbf4[(g // 4) % 2]
            xbk = "xbf4_0"
            p.dma("gp", xb[:, 0:gl * D].rearrange("p (n d) -> p n d", n=gl),
                  x_r[:, t0 + g:t0 + g + gl, :],
                  reads=[k for k in (xin_g(t0 + g + j) for j in range(gl))
                         if k], writes=[xbk])
            for c in range(DC):
                srcs = [(xb[:, j * D + c * 128: j * D + (c + 1) * 128], xbk)
                        for j in range(gl)]
                par = XT[c][:, :]
                dsts = [(par, (g + j) * 128, XK[c])
                        for j in range(gl)]
                emit_transposes(ctx, srcs, dsts, pst_bf, PSTK)

        for oc in range(NO1):
            bank, bk = banks[4 + oc % 2]
            for ic in range(DC):
                p.op("pe", (lambda t, oc=oc, ic=ic, bank=bank, XT=XT:
                            t.matmul(
                    bank[:, 0:TB], K0d[ic][:, oc * 128:(oc + 1) * 128],
                    XT[ic][:, 0:TB], start=(ic == 0), stop=(ic == DC - 1))),
                    reads=[f"K0d{ic}", XK[ic]], writes=[bk])
            p.op("sc", (lambda s, oc=oc, bank=bank: s.activation(
                h1T[oc][:, 0:TB], bank[:, 0:TB], AF.Relu,
                scale=acol["s0"][:, 0:1])),
                reads=[bk, "s0a"], writes=[f"h1T{oc}"])

        ghi1, glo1, hk1, lk1 = sigma_rows(h1T, [f"h1T{i}" for i in range(NO1)], 0)

        for sw in range(SW2):
            oc0 = sw * SWK
            K = min(SWK, NO2 - oc0)
            grp = 0
            for ic in range(NO1):
                kb = k1sb[ic % 2]
                kbk = f"k1sb{ic % 2}"
                p.dma("sync", kb[:, 0:K * 128],
                      k1bf[ic * 128:(ic + 1) * 128,
                           oc0 * 128:(oc0 + K) * 128],
                      reads=[k1bf_key(ic)], writes=[kbk])
                for ocl in range(K):
                    bank, bk = banks[grp + ocl]
                    p.op("pe", (lambda t, ic=ic, ocl=ocl, kb=kb, bank=bank:
                                t.matmul(bank[:, 0:TB],
                                         kb[:, ocl * 128:(ocl + 1) * 128],
                                         h1T[ic][:, 0:TB],
                                         start=(ic == 0), stop=False)),
                        reads=[kbk, f"h1T{ic}"], writes=[bk])
            for ocl in range(K):
                bank, bk = banks[grp + ocl]
                p.op("pe", (lambda t, bank=bank: t.matmul(
                    bank[:, 0:TB], neghalf[0:1, :], ghi1[0:1, :],
                    start=False, stop=False)),
                    reads=["c_neghalf", hk1], writes=[bk])
                p.op("pe", (lambda t, bank=bank: t.matmul(
                    bank[:, 0:TB], neghalf[0:1, :], glo1[0:1, :],
                    start=False, stop=True)),
                    reads=["c_neghalf", lk1], writes=[bk])
                p.op("sc", (lambda s, oc=oc0 + ocl, bank=bank: s.activation(
                    h2T[oc][:, 0:TB], bank[:, 0:TB], AF.Relu,
                    scale=acol["s1"][:, 0:1])),
                    reads=[bk, "s1a"], writes=[f"h2T{oc0 + ocl}"])

        ghi2, glo2, hk2, lk2 = sigma_rows(h2T, [f"h2T{i}" for i in range(NO2)], 1)

        for ic in range(NO2):
            kb = k2sb[ic % 2]
            kbk = f"k2sb{ic % 2}"
            p.dma("sync", kb[:, 0:D],
                  k2bf[ic * 128:(ic + 1) * 128, :],
                  reads=[k2bf_key(ic)], writes=[kbk])
            for oc in range(DC):
                bank, bk = banks[oc]
                p.op("pe", (lambda t, ic=ic, oc=oc, kb=kb, bank=bank:
                            t.matmul(bank[:, 0:TB],
                                     kb[:, oc * 128:(oc + 1) * 128],
                                     h2T[ic][:, 0:TB],
                                     start=(ic == 0), stop=False)),
                    reads=[kbk, f"h2T{ic}"], writes=[bk])
        for oc in range(DC):
            bank, bk = banks[oc]
            p.op("pe", (lambda t, bank=bank: t.matmul(
                bank[:, 0:TB], neghalf[0:1, :], ghi2[0:1, :],
                start=False, stop=False)),
                reads=["c_neghalf", hk2], writes=[bk])
            p.op("pe", (lambda t, bank=bank: t.matmul(
                bank[:, 0:TB], neghalf[0:1, :], glo2[0:1, :],
                start=False, stop=True)),
                reads=["c_neghalf", lk2], writes=[bk])
            p.op("sc", (lambda s, oc=oc, bank=bank: s.activation(
                ySB[oc][:, 0:TB], bank[:, 0:TB], AF.Copy,
                scale=acol["s2"][:, 0:1])),
                reads=[bk, "s2a"], writes=[f"ySB{oc}"])

        for tt in range(NTT):
            n = t0 + tt
            xk = f"xr{tt % 2}"
            p.dma("sync", xr[tt % 2][:, :], x_r[:, n, :],
                  reads=[k for k in [xin_g(n)] if k], writes=[xk])
            pi = tt % 2
            for c in range(DC):
                p.op("pe", (lambda t, c=c, tt=tt, pi=pi: t.transpose(
                    pst_bf[pi][:, c * 128:(c + 1) * 128],
                    ySB[c][:, tt * 128:(tt + 1) * 128], ctx.ident[:, :])),
                    reads=[f"ySB{c}", "c_ident"], writes=[PSTK[pi]])
            p.op("ve", (lambda v, pi=pi, xk=xk, tt=tt: v.tensor_tensor(
                yrt[:, :], pst_bf[pi][:, 0:D], xr[tt % 2][:, :], ALU.add)),
                reads=[PSTK[pi], xk], writes=["yr"])
            ob, obk = ot[tt % 2], f"ot{tt % 2}"
            emit_ln(ctx, yrt[:, :], "yr", ob[:, :], obk, scr, D)
            p.dma("gp", xo_r[:, n, :], ob[:, :], reads=[obk],
                  writes=[xout_g(n)])


def build_l0_ffn_nc(cfg):
    T, D, B = cfg["T"], cfg["D"], cfg["B"]
    F1, F2 = cfg["F1"], cfg["F2"]
    nc = bass.Bass("TRN2", target_bir_lowering=False)
    io = {
        "x_in": nc.dram_tensor("x_in", [B, T, D], F32, kind="ExternalInput").ap(),
        "k0": nc.dram_tensor("k0", [2 * D, F1], U8, kind="ExternalInput").ap(),
        "k1": nc.dram_tensor("k1", [F1, F2], U8, kind="ExternalInput").ap(),
        "k2": nc.dram_tensor("k2", [F2, D], U8, kind="ExternalInput").ap(),
        "s0c": nc.dram_tensor("s0c", [128, 1], F32, kind="ExternalInput").ap(),
        "s1c": nc.dram_tensor("s1c", [128, 1], F32, kind="ExternalInput").ap(),
        "s2c": nc.dram_tensor("s2c", [128, 1], F32, kind="ExternalInput").ap(),
        "ident": nc.dram_tensor("ident", [128, 128], BF16,
                                kind="ExternalInput").ap(),
        "x_out": nc.dram_tensor("x_out", [B, T, D], F32,
                                kind="ExternalOutput").ap(),
    }
    stack = contextlib.ExitStack()
    with stack:
        ctx = Ctx(nc, stack)
        emit_l0_ffn(ctx, cfg, io)
        ctx.p.emit(stack, tail="sync")
    return nc



def emit_l1_attn(ctx, cfg, io):
    nc, p = ctx.nc, ctx.p
    xin_key = io.get("xin_key") or (lambda s, n: None)
    xout_key = io.get("xout_key") or (lambda s, n: f"x_out@{s}_{n}")
    T, D, H, B = cfg["T"], cfg["D"], cfg["H"], cfg["B"]
    DH = 64
    assert D // H == DH
    NT = T // 128
    NKC = NT
    DC = D // 128
    QB = min(512, T)
    NQB = T // QB
    NQT = QB // 128
    KB = min(512, T)
    OFFS = T - 128
    RW = KB + OFFS
    MBASE = max(0, OFFS - 512)
    MW = OFFS + KB - MBASE
    W5 = min(512, D)
    NH5 = D // W5

    ctx.ident = ctx.sb([128, 128], BF16, "identsb")
    p.dma("sync", ctx.ident[:, :], io["ident"], writes=[("c_ident", None)])
    ctx.eps_col = ctx.sb([128, 1], F32, "eps_col")
    p.op("ve", lambda v: v.memset(ctx.eps_col[:, :], 1e-5), writes=["c_eps"])
    ones1 = ctx.sb([1, 128], BF16, "ones1")
    p.op("ve", lambda v: v.memset(ones1[:, :], 1.0), writes=["c_ones1"])

    psm = ctx.ps([128, 2048], F32, "psm")
    pso = ctx.ps([128, 512], F32, "pso")
    psb = ctx.ps([128, 512], F32, "psb")
    pst = [ctx.ps([128, 512], F32, "pst0"), ctx.ps([128, 512], F32, "pst1")]
    pst_bf = [t[:, :].bitcast(BF16) for t in pst]
    PSTK = ["pst0", "pst1"]

    def psm_keys(c0, c1):
        return [f"psm{b}" for b in range(c0 // 512, (c1 + 511) // 512)]

    I16 = mybir.dt.int16
    rstrip = ctx.sb([128, RW], I16, "rstrip")
    mstrip = ctx.sb([128, KB], F32, "mstrip")
    p.op("gp", lambda g: g.iota(rstrip[:, :], [[-1, RW]], base=0,
                                channel_multiplier=1,
                                allow_small_or_imprecise_dtypes=True),
         writes=["rstrip"])
    p.op("ve", lambda v: v.tensor_scalar(
        mstrip[:, :], rstrip[:, 0:KB], 0.0, NEG_BIG,
        ALU.is_gt, ALU.mult),
        reads=["rstrip"], writes=["mstrip"])

    slopes = ctx.sb([128, H], F32, "slopes")
    p.dma("sync", slopes[:, :], io["slopes_bc"], writes=["slopes"])
    sqs = ctx.sb([128, 512], BF16, "sqs")
    onescol2 = ctx.sb([128, 1], BF16, "onescol2")
    p.op("ve", lambda v: v.memset(onescol2[:, :], 1.0), writes=["c_onescol"])
    nrow = ctx.sb([1, 64], F32, "nrow")
    crow = ctx.sb([1, 64], F32, "crow")
    negcB = ctx.sb([128, 16], F32, "negcB")

    XT = [ctx.sb([128, T], BF16, f"XTt{c}") for c in range(DC)]
    XTK = [f"XT{c}" for c in range(DC)]
    QT = [ctx.sb([128, T], BF16, f"QTt{c}") for c in range(DC)]
    KTt = [ctx.sb([128, T], BF16, f"KTt{c}") for c in range(DC)]
    VT = [ctx.sb([128, T], BF16, f"VTt{c}") for c in range(DC)]
    Vst = [ctx.sb([128, T], BF16, f"Vst{i}") for i in range(2)]
    for i in range(2):
        p.op("ve", (lambda v, i=i: v.memset(Vst[i][:, :], 1.0)),
             writes=[f"Vst{i}"])
    wcol = [ctx.sb([128, D], BF16, f"wcol{i}") for i in range(2)]
    bcol = ctx.sb([128, 1], F32, "bcol")
    wpb = [ctx.sb([128, D], BF16, f"wpb{c}") for c in range(DC)]

    OTb = [XT[c][:, 0:T] for c in range(DC)]
    _reca = ctx.sb([64, 512], F32, "reca")[:, :]
    rec2 = [_reca, _reca]
    OTK = [f"OT{c}" for c in range(DC)]
    PTK = ["PT0", "PT1", "PT2", "PT3"]
    ATT_KEYS = OTK + PTK + ["rec0", "rec1"]

    xbf4_1 = ctx.sb([128, 4 * D], BF16, "xbf4_0")
    xbf4 = [xbf4_1, xbf4_1]
    PTt = [ctx.sb([128, 512], BF16, "PTa")[:, :],
           ctx.sb([128, 512], BF16, "PTb")[:, :],
           xbf4_1[:, 0:512], xbf4_1[:, 512:1024]]
    xr = [ctx.sb([128, D], F32, f"xr{i}") for i in range(2)]
    yrt = ctx.sb([128, D], F32, "yrt")
    ot = [ctx.sb([128, D], F32, f"ott{i}") for i in range(2)]
    scr = {k: ctx.sb([128, 1], F32, f"lns_{k}") for k in
           ("s1", "ss", "mean", "ms", "var", "rstd")}
    scr["sqd"] = ctx.sb([128, D], F32, "lnsqd")
    buB = ctx.sb([128, D], F32, "buB")
    bu_bf = ctx.sb([1, D], BF16, "bu_bf")

    for c in range(DC):
        p.dma("gp", wpb[c][:, :],
              io["wproj"][c * 128:(c + 1) * 128, :], writes=[f"wpb{c}"])

    p.dma("gp", bu_bf[0:1, :], io["bproj"].unsqueeze(0), writes=["bu_bf"])
    for i in range(NH5):
        p.op("pe", (lambda t, i=i: t.matmul(
            psb[:, 0:W5], ones1[0:1, :], bu_bf[0:1, i * W5:(i + 1) * W5],
            start=True, stop=True)),
            reads=["bu_bf", "c_ones1"], writes=["psb"])
        p.op("ve", (lambda v, i=i: v.tensor_copy(
            buB[:, i * W5:(i + 1) * W5], psb[:, 0:W5])),
            reads=["psb"], writes=["buB"])

    for s in range(B):
        x_seq = io["x_in"][s]
        xo_seq = io["x_out"][s]
        x_r = x_seq.rearrange("(m p) d -> p m d", p=128)
        xo_r = xo_seq.rearrange("(m p) d -> p m d", p=128)

        p.fence(XTK + ATT_KEYS + ["xbf4_0"])

        for g in range(0, NT, 4):
            gl = min(4, NT - g)
            xb = xbf4[0]
            xbk = "xbf4_0"
            p.dma("gp", xb[:, 0:gl * D].rearrange("p (n d) -> p n d", n=gl),
                  x_r[:, g:g + gl, :],
                  reads=[k for k in (xin_key(s, g + j) for j in range(gl))
                         if k], writes=[xbk])
            for c in range(DC):
                srcs = [(xb[:, j * D + c * 128: j * D + (c + 1) * 128], xbk)
                        for j in range(gl)]
                par = XT[c][:, :]
                dsts = [(par, (g + j) * 128, XTK[c])
                        for j in range(gl)]
                emit_transposes(ctx, srcs, dsts, pst_bf, PSTK)

        wkqv_r = io["wkqv"].rearrange("(n p) f -> p n f", p=128)
        for oc in range(3 * DC):
            wc = wcol[oc % 2]
            wck = f"wcol{oc % 2}"
            p.dma("gp", wc[:, :].rearrange("p (n c) -> p n c", n=DC),
                  wkqv_r[:, :, oc * 128:(oc + 1) * 128], writes=[wck])
            p.dma("sync", bcol[:, :],
                  io["bkqv"][oc * 128:(oc + 1) * 128].unsqueeze(1),
                  writes=["bcol"])
            if oc < DC:
                dst, dk, scl = KTt[oc], f"KT{oc}", 1.0
            elif oc < 2 * DC:
                dst, dk, scl = QT[oc - DC], f"QT{oc - DC}", 0.125
            else:
                dst, dk, scl = VT[oc - 2 * DC], f"VT{oc - 2 * DC}", 1.0
            if scl != 1.0:
                p.op("ve", lambda v: v.tensor_scalar(
                    bcol[:, :], bcol[:, :], 0.125, None, ALU.mult),
                    reads=["bcol"], writes=["bcol"])
            for t5 in range(T // KB):
                bank = psm[:, (t5 % 2) * 512:(t5 % 2) * 512 + KB]
                bk = f"psm{t5 % 2}"
                for ic in range(DC):
                    p.op("pe", (lambda t, t5=t5, ic=ic, wc=wc, bank=bank:
                                t.matmul(bank,
                                         wc[:, ic * 128:(ic + 1) * 128],
                                         XT[ic][:, t5 * KB:(t5 + 1) * KB],
                                         start=(ic == 0), stop=(ic == DC - 1))),
                        reads=[wck, XTK[ic]], writes=[bk])
                p.op("ve", (lambda v, t5=t5, dst=dst, scl=scl, bank=bank:
                            v.tensor_scalar(dst[:, t5 * KB:(t5 + 1) * KB],
                                            bank, scl, bcol[:, 0:1],
                                            ALU.mult, ALU.add)),
                    reads=[bk, "bcol"], writes=[dk])

        p.op("ve", lambda v: v.memset(nrow[0:1, :], 0.0), writes=["nrow"])
        for kind, tiles, tk in (("q", QT, "QT"), ("k", KTt, "KT")):
            for h in range(H):
                ch, hb = h // 2, (h % 2) * 64
                col = (0 if kind == "q" else 32) + h
                for t5 in range(T // KB):
                    p.op("sc", (lambda s_, ch=ch, hb=hb, t5=t5, tiles=tiles,
                                tk=tk: s_.activation(
                                    sqs[hb:hb + 64, 0:KB],
                                    tiles[ch][hb:hb + 64,
                                              t5 * KB:(t5 + 1) * KB],
                                    AF.Square)),
                        reads=[f"{tk}{ch}"], writes=["sqs"])
                    p.op("pe", (lambda t, hb=hb: t.matmul(
                        psb[0:1, 0:KB], onescol2[hb:hb + 64, 0:1],
                        sqs[hb:hb + 64, 0:KB], start=True, stop=True)),
                        reads=["sqs", "c_onescol"], writes=["psb"])
                    p.op("ve", (lambda v: v.tensor_reduce(
                        crow[0:1, 32:33], psb[0:1, 0:KB], AX.X, ALU.max)),
                        reads=["psb"], writes=["crow"])
                    p.op("ve", (lambda v, col=col: v.tensor_max(
                        nrow[0:1, col:col + 1], nrow[0:1, col:col + 1],
                        crow[0:1, 32:33])),
                        reads=["crow", "nrow"], writes=["nrow"])
        p.op("ve", lambda v: v.tensor_mul(
            crow[0:1, 0:H], nrow[0:1, 0:H], nrow[0:1, 32:32 + H]),
            reads=["nrow"], writes=["crow"])
        p.op("ve", lambda v: v.tensor_scalar_max(
            crow[0:1, 0:H], crow[0:1, 0:H], 1e-12),
            reads=["crow"], writes=["crow"])
        p.op("sc", lambda s_: s_.activation(
            crow[0:1, 0:H], crow[0:1, 0:H], AF.Ln),
            reads=["crow"], writes=["crow"])
        p.op("sc", lambda s_: s_.activation(
            crow[0:1, 0:H], crow[0:1, 0:H], AF.Exp, scale=0.5),
            reads=["crow"], writes=["crow"])
        p.op("ve", lambda v: v.tensor_scalar(
            crow[0:1, 32:32 + H], slopes[0:1, 0:H], -(T - 1.0), None,
            ALU.mult),
            reads=["slopes"], writes=["crow"])
        p.op("ve", lambda v: v.tensor_scalar_max(
            crow[0:1, 32:32 + H], crow[0:1, 32:32 + H], 0.0),
            reads=["crow"], writes=["crow"])
        p.op("ve", lambda v: v.tensor_add(
            crow[0:1, 0:H], crow[0:1, 0:H], crow[0:1, 32:32 + H]),
            reads=["crow"], writes=["crow"])
        crbf = crow[:, 48:56].bitcast(BF16)
        p.op("ve", lambda v: v.tensor_scalar(
            crbf[0:1, 0:H], crow[0:1, 0:H], -1.0, None, ALU.mult),
            reads=["crow"], writes=["crow"])
        p.op("pe", lambda t: t.matmul(
            psb[:, 0:H], ones1[0:1, :], crbf[0:1, 0:H],
            start=True, stop=True),
            reads=["crow", "c_ones1"], writes=["psb"])
        p.op("ve", lambda v: v.tensor_copy(negcB[:, 0:H], psb[:, 0:H]),
             reads=["psb"], writes=["negcB"])

        p.fence(XTK + ATT_KEYS)

        for ch in range(H // 2):
            heads = [2 * ch, 2 * ch + 1]
            for h in heads:
                hb = (h % 2) * 64
                vs, vsk = Vst[h % 2], f"Vst{h % 2}"
                for kg in range(0, NKC, 8):
                    gl = min(8, NKC - kg)
                    pi = (kg // 8) % 2
                    for j in range(gl):
                        kc = kg + j
                        p.op("pe", (lambda t, kc=kc, j=j, pi=pi, ch=ch,
                                    hb=hb: t.transpose(
                                        pst_bf[pi][:, j * 64:(j + 1) * 64],
                                        VT[ch][hb:hb + 64,
                                               kc * 128:(kc + 1) * 128],
                                        ctx.ident[hb:hb + 64, hb:hb + 64])),
                            reads=[f"VT{ch}", "c_ident"], writes=[PSTK[pi]])
                    p.op("ve", (lambda v, kg=kg, gl=gl, pi=pi, vs=vs:
                                v.tensor_copy(
                                    vs[:, kg * 128:(kg + gl) * 128].rearrange(
                                        "p (k c) -> p k c", c=128)[:, :, 0:64],
                                    pst_bf[pi][:, 0:gl * 64].rearrange(
                                        "p (k c) -> p k c", c=64))),
                        reads=[PSTK[pi]], writes=[vsk])
            for qb in range(NQB):
                NKCp = qb * NQT + NQT
                pend = []
                for kc in range(NKCp):
                    q_off = max(0, kc - qb * NQT)
                    nq = NQT - q_off
                    qc0 = qb * QB + q_off * 128
                    o = qc0 - kc * 128
                    for h in heads:
                        hb = (h % 2) * 64
                        bb = (h % 2) * 2 + kc % 2
                        bank = psm[:, bb * 512:bb * 512 + nq * 128]
                        p.op("pe", (lambda t, kc=kc, ch=ch, hb=hb, qc0=qc0,
                                    nq=nq, bank=bank, qb=qb: t.matmul(
                                        bank,
                                        KTt[ch][hb:hb + 64,
                                                kc * 128:(kc + 1) * 128],
                                        QT[ch][hb:hb + 64,
                                               qc0:qb * QB + QB],
                                        start=True, stop=True)),
                            reads=[f"QT{ch}", f"KT{ch}"],
                            writes=[f"psm{bb}"])
                    for h in heads:
                        hb = (h % 2) * 64
                        bb = (h % 2) * 2 + kc % 2
                        bank = psm[:, bb * 512:bb * 512 + nq * 128]
                        bks = [f"psm{bb}"]
                        pt = (h % 2) * 2 + kc % 2
                        pb, ptk = PTt[pt], PTK[pt]
                        p.op("ve", (lambda v, o=o, h=h, nq=nq, bank=bank:
                                    v.scalar_tensor_tensor(
                                        bank, rstrip[:, o:o + nq * 128],
                                        slopes[:, h:h + 1], bank,
                                        ALU.mult, ALU.add)),
                            reads=["rstrip", "slopes"] + bks, writes=bks)
                        if kc >= qb * NQT:
                            p.op("ve", (lambda v, nq=nq, bank=bank:
                                        v.tensor_tensor(
                                            bank, mstrip[:, 0:nq * 128],
                                            bank, ALU.add)),
                                reads=["mstrip"] + bks, writes=bks)
                        p.op("sc", (lambda s_, h=h, bank=bank, pb=pb, nq=nq:
                                    s_.activation(pb[:, 0:nq * 128], bank,
                                                  AF.Exp,
                                                  bias=negcB[:, h:h + 1])),
                            reads=bks + ["negcB"], writes=[ptk])
                        vs, vsk = Vst[h % 2], f"Vst{h % 2}"
                        pvo = pso if h % 2 == 0 else psb
                        pvk = "pso" if h % 2 == 0 else "psb"
                        pend.append((lambda t, kc=kc, pb=pb, vs=vs,
                                     q_off=q_off, nq=nq, NKCp=NKCp, pvo=pvo:
                                     t.matmul(
                                         pvo[:, q_off * 128:NQT * 128],
                                         vs[:, kc * 128:(kc + 1) * 128],
                                         pb[:, 0:nq * 128],
                                         start=(kc == 0),
                                         stop=(kc == NKCp - 1)),
                                     [vsk, ptk], [pvk]))
                    if len(pend) > 2:
                        for em, rd, wr in pend[:-2]:
                            p.op("pe", em, reads=rd, writes=wr)
                        pend = pend[-2:]
                for em, rd, wr in pend:
                    p.op("pe", em, reads=rd, writes=wr)
                for h in heads:
                    hb = (h % 2) * 64
                    pvo = pso if h % 2 == 0 else psb
                    pvk = "pso" if h % 2 == 0 else "psb"
                    rc, rck = rec2[0], "rec0"
                    p.op("sc", (lambda s_, pvo=pvo, rc=rc: s_.activation(
                        rc[0:64, 0:NQT * 128], pvo[64:128, 0:NQT * 128],
                        AF.Ln)),
                        reads=[pvk], writes=[rck])
                    p.op("sc", (lambda s_, rc=rc: s_.activation(
                        rc[0:64, 0:NQT * 128], rc[0:64, 0:NQT * 128],
                        AF.Exp, scale=-1.0)),
                        reads=[rck], writes=[rck])
                    p.op("ve", (lambda v, ch=ch, hb=hb, pvo=pvo, rc=rc,
                                qb=qb: v.tensor_tensor(
                        OTb[ch][hb:hb + 64, qb * QB:(qb + 1) * QB],
                        pvo[0:64, 0:NQT * 128],
                        rc[0:64, 0:NQT * 128], ALU.mult)),
                        reads=[pvk, rck], writes=[OTK[ch]])

        if True:
            for n in range(NT):
                xk = f"xr{n % 2}"
                p.dma("sync", xr[n % 2][:, :], x_r[:, n, :],
                      reads=[k for k in [xin_key(s, n)] if k], writes=[xk])
                pbase = (n % 2) * D if 2 * D <= 2048 else 0
                for hf in range(NH5):
                    for ic in range(DC):
                        p.op("pe", (lambda t, n=n, hf=hf, ic=ic, pb=pbase:
                                    t.matmul(
                            psm[:, pb + hf * W5:pb + (hf + 1) * W5],
                            OTb[ic][:, n * 128:(n + 1) * 128],
                            wpb[ic][:, hf * W5:(hf + 1) * W5],
                            start=(ic == 0), stop=(ic == DC - 1))),
                            reads=[OTK[ic], f"wpb{ic}"],
                            writes=psm_keys(pbase + hf * W5,
                                            pbase + (hf + 1) * W5))
                pskeys = psm_keys(pbase, pbase + D)
                p.op("ve", (lambda v, pb=pbase: v.tensor_tensor(
                    yrt[:, :], psm[:, pb:pb + D], buB[:, :], ALU.add)),
                    reads=pskeys + ["buB"], writes=["yr"])
                p.op("ve", (lambda v, n=n: v.tensor_add(
                    yrt[:, :], yrt[:, :], xr[n % 2][:, :])),
                    reads=["yr", xk], writes=["yr"])
                ob, obk = ot[n % 2], f"ot{n % 2}"
                emit_ln(ctx, yrt[:, :], "yr", ob[:, :], obk, scr, D)
                p.dma("gp", xo_r[:, n, :], ob[:, :], reads=[obk],
                      writes=[xout_key(s, n)])


def build_l1_attn_nc(cfg):
    T, D, H, B = cfg["T"], cfg["D"], cfg["H"], cfg["B"]
    nc = bass.Bass("TRN2", target_bir_lowering=False)
    io = {
        "x_in": nc.dram_tensor("x_in", [B, T, D], F32, kind="ExternalInput").ap(),
        "wkqv": nc.dram_tensor("wkqv", [D, 3 * D], F32,
                               kind="ExternalInput").ap(),
        "bkqv": nc.dram_tensor("bkqv", [3 * D], F32, kind="ExternalInput").ap(),
        "wproj": nc.dram_tensor("wproj", [D, D], F32, kind="ExternalInput").ap(),
        "bproj": nc.dram_tensor("bproj", [D], F32, kind="ExternalInput").ap(),
        "slopes_bc": nc.dram_tensor("slopes_bc", [128, H], F32,
                                    kind="ExternalInput").ap(),
        "ident": nc.dram_tensor("ident", [128, 128], BF16,
                                kind="ExternalInput").ap(),
        "x_out": nc.dram_tensor("x_out", [B, T, D], F32,
                                kind="ExternalOutput").ap(),
    }
    stack = contextlib.ExitStack()
    with stack:
        ctx = Ctx(nc, stack)
        emit_l1_attn(ctx, cfg, io)
        ctx.p.emit(stack, tail="sync")
    return nc



class Cursor:
    def __init__(self, arena_ap, start=0):
        self.ap = arena_ap
        self.off = start

    def alloc(self, cols, dtype):
        size = mybir.dt.size(dtype)
        off = (self.off + 31) // 32 * 32
        v = carve(self.ap, off, cols, dtype)
        self.off = off + cols * size
        return v


def emit_l1_ffn(ctx, cfg, io):
    nc, p = ctx.nc, ctx.p
    xin_key = io.get("xin_key") or (lambda s, n: None)
    xout_key = io.get("xout_key") or (lambda s, n: f"x_out@{s}_{n}")
    T, D, B = cfg["T"], cfg["D"], cfg["B"]
    HF = cfg["HF"]
    WW = cfg["WW"]
    HID = cfg["HID"]
    INNER = HF * 64
    NT = T // 128
    DC = D // 128
    NIC = INNER // 128
    NW = T // WW
    NWT = WW // 128
    NJ = (HID + 127) // 128
    OFF2 = 2 * WW - 128
    RW2 = 3 * WW - 128
    assert DC % 2 == 0

    ctx.ident = ctx.sb([128, 128], BF16, "identsb")
    p.dma("sync", ctx.ident[:, :], io["ident"], writes=[("c_ident", None)])
    ctx.eps_col = ctx.sb([128, 1], F32, "eps_col")
    p.op("ve", lambda v: v.memset(ctx.eps_col[:, :], 1e-5), writes=["c_eps"])
    ones1 = ctx.sb([1, 128], BF16, "ones1")
    p.op("ve", lambda v: v.memset(ones1[:, :], 1.0), writes=["c_ones1"])
    onescol = ctx.sb([128, 1], BF16, "onescol")
    p.op("ve", lambda v: v.memset(onescol[:, :], 1.0), writes=["c_onescol"])

    I16 = mybir.dt.int16
    r2 = ctx.sb([128, RW2], I16, "r2strip")
    m2 = ctx.sb([128, RW2], F32, "m2strip")
    p.op("gp", lambda g: g.iota(r2[:, :], [[1, RW2]], base=-OFF2,
                                channel_multiplier=-1,
                                allow_small_or_imprecise_dtypes=True),
         writes=["r2"])
    p.op("ve", lambda v: v.tensor_scalar(
        m2[:, :], r2[:, :], 0.0, None, ALU.is_gt),
        reads=["r2"], writes=["m2"])
    p.op("ve", lambda v: v.scalar_tensor_tensor(
        m2[:, :], r2[:, :], float(-WW) - 0.5, m2[:, :], ALU.is_lt, ALU.add),
        reads=["r2", "m2"], writes=["m2"])
    p.op("ve", lambda v: v.tensor_scalar(
        m2[:, :], m2[:, :], NEG_BIG, None, ALU.mult),
        reads=["m2"], writes=["m2"])

    psm = ctx.ps([128, 2048], F32, "psm")
    pso = ctx.ps([128, 512], F32, "pso")
    psb = ctx.ps([128, 512], F32, "psb")
    pst = [ctx.ps([128, 512], F32, "pst0"), ctx.ps([128, 512], F32, "pst1")]
    pst_bf = [t[:, :].bitcast(BF16) for t in pst]
    PSTK = ["pst0", "pst1"]
    banks = [(psm[:, i * 512:(i + 1) * 512], f"psm{i}") for i in range(4)] + \
            [(pso[:, :], "pso"), (psb[:, :], "psb"),
             (pst[0][:, :], "pst0"), (pst[1][:, :], "pst1")]

    xr = [ctx.sb([128, D], F32, f"xr{i}") for i in range(2)]
    yrt = ctx.sb([128, D], F32, "yrt")
    ot = [ctx.sb([128, D], F32, f"ott{i}") for i in range(2)]
    scr = {k: ctx.sb([128, 1], F32, f"lns_{k}") for k in
           ("s1", "ss", "mean", "ms", "var", "rstd")}
    scr["sqd"] = ctx.sb([128, D], F32, "lnsqd")
    lgc = ctx.sb([128, DC], F32, "lgc")
    lbc = ctx.sb([128, DC], F32, "lbc")
    b2c = ctx.sb([128, DC], F32, "b2c")
    qsrep = ctx.sb([128, INNER], F32, "qsrep")
    ksrep = ctx.sb([128, INNER], F32, "ksrep")
    b1a = ctx.sb([128, NJ], F32, "b1a")
    b1g = ctx.sb([128, NJ], F32, "b1g")
    rno2 = [ctx.sb([128, HF], F32, f"rnob{i}") for i in range(2)]
    nmax = ctx.sb([128, 2], F32, "nmax")
    sgq = ctx.sb([128, 512], F32, "sgq")
    qn = ctx.sb([128, INNER], F32, "qn")
    kn = ctx.sb([128, INNER], F32, "kn")
    tbl = [ctx.sb([128, INNER], BF16, f"tbl{i}") for i in range(8)]
    TBLK = [f"tbl{i}" for i in range(8)]

    ABYTES = (ctx.avail_bytes() - 512) // 32 * 32
    arena = ctx.sb([128, ABYTES], U8, "arenaL")

    cur = Cursor(arena[:, :])
    XTs = [cur.alloc(T + 4, BF16) for _ in range(DC)]
    XTS_OFF_END = cur.off
    QF = [cur.alloc(T, BF16) for _ in range(NIC)]
    KF = [cur.alloc(T, BF16) for _ in range(NIC)]
    OF = [cur.alloc(T, BF16) for _ in range(NIC)]
    VtokA = cur.alloc(NT * INNER, BF16)
    Vtok = [VtokA[:, n * INNER:(n + 1) * INNER] for n in range(NT)]
    qr4 = cur.alloc(4 * INNER, BF16)
    kr4 = cur.alloc(4 * INNER, BF16)
    A_COMMON_END = cur.off
    cur_w = Cursor(arena[:, :], A_COMMON_END)
    wqb = [cur_w.alloc(3 * INNER, BF16) for _ in range(DC)]
    xbfA = [cur_w.alloc(4 * D, BF16) for _ in range(2)]
    sgq2 = cur_w.alloc(INNER, F32)
    cur_a = Cursor(arena[:, :], A_COMMON_END)
    Pstr = [cur_a.alloc(2 * WW, BF16) for _ in range(NWT)]
    Vst = [cur_a.alloc(2 * WW, BF16) for _ in range(2)]
    PTt = [cur_a.alloc(WW, BF16) for _ in range(2)]
    rec = cur_a.alloc(512, F32)[0:64, :]
    wob = [cur_a.alloc(D, BF16) for _ in range(NIC)]
    assert max(cur_w.off, cur_a.off) <= ABYTES

    cur_e = Cursor(arena[:, :], XTS_OFF_END)
    hn = [cur_e.alloc(T, BF16) for _ in range(DC)]
    hgt_lo = min(NJ, XTS_OFF_END // (2 * T))
    HGT = [carve(arena[:, :], j * (2 * T), T, BF16) for j in range(hgt_lo)]
    for j in range(hgt_lo, NJ):
        HGT.append(cur_e.alloc(T, BF16))
    E_HGT0 = cur_e.off
    wcolA = [cur_e.alloc(D, BF16) for _ in range(2)]
    wcolG = [cur_e.alloc(D, BF16) for _ in range(2)]
    w2sb = [cur_e.alloc(D, BF16) for _ in range(2)]
    sqf = cur_e.alloc(512, BF16)
    mrow = cur_e.alloc(512, F32)[0:1, :]
    vrow = cur_e.alloc(512, F32)[0:1, :]
    rrow = cur_e.alloc(512, F32)[0:1, :]
    mrow_bf = cur_e.alloc(512, BF16)[0:1, :]
    rrow_bf = cur_e.alloc(512, BF16)[0:1, :]
    muB = cur_e.alloc(512, F32)
    rsB = cur_e.alloc(512, F32)
    gel = cur_e.alloc(512, BF16)
    ysb8 = [cur_e.alloc(512, BF16) for _ in range(DC)]
    xbf_e = carve(arena[:, :], E_HGT0, 4 * D, BF16)
    assert cur_e.off <= ABYTES, (cur_e.off, ABYTES)

    XTK = [f"XTs{c}" for c in range(DC)]
    QFK = [f"QF{c}" for c in range(NIC)]
    KFK = [f"KF{c}" for c in range(NIC)]
    OFK = [f"OF{c}" for c in range(NIC)]
    VTK = [f"Vt{n}" for n in range(NT)]
    WQK = [f"wqb{c}" for c in range(DC)]
    PSK = [f"Pst{j}" for j in range(NWT)]
    VSK = ["Vs0", "Vs1"]
    PTK = ["PT0", "PT1"]
    WOK = [f"wob{c}" for c in range(NIC)]
    HNK = [f"hn{c}" for c in range(DC)]
    HGK = [f"HG{j}" for j in range(NJ)]
    EK = (HNK + HGK + ["wcA0", "wcA1", "wcG0", "wcG1", "w2s0", "w2s1", "sqf",
                       "mrow", "vrow", "rrow", "mrowb", "rrowb", "muB", "rsB",
                       "gel"])
    AK = (XTK + QFK + KFK + OFK + VTK + WQK + PSK + VSK + PTK + WOK +
          ["qr4", "kr4"])

    xcur = nc.dram_tensor("xcur", [B, T, D], F32, kind="Internal").ap()
    if "w1bf" in ctx.pre:
        w1bf, w2bf = ctx.pre["w1bf"], ctx.pre["w2bf"]
    else:
        w1bf = nc.dram_tensor(ctx.prefix + "w1bf", [2, D, 2 * HID], BF16,
                              kind="Internal").ap()
        w2bf = nc.dram_tensor(ctx.prefix + "w2bf", [2, HID, D], BF16,
                              kind="Internal").ap()
        for l in range(2):
            for q in range(2):
                r0, r1 = q * D // 2, (q + 1) * D // 2
                p.dma("gp", w1bf[l][r0:r1, :], io["w1"][l][r0:r1, :],
                      writes=[f"w1bf{l}_{q}"])
            p.dma("gp", w2bf[l], io["w2"][l], writes=[f"w2bf{l}"])

    def xc_key(s, n):
        return f"xc@{s}_{n}"

    for s in range(B):
        x_in_r = io["x_in"][s].rearrange("(m p) d -> p m d", p=128)
        xc_r = xcur[s].rearrange("(m p) d -> p m d", p=128)
        xo_r = io["x_out"][s].rearrange("(m p) d -> p m d", p=128)

        for l in range(2):
            xsrc_r = x_in_r if l == 0 else xc_r
            xsrc_keys = [xin_key(s, n) if l == 0 else xc_key(s, n)
                         for n in range(NT)]

            p.fence(AK + EK + ["sgq2"])

            for c in range(DC):
                p.op("ve", (lambda v, c=c: v.memset(XTs[c][:, 0:1], 0.0)),
                     writes=[XTK[c]])
            for g in range(0, NT, 4):
                gl = min(4, NT - g)
                xb = xbfA[(g // 4) % 2]
                xbk = f"xbfl{(g // 4) % 2}"
                rds = [k for k in ([xsrc_keys[g + j] for j in range(gl)])
                       if k is not None]
                p.op("gp", (lambda e, xb=xb, gl=gl, g=g, xsrc_r=xsrc_r:
                            e.dma_start(
                                out=xb[:, 0:gl * D].rearrange(
                                    "p (n d) -> p n d", n=gl),
                                in_=xsrc_r[:, g:g + gl, :])),
                     reads=rds, writes=[xbk], dma=True)
                for c in range(DC):
                    srcs = [(xb[:, j * D + c * 128: j * D + (c + 1) * 128],
                             xbk) for j in range(gl)]
                    par = XTs[c][:, :]
                    dsts = [(par, 1 + (g + j) * 128, XTK[c])
                            for j in range(gl)]
                    emit_transposes(ctx, srcs, dsts, pst_bf, PSTK)

            def xts_slice(c, c0, c1):
                if c < DC // 2:
                    return XTs[c][:, 1 + c0:1 + c1]
                return XTs[c][:, c0:c1]

            for c in range(DC):
                p.dma("gp", wqb[c][:, :],
                      io["wqkv"][l][c * 128:(c + 1) * 128, :], writes=[WQK[c]])
            p.dma("sync", qsrep[:, :], io["qs_rep"][l], writes=["qsrep"])
            p.dma("sync", ksrep[:, :], io["ks_rep"][l], writes=["ksrep"])
            for c in range(DC):
                p.dma("sync", lgc[:, c:c + 1],
                      io["ln_g"][l][c * 128:(c + 1) * 128].unsqueeze(1),
                      writes=["lgc"])
                p.dma("sync", lbc[:, c:c + 1],
                      io["ln_b"][l][c * 128:(c + 1) * 128].unsqueeze(1),
                      writes=["lbc"])
                p.dma("sync", b2c[:, c:c + 1],
                      io["b2"][l][c * 128:(c + 1) * 128].unsqueeze(1),
                      writes=["b2c"])
            for j in range(NJ):
                ca0, ca1 = j * 128, min((j + 1) * 128, HID)
                p.dma("sync", b1a[0:ca1 - ca0, j:j + 1],
                      io["b1"][l][ca0:ca1].unsqueeze(1), writes=["b1a"])
                p.dma("sync", b1g[0:ca1 - ca0, j:j + 1],
                      io["b1"][l][HID + ca0:HID + ca1].unsqueeze(1),
                      writes=["b1g"])

            for n in range(NT):
                ti = n % 2
                for w, nm in ((0, "c1"), (1, "s1m"), (2, "c2"), (3, "s2m")):
                    p.dma("gp", tbl[4 * ti + w][:, :],
                          io[nm].rearrange("(m p) f -> p m f", p=128)[:, n, :],
                          writes=[TBLK[4 * ti + w]])
                for part, dstq, dk in ((0, qn, "qn"), (1, kn, "kn")):
                    bank, bk = banks[(n % 2) * 3 + part]
                    for ic in range(DC):
                        p.op("pe", (lambda t, n=n, ic=ic, part=part, bank=bank:
                                    t.matmul(
                                        bank[:, 0:INNER],
                                        xts_slice(ic, n * 128, (n + 1) * 128),
                                        wqb[ic][:, part * INNER:
                                                (part + 1) * INNER],
                                        start=(ic == 0), stop=(ic == DC - 1))),
                            reads=[XTK[ic], WQK[ic]], writes=[bk])
                    p.op("sc", (lambda s_, bank=bank: s_.activation(
                        sgq[:, 0:INNER], bank[:, 0:INNER], AF.Square)),
                        reads=[bk], writes=["sgq"])
                    rno, rnk = rno2[part], f"rnob{part}"
                    p.op("ve", (lambda v, rno=rno: v.tensor_reduce(
                        rno[:, :], sgq[:, 0:INNER].rearrange(
                            "p (h c) -> p h c", c=64), AX.X, ALU.add)),
                        reads=["sgq"], writes=[rnk])
                    p.op("ve", (lambda v, rno=rno: v.tensor_scalar_max(
                        rno[:, :], rno[:, :], 1e-24)),
                        reads=[rnk], writes=[rnk])
                    p.op("sc", (lambda s_, rno=rno: s_.activation(
                        rno[:, :], rno[:, :], AF.Ln)),
                        reads=[rnk], writes=[rnk])
                    p.op("sc", (lambda s_, rno=rno: s_.activation(
                        rno[:, :], rno[:, :], AF.Exp, scale=-0.5)),
                        reads=[rnk], writes=[rnk])
                    rnb = rno[:, 0:HF].unsqueeze(2).to_broadcast(
                        [128, HF, 64])
                    p.op("ve", (lambda v, dstq=dstq, bank=bank, rnb=rnb:
                                v.tensor_tensor(
                                    dstq[:, 0:INNER].rearrange(
                                        "p (h c) -> p h c", c=64),
                                    bank[:, 0:INNER].rearrange(
                                        "p (h c) -> p h c", c=64),
                                    rnb, ALU.mult)),
                        reads=[bk, rnk], writes=[dk])
                    rep = qsrep if part == 0 else ksrep
                    rk = "qsrep" if part == 0 else "ksrep"
                    p.op("ve", (lambda v, dstq=dstq, rep=rep: v.tensor_mul(
                        dstq[:, 0:INNER], dstq[:, 0:INNER], rep[:, 0:INNER])),
                        reads=[dk, rk], writes=[dk])
                    ctab = tbl[4 * ti + (0 if part == 0 else 2)]
                    stab = tbl[4 * ti + (1 if part == 0 else 3)]
                    ctk = TBLK[4 * ti + (0 if part == 0 else 2)]
                    stk = TBLK[4 * ti + (1 if part == 0 else 3)]
                    dst4 = qr4 if part == 0 else kr4
                    d4k = "qr4" if part == 0 else "kr4"
                    j4 = n % 4
                    v3 = lambda a: a.rearrange("p (h two c) -> p h two c",
                                               two=2, c=32)
                    src3 = v3(dstq[:, 0:INNER])
                    ct3, st3 = v3(ctab[:, 0:INNER]), v3(stab[:, 0:INNER])
                    dst3 = v3(dst4[:, j4 * INNER:(j4 + 1) * INNER])
                    eng_r = "gp" if part == 0 else "ve"
                    sgb = sgq if part == 0 else sgq2
                    sk = "sgq" if part == 0 else "sgq2"
                    h2i = INNER // 2
                    sA, sB = sgb[:, 0:h2i], sgb[:, h2i:2 * h2i]
                    rr = lambda a: a.rearrange("p (h c) -> p h c", c=32)
                    p.op(eng_r, (lambda v, src3=src3, ct3=ct3, sA=sA:
                                 v.tensor_mul(
                        sA, src3[:, :, 0, :], ct3[:, :, 0, :])),
                        reads=[dk, ctk], writes=[sk])
                    p.op(eng_r, (lambda v, src3=src3, st3=st3, sB=sB:
                                 v.tensor_mul(
                        sB, src3[:, :, 1, :], st3[:, :, 0, :])),
                        reads=[dk, stk], writes=[sk])
                    p.op(eng_r, (lambda v, dst3=dst3, sA=sA, sB=sB, rr=rr:
                                 v.tensor_sub(
                        dst3[:, :, 0, :], rr(sA), rr(sB))),
                        reads=[sk], writes=[d4k])
                    p.op(eng_r, (lambda v, src3=src3, ct3=ct3, sA=sA:
                                 v.tensor_mul(
                        sA, src3[:, :, 1, :], ct3[:, :, 1, :])),
                        reads=[dk, ctk], writes=[sk])
                    p.op(eng_r, (lambda v, src3=src3, st3=st3, sB=sB:
                                 v.tensor_mul(
                        sB, src3[:, :, 0, :], st3[:, :, 1, :])),
                        reads=[dk, stk], writes=[sk])
                    p.op(eng_r, (lambda v, dst3=dst3, sA=sA, sB=sB, rr=rr:
                                 v.tensor_add(
                        dst3[:, :, 1, :], rr(sA), rr(sB))),
                        reads=[sk], writes=[d4k])
                bank, bk = banks[(n % 2) * 3 + 2]
                for ic in range(DC):
                    p.op("pe", (lambda t, n=n, ic=ic, bank=bank: t.matmul(
                        bank[:, 0:INNER],
                        xts_slice(ic, n * 128, (n + 1) * 128),
                        wqb[ic][:, 2 * INNER:3 * INNER],
                        start=(ic == 0), stop=(ic == DC - 1))),
                        reads=[XTK[ic], WQK[ic]], writes=[bk])
                p.op("ve", (lambda v, n=n, bank=bank: v.tensor_copy(
                    Vtok[n][:, 0:INNER], bank[:, 0:INNER])),
                    reads=[bk], writes=[VTK[n]])
                if n % 4 == 3 or n == NT - 1:
                    g0 = (n // 4) * 4
                    gl = n - g0 + 1
                    for c in range(NIC):
                        srcs = [(qr4[:, j * INNER + c * 128:
                                     j * INNER + (c + 1) * 128], "qr4")
                                for j in range(gl)]
                        par = QF[c][:, :]
                        dsts = [(par, (g0 + j) * 128,
                                 QFK[c]) for j in range(gl)]
                        emit_transposes(ctx, srcs, dsts, pst_bf, PSTK)
                        srcs = [(kr4[:, j * INNER + c * 128:
                                     j * INNER + (c + 1) * 128], "kr4")
                                for j in range(gl)]
                        par = KF[c][:, :]
                        dsts = [(par, (g0 + j) * 128,
                                 KFK[c]) for j in range(gl)]
                        emit_transposes(ctx, srcs, dsts, pst_bf, PSTK)

            p.fence(AK)

            QTPAIR = min(2 * WW, 1024)
            for w in range(NW):
                kc0 = max(0, (w - 1) * NWT)
                nkc = (w + 1) * NWT - kc0
                if w <= 1:
                    for qt in range(NWT):
                        if w == 0:
                            regs = [(qt * 128 + 128, WW)]
                        else:
                            regs = [(0, qt * 128),
                                    (qt * 128 + 128 + WW, 2 * WW)]
                        for lo, hi in regs:
                            if hi > lo:
                                p.op("ve", (lambda v, qt=qt, lo=lo, hi=hi:
                                            v.memset(
                                                Pstr[qt][:, lo:hi], 0.0)),
                                     writes=[PSK[qt]])
                for h in range(HF):
                    ch, hb = h // 2, (h % 2) * 64
                    vs = Vst[h % 2]
                    vsk = VSK[h % 2]
                    if w == 0 and h < 2:
                        p.op("ve", (lambda v, vs=vs: v.memset(vs[:, :], 1.0)),
                             writes=[vsk])
                    src3 = VtokA[:, :].rearrange(
                        "p (n c) -> p n c",
                        c=INNER)[:, kc0:kc0 + nkc, h * 64:(h + 1) * 64]
                    dst3 = vs[:, 0:nkc * 128].rearrange(
                        "p (j c) -> p j c", c=128)[:, :, 0:64]
                    p.op("gp", (lambda g, dst3=dst3, src3=src3:
                                g.tensor_copy(dst3, src3)),
                         reads=[VTK[kc0 + j] for j in range(nkc)],
                         writes=[vsk])
                    for qt in range(NWT):
                        qtg = w * NWT + qt
                        pq = qt % 2
                        qb = pq * QTPAIR
                        nkb = 2 if w > 0 else 1
                        lo = qt * 128 if w > 0 else 0
                        hi = (qt * 128 + 128 + WW) if w > 0 \
                            else (qt * 128 + 128)
                        spank = [f"psm{i}" for i in
                                 range((qb + lo) // 512,
                                       (qb + hi + 511) // 512)]
                        for kb in range(nkb):
                            wb = (w - 1 + kb) if w > 0 else w
                            if w == 0:
                                c0, c1 = 0, qt * 128 + 128
                            elif kb == 0:
                                c0, c1 = qt * 128, WW
                            else:
                                c0, c1 = 0, qt * 128 + 128
                            a0 = qb + kb * WW + c0
                            a1 = qb + kb * WW + c1
                            bks = [f"psm{i}" for i in
                                   range(a0 // 512, (a1 + 511) // 512)]
                            p.op("pe", (lambda t, qtg=qtg, wb=wb, ch=ch,
                                        hb=hb, a0=a0, a1=a1, c0=c0, c1=c1:
                                        t.matmul(
                                            psm[:, a0:a1],
                                            QF[ch][hb:hb + 64,
                                                   qtg * 128:(qtg + 1) * 128],
                                            KF[ch][hb:hb + 64,
                                                   wb * WW + c0:
                                                   wb * WW + c1],
                                            start=True, stop=True)),
                                reads=[QFK[ch], KFK[ch]], writes=bks)
                            off = (wb * WW) - qtg * 128 + OFF2
                            d0 = qb + kb * WW + qt * 128
                            dk = [f"psm{d0 // 512}"]
                            p.op("ve", (lambda v, off=off, qt=qt, d0=d0:
                                        v.tensor_tensor(
                                            psm[:, d0:d0 + 128],
                                            m2[:, off + qt * 128:
                                               off + qt * 128 + 128],
                                            psm[:, d0:d0 + 128], ALU.add)),
                                reads=["m2"] + dk, writes=dk)
                        p.op("ve", (lambda v, qb=qb, lo=lo, hi=hi, pq=pq:
                                    v.tensor_reduce(
                                        nmax[:, pq:pq + 1],
                                        psm[:, qb + lo:qb + hi], AX.X,
                                        ALU.max, negate=True)),
                            reads=spank, writes=[f"nmax{pq}"])
                        p.op("sc", (lambda s_, qt=qt, qb=qb, lo=lo, hi=hi,
                                    pq=pq: s_.activation(
                                        Pstr[qt][:, lo:hi],
                                        psm[:, qb + lo:qb + hi], AF.Exp,
                                        bias=nmax[:, pq:pq + 1])),
                            reads=spank + [f"nmax{pq}"], writes=[PSK[qt]])
                    pvpend = None
                    for j in range(nkc):
                        pb = PTt[j % 2]
                        for qt in range(NWT):
                            p.op("pe", (lambda t, j=j, qt=qt: t.transpose(
                                pst_bf[j % 2][:, qt * 128:(qt + 1) * 128],
                                Pstr[qt][:, j * 128:(j + 1) * 128],
                                ctx.ident[:, :])),
                                reads=[PSK[qt], "c_ident"],
                                writes=[PSTK[j % 2]])
                        if j % 2 == 0:
                            p.op("ve", (lambda v, j=j, pb=pb: v.tensor_copy(
                                pb[:, 0:WW], pst_bf[j % 2][:, 0:WW])),
                                reads=[PSTK[j % 2]], writes=[PTK[j % 2]])
                        else:
                            p.op("sc", (lambda s_, j=j, pb=pb: s_.activation(
                                pb[:, 0:WW], pst_bf[j % 2][:, 0:WW],
                                AF.Copy, bias=0.0)),
                                reads=[PSTK[j % 2]], writes=[PTK[j % 2]])
                        if pvpend is not None:
                            em, rd, wr = pvpend
                            p.op("pe", em, reads=rd, writes=wr)
                        pvpend = ((lambda t, j=j, pb=pb, vs=vs, nkc=nkc:
                                   t.matmul(
                                       pso[:, 0:WW],
                                       vs[:, j * 128:(j + 1) * 128],
                                       pb[:, 0:WW],
                                       start=(j == 0), stop=(j == nkc - 1))),
                                  [vsk, PTK[j % 2]], ["pso"])
                    em, rd, wr = pvpend
                    p.op("pe", em, reads=rd, writes=wr)
                    p.op("sc", lambda s_: s_.activation(
                        rec[0:64, 0:WW], pso[64:128, 0:WW], AF.Ln),
                        reads=["pso"], writes=["rec"])
                    p.op("sc", lambda s_: s_.activation(
                        rec[0:64, 0:WW], rec[0:64, 0:WW], AF.Exp,
                        scale=-1.0),
                        reads=["rec"], writes=["rec"])
                    p.op("ve", (lambda v, ch=ch, hb=hb, w=w: v.tensor_tensor(
                        OF[ch][hb:hb + 64, w * WW:(w + 1) * WW],
                        pso[0:64, 0:WW], rec[0:64, 0:WW], ALU.mult)),
                        reads=["pso", "rec"], writes=[OFK[ch]])

            for c in range(NIC):
                p.dma("gp", wob[c][:, :],
                      io["wo"][l][c * 128:(c + 1) * 128, :], writes=[WOK[c]])
            W5 = min(512, D)
            NH5 = D // W5
            for n in range(NT):
                xk = f"xr{n % 2}"
                p.op("sync", (lambda e, n=n, xsrc_r=xsrc_r: e.dma_start(
                    out=xr[n % 2][:, :], in_=xsrc_r[:, n, :])),
                    reads=[k for k in [xsrc_keys[n]] if k], writes=[xk],
                    dma=True)
                pbase = (n % 2) * D if 2 * D <= 2048 else 0
                for hf5 in range(NH5):
                    for ic in range(NIC):
                        p.op("pe", (lambda t, n=n, hf5=hf5, ic=ic, pb=pbase:
                                    t.matmul(
                            psm[:, pb + hf5 * W5:pb + (hf5 + 1) * W5],
                            OF[ic][:, n * 128:(n + 1) * 128],
                            wob[ic][:, hf5 * W5:(hf5 + 1) * W5],
                            start=(ic == 0), stop=(ic == NIC - 1))),
                            reads=[OFK[ic], WOK[ic]],
                            writes=[f"psm{(pbase + hf5 * W5) // 512}"])
                p.op("ve", (lambda v, n=n, pb=pbase: v.tensor_tensor(
                    yrt[:, :], psm[:, pb:pb + D], xr[n % 2][:, :], ALU.add)),
                    reads=[f"psm{(pbase + i * 512) // 512}"
                           for i in range((D + 511) // 512)] + [xk],
                    writes=["yr"])
                p.op("ve", (lambda v, n=n: v.tensor_copy(
                    ot[n % 2][:, :], yrt[:, :])),
                    reads=["yr"], writes=[f"ot{n % 2}"])
                p.op("gp", (lambda e, n=n, xc_r=xc_r: e.dma_start(
                    out=xc_r[:, n, :], in_=ot[n % 2][:, :])),
                    reads=[f"ot{n % 2}"], writes=[xc_key(s, n)], dma=True)

            p.fence(AK + EK + ["sgq2"])

            for c in range(DC):
                p.op("ve", (lambda v, c=c: v.memset(XTs[c][:, 0:1], 0.0)),
                     writes=[XTK[c]])
            for g in range(0, NT, 4):
                gl = min(4, NT - g)
                xb = xbf_e
                xbk = "xbfe"
                p.op("gp", (lambda e, xb=xb, gl=gl, g=g, xc_r=xc_r:
                            e.dma_start(
                                out=xb[:, 0:gl * D].rearrange(
                                    "p (n d) -> p n d", n=gl),
                                in_=xc_r[:, g:g + gl, :])),
                     reads=[xc_key(s, g + j) for j in range(gl)],
                     writes=[xbk], dma=True)
                for c in range(DC):
                    srcs = [(xb[:, j * D + c * 128: j * D + (c + 1) * 128],
                             xbk) for j in range(gl)]
                    par = XTs[c][:, :]
                    dsts = [(par, 1 + (g + j) * 128, XTK[c])
                            for j in range(gl)]
                    emit_transposes(ctx, srcs, dsts, pst_bf, PSTK)

            NB5 = T // 512 if T >= 512 else 1
            B5 = min(512, T)
            for t5 in range(NB5):
                c0, c1 = t5 * B5, (t5 + 1) * B5
                for c in range(DC):
                    p.op("pe", (lambda t, c=c, c0=c0, c1=c1: t.matmul(
                        pso[0:1, 0:B5], onescol[:, 0:1], xts_slice(c, c0, c1),
                        start=(c == 0), stop=(c == DC - 1))),
                        reads=[XTK[c], "c_onescol"], writes=["pso"])
                p.op("ve", lambda v: v.tensor_scalar(
                    mrow[0:1, 0:B5], pso[0:1, 0:B5], 1.0 / D, None, ALU.mult),
                    reads=["pso"], writes=["mrow"])
                for c in range(DC):
                    p.op("sc", (lambda s_, c=c, c0=c0, c1=c1: s_.activation(
                        sqf[:, 0:B5], xts_slice(c, c0, c1), AF.Square)),
                        reads=[XTK[c]], writes=["sqf"])
                    p.op("pe", (lambda t, c=c: t.matmul(
                        psb[0:1, 0:B5], onescol[:, 0:1], sqf[:, 0:B5],
                        start=(c == 0), stop=(c == DC - 1))),
                        reads=["sqf", "c_onescol"], writes=["psb"])
                p.op("ve", lambda v: v.tensor_scalar(
                    vrow[0:1, 0:B5], psb[0:1, 0:B5], 1.0 / D, None, ALU.mult),
                    reads=["psb"], writes=["vrow"])
                p.op("ve", lambda v: v.tensor_mul(
                    rrow[0:1, 0:B5], mrow[0:1, 0:B5], mrow[0:1, 0:B5]),
                    reads=["mrow"], writes=["rrow"])
                p.op("ve", lambda v: v.tensor_sub(
                    vrow[0:1, 0:B5], vrow[0:1, 0:B5], rrow[0:1, 0:B5]),
                    reads=["vrow", "rrow"], writes=["vrow"])
                p.op("sc", lambda s_: s_.activation(
                    rrow[0:1, 0:B5], vrow[0:1, 0:B5], AF.Ln,
                    bias=ctx.eps_col[0:1, 0:1]),
                    reads=["vrow", "c_eps"], writes=["rrow"])
                p.op("sc", lambda s_: s_.activation(
                    rrow[0:1, 0:B5], rrow[0:1, 0:B5], AF.Exp, scale=-0.5),
                    reads=["rrow"], writes=["rrow"])
                p.op("ve", lambda v: v.tensor_copy(
                    mrow_bf[0:1, 0:B5], mrow[0:1, 0:B5]),
                    reads=["mrow"], writes=["mrowb"])
                p.op("ve", lambda v: v.tensor_copy(
                    rrow_bf[0:1, 0:B5], rrow[0:1, 0:B5]),
                    reads=["rrow"], writes=["rrowb"])
                p.op("pe", lambda t: t.matmul(
                    pso[:, 0:B5], ones1[0:1, :], mrow_bf[0:1, 0:B5],
                    start=True, stop=True),
                    reads=["mrowb", "c_ones1"], writes=["pso"])
                p.op("ve", lambda v: v.tensor_copy(muB[:, 0:B5],
                                                   pso[:, 0:B5]),
                     reads=["pso"], writes=["muB"])
                p.op("pe", lambda t: t.matmul(
                    pso[:, 0:B5], ones1[0:1, :], rrow_bf[0:1, 0:B5],
                    start=True, stop=True),
                    reads=["rrowb", "c_ones1"], writes=["pso"])
                p.op("ve", lambda v: v.tensor_copy(rsB[:, 0:B5],
                                                   pso[:, 0:B5]),
                     reads=["pso"], writes=["rsB"])
                for c in range(DC):
                    p.op("ve", (lambda v, c=c, c0=c0, c1=c1: v.tensor_sub(
                        sqf[:, 0:B5], xts_slice(c, c0, c1), muB[:, 0:B5])),
                        reads=[XTK[c], "muB"], writes=["sqf"])
                    p.op("ve", lambda v: v.tensor_mul(
                        sqf[:, 0:B5], sqf[:, 0:B5], rsB[:, 0:B5]),
                        reads=["sqf", "rsB"], writes=["sqf"])
                    p.op("ve", (lambda v, c=c, c0=c0, c1=c1: v.tensor_scalar(
                        hn[c][:, c0:c1], sqf[:, 0:B5], lgc[:, c:c + 1],
                        lbc[:, c:c + 1], ALU.mult, ALU.add)),
                        reads=["sqf", "lgc", "lbc"], writes=[HNK[c]])

            p.fence(XTK + HGK + ["xbfe", "wcA0", "wcA1", "wcG0", "wcG1"])
            for j in range(NJ):
                ca0, ca1 = j * 128, min((j + 1) * 128, HID)
                cw = ca1 - ca0
                wa, wg = wcolA[j % 2], wcolG[j % 2]
                wak, wgk = f"wcA{j % 2}", f"wcG{j % 2}"
                w1r = w1bf[l].rearrange("(n p) f -> p n f", p=128)
                w1rd = [f"w1bf{l}_0", f"w1bf{l}_1"]
                p.op("sync", (lambda e, wa=wa, cw=cw, ca0=ca0, w1r=w1r:
                            e.dma_start(out=wa[:, 0:DC * cw].rearrange(
                                "p (n c) -> p n c", n=DC),
                                in_=w1r[:, :, ca0:ca0 + cw])),
                     reads=w1rd, writes=[wak], dma=True)
                p.op("sync", (lambda e, wg=wg, cw=cw, ca0=ca0, w1r=w1r:
                            e.dma_start(out=wg[:, 0:DC * cw].rearrange(
                                "p (n c) -> p n c", n=DC),
                                in_=w1r[:, :, HID + ca0:HID + ca0 + cw])),
                     reads=w1rd, writes=[wgk], dma=True)
                for t5 in range(NB5):
                    c0, c1 = t5 * B5, (t5 + 1) * B5
                    bA, bAk = banks[2 * (t5 % 2)]
                    bG, bGk = banks[2 * (t5 % 2) + 1]
                    for ic in range(DC):
                        p.op("pe", (lambda t, ic=ic, cw=cw, c0=c0, c1=c1,
                                    wa=wa, bA=bA: t.matmul(
                                        bA[0:cw, 0:B5],
                                        wa[:, ic * cw:(ic + 1) * cw],
                                        hn[ic][:, c0:c1],
                                        start=(ic == 0), stop=(ic == DC - 1))),
                            reads=[wak, HNK[ic]], writes=[bAk])
                        p.op("pe", (lambda t, ic=ic, cw=cw, c0=c0, c1=c1,
                                    wg=wg, bG=bG: t.matmul(
                                        bG[0:cw, 0:B5],
                                        wg[:, ic * cw:(ic + 1) * cw],
                                        hn[ic][:, c0:c1],
                                        start=(ic == 0), stop=(ic == DC - 1))),
                            reads=[wgk, HNK[ic]], writes=[bGk])
                    p.op("sc", (lambda s_, j=j, cw=cw, bG=bG: s_.activation(
                        gel[0:cw, 0:B5], bG[0:cw, 0:B5], AF.Gelu,
                        bias=b1g[0:cw, j:j + 1])),
                        reads=[bGk, "b1g"], writes=["gel"])
                    p.op("ve", (lambda v, j=j, cw=cw, c0=c0, c1=c1, bA=bA:
                                v.scalar_tensor_tensor(
                                    HGT[j][0:cw, c0:c1], bA[0:cw, 0:B5],
                                    b1a[0:cw, j:j + 1], gel[0:cw, 0:B5],
                                    ALU.add, ALU.mult)),
                        reads=[bAk, "b1a", "gel"], writes=[HGK[j]])

            for t5 in range(NB5):
                c0, c1 = t5 * B5, (t5 + 1) * B5
                for j in range(NJ):
                    ca0, ca1 = j * 128, min((j + 1) * 128, HID)
                    cw = ca1 - ca0
                    w2t = w2sb[j % 2]
                    w2k = f"w2s{j % 2}"
                    p.op("sync", (lambda e, w2t=w2t, ca0=ca0, ca1=ca1, l=l:
                                e.dma_start(out=w2t[0:ca1 - ca0, :],
                                            in_=w2bf[l][ca0:ca1, :])),
                         reads=[f"w2bf{l}"], writes=[w2k], dma=True)
                    for oc in range(DC):
                        bank, bk = banks[oc]
                        p.op("pe", (lambda t, j=j, oc=oc, cw=cw, c0=c0, c1=c1,
                                    w2t=w2t, bank=bank: t.matmul(
                                        bank[:, 0:B5],
                                        w2t[0:cw, oc * 128:(oc + 1) * 128],
                                        HGT[j][0:cw, c0:c1],
                                        start=(j == 0), stop=(j == NJ - 1))),
                            reads=[w2k, HGK[j]], writes=[bk])
                for oc in range(DC):
                    bank, bk = banks[oc]
                    p.op("ve", (lambda v, oc=oc, bank=bank: v.tensor_scalar(
                        ysb8[oc][:, 0:B5], bank[:, 0:B5], 1.0,
                        b2c[:, oc:oc + 1], ALU.mult, ALU.add)),
                        reads=[bk, "b2c"], writes=[f"ysb{oc}"])
                for tt in range(B5 // 128):
                    n = t5 * (B5 // 128) + tt
                    xk = f"xr{tt % 2}"
                    p.op("sync", (lambda e, n=n, tt=tt, xc_r=xc_r:
                                  e.dma_start(
                        out=xr[tt % 2][:, :], in_=xc_r[:, n, :])),
                        reads=[xc_key(s, n)], writes=[xk], dma=True)
                    pi = tt % 2
                    for c in range(DC):
                        p.op("pe", (lambda t, c=c, tt=tt, pi=pi: t.transpose(
                            pst_bf[pi][:, c * 128:(c + 1) * 128],
                            ysb8[c][:, tt * 128:(tt + 1) * 128],
                            ctx.ident[:, :])),
                            reads=[f"ysb{c}", "c_ident"], writes=[PSTK[pi]])
                    p.op("ve", (lambda v, pi=pi, tt=tt: v.tensor_tensor(
                        yrt[:, :], pst_bf[pi][:, 0:D], xr[tt % 2][:, :],
                        ALU.add)),
                        reads=[PSTK[pi], xk], writes=["yr"])
                    p.op("ve", (lambda v, tt=tt: v.tensor_copy(
                        ot[tt % 2][:, :], yrt[:, :])),
                        reads=["yr"], writes=[f"ot{tt % 2}"])
                    p.op("gp", (lambda e, n=n, tt=tt, xc_r=xc_r:
                                e.dma_start(
                        out=xc_r[:, n, :], in_=ot[tt % 2][:, :])),
                        reads=[f"ot{tt % 2}"], writes=[xc_key(s, n)],
                        dma=True)

        for n in range(NT):
            xk = f"xr{n % 2}"
            p.op("sync", (lambda e, n=n, x_in_r=x_in_r: e.dma_start(
                out=xr[n % 2][:, :], in_=x_in_r[:, n, :])),
                reads=[k for k in [xin_key(s, n)] if k],
                writes=[xk], dma=True)
            p.op("sync", (lambda e, n=n, xc_r=xc_r: e.dma_start(
                out=ot[n % 2][:, :], in_=xc_r[:, n, :])),
                reads=[xc_key(s, n)], writes=[f"ot{n % 2}"], dma=True)
            p.op("ve", (lambda v, n=n: v.tensor_add(
                yrt[:, :], xr[n % 2][:, :], ot[n % 2][:, :])),
                reads=[xk, f"ot{n % 2}"], writes=["yr"])
            ob, obk = ot[n % 2], f"ot{n % 2}"
            emit_ln(ctx, yrt[:, :], "yr", ob[:, :], obk, scr, D)
            p.op("gp", (lambda e, n=n, xo_r=xo_r: e.dma_start(
                out=xo_r[:, n, :], in_=ot[n % 2][:, :])),
                reads=[obk], writes=[xout_key(s, n)], dma=True)


def l1_xpos_tables(T, WW):
    DH = 64
    half_idx = np.arange(0, DH, 2, dtype=np.float64)
    inv_freq = 1.0 / 10000.0 ** (half_idx / DH)
    t = np.arange(T, dtype=np.float64)
    freqs = t[:, None] * inv_freq[None, :]
    freqs = np.concatenate([freqs, freqs], axis=-1)
    base = (half_idx + 0.4 * DH) / (1.4 * DH)
    power = (t - T // 2) / float(WW // 2)
    scale = base[None, :] ** power[:, None]
    scale = np.concatenate([scale, scale], axis=-1)
    cos, sin = np.cos(freqs), np.sin(freqs)
    c1 = (cos * scale * 8.0).astype(np.float32)
    s1 = (sin * scale * 8.0).astype(np.float32)
    c2 = (cos / scale).astype(np.float32)
    s2 = (sin / scale).astype(np.float32)
    return c1, s1, c2, s2


def build_l1_ffn_nc(cfg):
    T, D, B = cfg["T"], cfg["D"], cfg["B"]
    HF, WW, HID = cfg["HF"], cfg["WW"], cfg["HID"]
    INNER = HF * 64
    nc = bass.Bass("TRN2", target_bir_lowering=False)
    io = {
        "x_in": nc.dram_tensor("x_in", [B, T, D], F32, kind="ExternalInput").ap(),
        "wqkv": nc.dram_tensor("wqkv", [2, D, 3 * INNER], F32,
                               kind="ExternalInput").ap(),
        "qs_rep": nc.dram_tensor("qs_rep", [2, 128, INNER], F32,
                                 kind="ExternalInput").ap(),
        "ks_rep": nc.dram_tensor("ks_rep", [2, 128, INNER], F32,
                                 kind="ExternalInput").ap(),
        "wo": nc.dram_tensor("wo", [2, INNER, D], F32,
                             kind="ExternalInput").ap(),
        "ln_g": nc.dram_tensor("ln_g", [2, D], F32, kind="ExternalInput").ap(),
        "ln_b": nc.dram_tensor("ln_b", [2, D], F32, kind="ExternalInput").ap(),
        "w1": nc.dram_tensor("w1", [2, D, 2 * HID], F32,
                             kind="ExternalInput").ap(),
        "b1": nc.dram_tensor("b1", [2, 2 * HID], F32,
                             kind="ExternalInput").ap(),
        "w2": nc.dram_tensor("w2", [2, HID, D], F32,
                             kind="ExternalInput").ap(),
        "b2": nc.dram_tensor("b2", [2, D], F32, kind="ExternalInput").ap(),
        "c1": nc.dram_tensor("c1", [T, INNER], F32, kind="ExternalInput").ap(),
        "s1m": nc.dram_tensor("s1m", [T, INNER], F32,
                              kind="ExternalInput").ap(),
        "c2": nc.dram_tensor("c2", [T, INNER], F32, kind="ExternalInput").ap(),
        "s2m": nc.dram_tensor("s2m", [T, INNER], F32,
                              kind="ExternalInput").ap(),
        "ident": nc.dram_tensor("ident", [128, 128], BF16,
                                kind="ExternalInput").ap(),
        "x_out": nc.dram_tensor("x_out", [B, T, D], F32,
                                kind="ExternalOutput").ap(),
    }
    stack = contextlib.ExitStack()
    with stack:
        ctx = Ctx(nc, stack)
        emit_l1_ffn(ctx, cfg, io)
        ctx.p.emit(stack, tail="sync")
    return nc



def l0_rope_tables(T, D, base=10000.0):
    half = D // 2
    theta = 1.0 / base ** (np.arange(half, dtype=np.float32) / half)
    ang = np.arange(T, dtype=np.float32)[:, None] * theta[None, :]
    return np.cos(ang).astype(np.float32), np.sin(ang).astype(np.float32)


def ident_bf16():
    return np.eye(128, dtype=np.float32).astype(BF16NP)



def build_l0_attn_nc(cfg):
    T, D, B = cfg["T"], cfg["D"], cfg["B"]
    nc = bass.Bass("TRN2", target_bir_lowering=False)
    io = {
        "x_in": nc.dram_tensor("x_in", [B, T, D], F32, kind="ExternalInput").ap(),
        "wq": nc.dram_tensor("wq", [D, D], F32, kind="ExternalInput").ap(),
        "wk": nc.dram_tensor("wk", [D, D], F32, kind="ExternalInput").ap(),
        "wv": nc.dram_tensor("wv", [D, D], F32, kind="ExternalInput").ap(),
        "wu": nc.dram_tensor("wu", [D, D], F32, kind="ExternalInput").ap(),
        "bu": nc.dram_tensor("bu", [D], F32, kind="ExternalInput").ap(),
        "rope_cos": nc.dram_tensor("rope_cos", [T, D // 2], F32,
                                   kind="ExternalInput").ap(),
        "rope_sin": nc.dram_tensor("rope_sin", [T, D // 2], F32,
                                   kind="ExternalInput").ap(),
        "ident": nc.dram_tensor("ident", [128, 128], BF16,
                                kind="ExternalInput").ap(),
        "x_out": nc.dram_tensor("x_out", [B, T, D], F32,
                                kind="ExternalOutput").ap(),
    }
    stack = contextlib.ExitStack()
    with stack:
        ctx = Ctx(nc, stack)
        emit_l0_attn(ctx, cfg, io)
        ctx.p.emit(stack, tail="sync")
    return nc


def build_full_nc(fcfg):
    T, D, B, H = fcfg["T"], fcfg["D"], fcfg["B"], fcfg["H"]
    F1, F2 = fcfg["F1"], fcfg["F2"]
    HF, WW, HID = fcfg["HF"], fcfg["WW"], fcfg["HID"]
    INNER = HF * 64
    nc = bass.Bass("TRN2", target_bir_lowering=False)

    def gi(name, shape, dtype=F32):
        return nc.dram_tensor(name, shape, dtype, kind="ExternalInput").ap()

    x_in = gi("x_in", [B, T, D])
    x_out = nc.dram_tensor("x_out", [B, T, D], F32,
                           kind="ExternalOutput").ap()
    xh = [nc.dram_tensor(f"xh{i}", [B, T, D], F32, kind="Internal").ap()
          for i in range(3)]
    ident = gi("ident", [128, 128], BF16)

    def keyf(name):
        return lambda s, n: f"{name}@{s}_{n}"

    io0 = dict(x_in=x_in, x_out=xh[0], ident=ident,
               xin_key=None, xout_key=keyf("xh0"),
               wq=gi("wq", [D, D]), wk=gi("wk", [D, D]), wv=gi("wv", [D, D]),
               wu=gi("wu", [D, D]), bu=gi("bu", [D]),
               rope_cos=gi("rope_cos", [T, D // 2]),
               rope_sin=gi("rope_sin", [T, D // 2]))
    io1 = dict(x_in=xh[0], x_out=xh[1], ident=ident,
               xin_key=keyf("xh0"), xout_key=keyf("xh1"),
               k0=gi("k0", [2 * D, F1], U8), k1=gi("k1", [F1, F2], U8),
               k2=gi("k2", [F2, D], U8), s0c=gi("s0c", [128, 1]),
               s1c=gi("s1c", [128, 1]), s2c=gi("s2c", [128, 1]))
    io2 = dict(x_in=xh[1], x_out=xh[2], ident=ident,
               xin_key=keyf("xh1"), xout_key=keyf("xh2"),
               wkqv=gi("wkqv", [D, 3 * D]), bkqv=gi("bkqv", [3 * D]),
               wproj=gi("wproj", [D, D]), bproj=gi("bproj", [D]),
               slopes_bc=gi("slopes_bc", [128, H]))
    io3 = dict(x_in=xh[2], x_out=x_out, ident=ident,
               xin_key=keyf("xh2"), xout_key=keyf("x_out"),
               wqkv=gi("wqkv", [2, D, 3 * INNER]),
               qs_rep=gi("qs_rep", [2, 128, INNER]),
               ks_rep=gi("ks_rep", [2, 128, INNER]),
               wo=gi("wo", [2, INNER, D]), ln_g=gi("ln_g", [2, D]),
               ln_b=gi("ln_b", [2, D]), w1=gi("w1", [2, D, 2 * HID]),
               b1=gi("b1", [2, 2 * HID]), w2=gi("w2", [2, HID, D]),
               b2=gi("b2", [2, D]),
               c1=gi("c1", [T, INNER]), s1m=gi("s1m", [T, INNER]),
               c2=gi("c2", [T, INNER]), s2m=gi("s2m", [T, INNER]))

    cfg_a = dict(T=T, D=D, H=H, B=B)
    cfg_f = dict(T=T, D=D, B=B, F1=F1, F2=F2)
    cfg_l = dict(T=T, D=D, B=B, HF=HF, WW=WW, HID=HID)

    stack = contextlib.ExitStack()
    with stack:
        ctx = Ctx(nc, stack)
        ctx.enable_shared(stack)
        k1bf = nc.dram_tensor("F0_k1bf", [F1, F2], BF16, kind="Internal").ap()
        k2bf = nc.dram_tensor("F0_k2bf", [F2, D], BF16, kind="Internal").ap()
        for q in range(4):
            r0, r1 = q * F1 // 4, (q + 1) * F1 // 4
            ctx.p.dma("gp", k1bf[r0:r1, :], io1["k1"][r0:r1, :],
                      writes=[f"k1bf{q}"])
            r0, r1 = q * F2 // 4, (q + 1) * F2 // 4
            ctx.p.dma("gp", k2bf[r0:r1, :], io1["k2"][r0:r1, :],
                      writes=[f"k2bf{q}"])
        w1bf = nc.dram_tensor("F1_w1bf", [2, D, 2 * HID], BF16,
                              kind="Internal").ap()
        w2bf = nc.dram_tensor("F1_w2bf", [2, HID, D], BF16,
                              kind="Internal").ap()
        for l in range(2):
            for q in range(2):
                r0, r1 = q * D // 2, (q + 1) * D // 2
                ctx.p.dma("gp", w1bf[l][r0:r1, :], io3["w1"][l][r0:r1, :],
                          writes=[f"w1bf{l}_{q}"])
            ctx.p.dma("gp", w2bf[l], io3["w2"][l], writes=[f"w2bf{l}"])
        ctx.pre = dict(k1bf=k1bf, k2bf=k2bf, w1bf=w1bf, w2bf=w2bf)
        blocks = [("A0", emit_l0_attn, cfg_a, io0),
                  ("F0", emit_l0_ffn, cfg_f, io1),
                  ("A1", emit_l1_attn, cfg_a, io2),
                  ("F1", emit_l1_ffn, cfg_l, io3)]
        for name, emitter, cfg, io in blocks:
            ctx.block_reset()
            ctx.prefix = name + "_"
            emitter(ctx, cfg, io)
            ctx.p.fence_all()
        ctx.p.emit(stack, tail="sync")
    return nc


_CFG0 = dict(T=2048, D=1024, H=16, B=2)
_CFG0F = dict(T=2048, D=1024, B=2, F1=4096, F2=4096)
_CFG1 = dict(T=2048, D=1024, H=16, B=2)
_CFG1F = dict(T=2048, D=1024, B=2, HF=8, WW=512, HID=2730)
_NCORE = 8

_BUILT = {}


def _get_nc(name):
    if name not in _BUILT:
        if name == "l0a":
            _BUILT[name] = build_l0_attn_nc(_CFG0)
        elif name == "l0f":
            _BUILT[name] = build_l0_ffn_nc(_CFG0F)
        elif name == "l1a":
            _BUILT[name] = build_l1_attn_nc(_CFG1)
        elif name == "l1f":
            _BUILT[name] = build_l1_ffn_nc(_CFG1F)
    return _BUILT[name]


def _col(v):
    return np.ascontiguousarray(
        np.broadcast_to(np.float32(v).reshape(1, 1), (128, 1)))


_FCFG = dict(T=2048, D=1024, B=2, H=16, F1=4096, F2=4096, HF=8, WW=512,
             HID=2730)


def _full_common(inputs):
    T, D, H = 2048, 1024, 16
    HF, WW = 8, 512
    INNER = HF * 64
    cos, sin = l0_rope_tables(T, D)
    c1, s1, c2, s2 = l1_xpos_tables(T, WW)
    slopes = np.asarray(inputs["L1_attn_slopes"], dtype=np.float32)
    qs = np.asarray(inputs["L1_ffn_q_scale"], dtype=np.float32)
    ks = np.asarray(inputs["L1_ffn_k_scale"], dtype=np.float32)
    qs_rep = np.ascontiguousarray(np.stack(
        [np.broadcast_to(np.tile(qs[l], HF)[None, :], (128, INNER))
         for l in range(2)]))
    ks_rep = np.ascontiguousarray(np.stack(
        [np.broadcast_to(np.tile(ks[l], HF)[None, :], (128, INNER))
         for l in range(2)]))
    return dict(
        ident=ident_bf16(), rope_cos=cos, rope_sin=sin,
        wq=inputs["L0_attn_Wq"], wk=inputs["L0_attn_Wk"],
        wv=inputs["L0_attn_Wv"], wu=inputs["L0_attn_Wu"],
        bu=inputs["L0_attn_bu"],
        k0=np.ascontiguousarray(inputs["L0_ffn_k0"]).view(np.uint8),
        k1=np.ascontiguousarray(inputs["L0_ffn_k1"]).view(np.uint8),
        k2=np.ascontiguousarray(inputs["L0_ffn_k2"]).view(np.uint8),
        s0c=_col(inputs["L0_ffn_s0"]), s1c=_col(inputs["L0_ffn_s1"]),
        s2c=_col(inputs["L0_ffn_s2"]),
        wkqv=inputs["L1_attn_W_kqv"], bkqv=inputs["L1_attn_b_kqv"],
        wproj=inputs["L1_attn_W_proj"], bproj=inputs["L1_attn_b_proj"],
        slopes_bc=np.ascontiguousarray(
            np.broadcast_to(slopes[None, :], (128, H))),
        wqkv=inputs["L1_ffn_wqkv"], qs_rep=qs_rep, ks_rep=ks_rep,
        wo=inputs["L1_ffn_wo"], ln_g=inputs["L1_ffn_ln_g"],
        ln_b=inputs["L1_ffn_ln_b"], w1=inputs["L1_ffn_w1"],
        b1=inputs["L1_ffn_b1"], w2=inputs["L1_ffn_w2"],
        b2=inputs["L1_ffn_b2"],
        c1=np.ascontiguousarray(np.tile(c1, (1, HF))),
        s1m=np.ascontiguousarray(np.tile(s1, (1, HF))),
        c2=np.ascontiguousarray(np.tile(c2, (1, HF))),
        s2m=np.ascontiguousarray(np.tile(s2, (1, HF))))


def kernel(**inputs):
    from concourse.bass_utils import run_bass_kernel_spmd
    T, D = 2048, 1024
    NC = _NCORE
    x = np.ascontiguousarray(inputs["L0_attn_x"])
    shards = np.split(x, NC, axis=0)
    if "full" not in _BUILT:
        _BUILT["full"] = build_full_nc(_FCFG)
    nc = _BUILT["full"]
    common = _full_common(inputs)
    res = run_bass_kernel_spmd(
        nc, [dict(x_in=s, **common) for s in shards], core_ids=list(range(NC)))
    shards = [r["x_out"].reshape(2, T, D) for r in res.results]
    return np.concatenate(shards, axis=0).astype(np.float32)


def kernel_unfused(**inputs):
    from concourse.bass_utils import run_bass_kernel_spmd
    T, D, H = 2048, 1024, 16
    NC = _NCORE
    x = np.ascontiguousarray(inputs["L0_attn_x"])
    shards = np.split(x, NC, axis=0)
    ident = ident_bf16()
    cos, sin = l0_rope_tables(T, D)

    nc = _get_nc("l0a")
    common = dict(wq=inputs["L0_attn_Wq"], wk=inputs["L0_attn_Wk"],
                  wv=inputs["L0_attn_Wv"], wu=inputs["L0_attn_Wu"],
                  bu=inputs["L0_attn_bu"], rope_cos=cos, rope_sin=sin,
                  ident=ident)
    res = run_bass_kernel_spmd(
        nc, [dict(x_in=s, **common) for s in shards], core_ids=list(range(NC)))
    shards = [r["x_out"].reshape(2, T, D) for r in res.results]

    nc = _get_nc("l0f")
    common = dict(k0=np.ascontiguousarray(inputs["L0_ffn_k0"]).view(np.uint8),
                  k1=np.ascontiguousarray(inputs["L0_ffn_k1"]).view(np.uint8),
                  k2=np.ascontiguousarray(inputs["L0_ffn_k2"]).view(np.uint8),
                  s0c=_col(inputs["L0_ffn_s0"]), s1c=_col(inputs["L0_ffn_s1"]),
                  s2c=_col(inputs["L0_ffn_s2"]), ident=ident)
    res = run_bass_kernel_spmd(
        nc, [dict(x_in=s, **common) for s in shards], core_ids=list(range(NC)))
    shards = [r["x_out"].reshape(2, T, D) for r in res.results]

    nc = _get_nc("l1a")
    slopes = np.asarray(inputs["L1_attn_slopes"], dtype=np.float32)
    common = dict(wkqv=inputs["L1_attn_W_kqv"], bkqv=inputs["L1_attn_b_kqv"],
                  wproj=inputs["L1_attn_W_proj"],
                  bproj=inputs["L1_attn_b_proj"],
                  slopes_bc=np.ascontiguousarray(
                      np.broadcast_to(slopes[None, :], (128, H))),
                  ident=ident)
    res = run_bass_kernel_spmd(
        nc, [dict(x_in=s, **common) for s in shards], core_ids=list(range(NC)))
    shards = [r["x_out"].reshape(2, T, D) for r in res.results]

    nc = _get_nc("l1f")
    HF, WW = 8, 512
    INNER = HF * 64
    c1, s1, c2, s2 = l1_xpos_tables(T, WW)
    qs = np.asarray(inputs["L1_ffn_q_scale"], dtype=np.float32)
    ks = np.asarray(inputs["L1_ffn_k_scale"], dtype=np.float32)
    qs_rep = np.ascontiguousarray(np.stack(
        [np.broadcast_to(np.tile(qs[l], HF)[None, :], (128, INNER))
         for l in range(2)]))
    ks_rep = np.ascontiguousarray(np.stack(
        [np.broadcast_to(np.tile(ks[l], HF)[None, :], (128, INNER))
         for l in range(2)]))
    common = dict(wqkv=inputs["L1_ffn_wqkv"], qs_rep=qs_rep, ks_rep=ks_rep,
                  wo=inputs["L1_ffn_wo"], ln_g=inputs["L1_ffn_ln_g"],
                  ln_b=inputs["L1_ffn_ln_b"], w1=inputs["L1_ffn_w1"],
                  b1=inputs["L1_ffn_b1"], w2=inputs["L1_ffn_w2"],
                  b2=inputs["L1_ffn_b2"],
                  c1=np.ascontiguousarray(np.tile(c1, (1, HF))),
                  s1m=np.ascontiguousarray(np.tile(s1, (1, HF))),
                  c2=np.ascontiguousarray(np.tile(c2, (1, HF))),
                  s2m=np.ascontiguousarray(np.tile(s2, (1, HF))),
                  ident=ident)
    res = run_bass_kernel_spmd(
        nc, [dict(x_in=s, **common) for s in shards], core_ids=list(range(NC)))
    shards = [r["x_out"].reshape(2, T, D) for r in res.results]

    return np.concatenate(shards, axis=0).astype(np.float32)


# version 28 (final)
# speedup vs baseline: 1.0164x; 1.0164x over previous
import contextlib

import numpy as np
import ml_dtypes

import concourse.bass as bass
import concourse.mybir as mybir

F32 = mybir.dt.float32
BF16 = mybir.dt.bfloat16
U8 = mybir.dt.uint8
AF = mybir.ActivationFunctionType
ALU = mybir.AluOpType
AX = mybir.AxisListType

BF16NP = ml_dtypes.bfloat16

ENGINES = ("sync", "gp", "ve", "sc", "pe")

NEG_BIG = -1.0e30



class _Op:
    __slots__ = ("idx", "eng", "emit", "dma", "deps", "inc", "cum", "waits")

    def __init__(self, idx, eng, emit, dma):
        self.idx = idx
        self.eng = eng
        self.emit = emit
        self.dma = dma
        self.deps = set()
        self.inc = False
        self.cum = 0
        self.waits = None


def _key(item):
    if isinstance(item, str):
        return item
    if isinstance(item, tuple):
        return item[0]
    try:
        return item.name
    except Exception:
        return item.tensor.name


NDMA_LANES = 8


class Prog:
    def __init__(self, nc):
        self.nc = nc
        self.ops = []
        self.last_writer = {}
        self.readers = {}
        self._lane_ctr = {}
        self._lane_last = {}
        self._barrier = None

    def op(self, eng, emit, reads=(), writes=(), dma=False):
        idx = len(self.ops)
        if dma:
            c = self._lane_ctr.get(eng, 0)
            self._lane_ctr[eng] = c + 1
            lane = c % {"sc": 4, "gp": 6}.get(eng, NDMA_LANES)
            dma = f"d{lane}"
        o = _Op(idx, eng, emit, dma)
        if dma:
            prev = self._lane_last.get((eng, dma))
            if prev is not None:
                o.deps.add(prev)
            self._lane_last[(eng, dma)] = idx
        rk = [_key(r) for r in reads]
        wk = [_key(w) for w in writes]
        if self._barrier is not None:
            o.deps.add(self._barrier)
        for r in rk:
            lw = self.last_writer.get(r)
            if lw is not None:
                o.deps.add(lw)
        for w in wk:
            lw = self.last_writer.get(w)
            if lw is not None:
                o.deps.add(lw)
            for rd in self.readers.get(w, ()):
                o.deps.add(rd)
        for r in rk:
            self.readers.setdefault(r, []).append(idx)
        for w in wk:
            self.last_writer[w] = idx
            self.readers[w] = []
        o.deps.discard(idx)
        self.ops.append(o)
        return idx

    def dma(self, queue, out, in_, reads=(), writes=(), **kw):
        self.op(queue, lambda e: e.dma_start(out=out, in_=in_, **kw),
                reads=list(reads), writes=list(writes), dma=True)

    def fence(self, keys):
        self.op("ve", lambda v: v.engine_nop(), writes=list(keys))

    def fence_all(self):
        keys = set(self.last_writer) | set(self.readers)
        idx = self.op("ve", lambda v: v.engine_nop(), writes=sorted(keys))
        self._barrier = idx

    def emit(self, stack, tail="sync"):
        nc = self.nc
        ops = self.ops
        for o in ops:
            if o.dma:
                o.inc = True
        for o in ops:
            for d in o.deps:
                p = ops[d]
                if p.eng == "pe" and o.eng == "pe":
                    continue
                p.inc = True
        cum = {}
        for o in ops:
            if o.inc:
                key = (o.eng, o.dma)
                cum[key] = cum.get(key, 0) + (16 if o.dma else 1)
                o.cum = cum[key]
        stream_vc = {e: {} for e in ENGINES}
        op_vc = [None] * len(ops)
        for o in ops:
            vce = stream_vc[o.eng]
            waits = {}
            for d in sorted(o.deps):
                p = ops[d]
                if p.eng == "pe" and o.eng == "pe":
                    continue
                key = (p.eng, p.dma)
                need = p.cum
                if vce.get(key, 0) >= need or waits.get(key, 0) >= need:
                    continue
                waits[key] = need
                pv = op_vc[d]
                if pv:
                    for k2, v2 in pv.items():
                        if vce.get(k2, 0) < v2:
                            vce[k2] = v2
            for k2, v2 in waits.items():
                if vce.get(k2, 0) < v2:
                    vce[k2] = v2
            o.waits = waits
            snap = dict(vce)
            if o.inc:
                k = (o.eng, o.dma)
                snap[k] = max(snap.get(k, 0), o.cum)
            op_vc[o.idx] = snap

        sems = {}
        for key in cum:
            sems[key] = stack.enter_context(
                nc.semaphore(f"s_{key[0]}_{key[1] or 'c'}"))
        streams = {e: [] for e in ENGINES}
        for o in ops:
            streams[o.eng].append(o)

        block = stack.enter_context(nc.Block())

        def mk(engname):
            def body(eng):
                for o in streams[engname]:
                    for key, val in sorted(o.waits.items(), key=lambda kv: (kv[0][0], str(kv[0][1]))):
                        eng.wait_ge(sems[key], val)
                    inst = o.emit(eng)
                    if o.inc:
                        inst.then_inc(sems[(o.eng, o.dma)], 16 if o.dma else 1)
                if engname == tail:
                    for key, val in sorted(cum.items(), key=lambda kv: (kv[0][0], str(kv[0][1]))):
                        if key[0] == engname and not key[1]:
                            continue
                        eng.wait_ge(sems[key], val)
            return body

        block.sync(mk("sync"))
        block.vector(mk("ve"))
        block.scalar(mk("sc"))
        block.tensor(mk("pe"))
        block.gpsimd(mk("gp"))


class Ctx:
    def __init__(self, nc, stack):
        self.nc = nc
        self.stack = stack
        self.p = Prog(nc)
        self.n = 0
        self.ident = None
        self.prefix = ""
        self.pre = {}
        self.garena = None
        self.gcur = 0
        self.gpsum = None
        self.gpsum_i = 0

    def enable_shared(self, stack):
        nc = self.nc
        self.gpsum = [
            stack.enter_context(nc.psum_tensor("Gpsm", [128, 2048], F32)),
            stack.enter_context(nc.psum_tensor("Gpso", [128, 512], F32)),
            stack.enter_context(nc.psum_tensor("Gpsb", [128, 512], F32)),
            stack.enter_context(nc.psum_tensor("Gpst0", [128, 512], F32)),
            stack.enter_context(nc.psum_tensor("Gpst1", [128, 512], F32)),
        ]
        avail = (nc.sbuf_bytes_remaining - 512) // 32 * 32
        self.garena = stack.enter_context(
            nc.sbuf_tensor("GARENA", [128, avail], U8))
        self.gbytes = avail

    def block_reset(self):
        self.gcur = 0
        self.gpsum_i = 0

    def avail_bytes(self):
        if self.garena is not None:
            return self.gbytes - ((self.gcur + 31) // 32 * 32)
        return self.nc.sbuf_bytes_remaining

    def sb(self, shape, dtype, name=None):
        self.n += 1
        name = self.prefix + (name or f"sb{self.n}")
        if self.garena is None:
            return self.stack.enter_context(
                self.nc.sbuf_tensor(name, shape, dtype))
        assert len(shape) == 2 and shape[0] <= 128, shape
        nb = shape[1] * mybir.dt.size(dtype)
        off = (self.gcur + 31) // 32 * 32
        self.gcur = off + nb
        assert self.gcur <= self.gbytes, (name, self.gcur, self.gbytes)
        return carve(self.garena[:, :], off, shape[1], dtype)[0:shape[0], :]

    def ps(self, shape, dtype, name=None):
        self.n += 1
        name = self.prefix + (name or f"ps{self.n}")
        if self.gpsum is None:
            return self.stack.enter_context(
                self.nc.psum_tensor(name, shape, dtype))
        t = self.gpsum[self.gpsum_i]
        self.gpsum_i += 1
        assert list(t.shape) == list(shape) and dtype == F32, (shape, dtype)
        return t


def carve(arena_ap, byte_off, cols, dtype):
    size = mybir.dt.size(dtype)
    return arena_ap[:, byte_off:byte_off + cols * size].bitcast(dtype)



def emit_transposes(ctx, srcs, dsts, pst_bf, pst_keys):
    p = ctx.p
    nb = len(srcs)
    bi = 0
    i = 0
    while i < nb:
        g = min(8, nb - i)
        pt = pst_bf[bi % 2]
        ptk = pst_keys[bi % 2]
        for j in range(g):
            sap, sk = srcs[i + j]
            p.op("pe",
                 (lambda t, o=pt, jj=j, a=sap: t.transpose(
                     o[:, jj * 128:(jj + 1) * 128], a, ctx.ident[:, :])),
                 reads=[sk, "c_ident"], writes=[ptk])
        if len(dsts[i]) == 3 and g > 1 and all(
                d[0] is dsts[i][0] and d[2] == dsts[i][2]
                and d[1] == dsts[i][1] + 128 * jj
                for jj, d in enumerate(dsts[i:i + g])):
            par, c0, dk = dsts[i]
            p.op("ve",
                 (lambda v, o=pt, g=g, par=par, c0=c0: v.tensor_copy(
                     par[:, c0:c0 + g * 128], o[:, 0:g * 128])),
                 reads=[ptk], writes=[dk])
        else:
            for j in range(g):
                d = dsts[i + j]
                if len(d) == 3:
                    dap, dk = d[0][:, d[1]:d[1] + 128], d[2]
                else:
                    dap, dk = d
                p.op("ve",
                     (lambda v, o=pt, jj=j, a=dap: v.tensor_copy(
                         a, o[:, jj * 128:(jj + 1) * 128])),
                     reads=[ptk], writes=[dk])
        bi += 1
        i += g


def emit_ln(ctx, yr_ap, yr_key, out_ap, out_key, scr, width):
    p = ctx.p
    s1, ss, mean, ms, var, rstd, sqd = (
        scr["s1"], scr["ss"], scr["mean"], scr["ms"], scr["var"], scr["rstd"],
        scr["sqd"])
    p.op("ve", lambda v: v.reduce_sum(s1[:, :], yr_ap, AX.X),
         reads=[yr_key], writes=["ln_s1"])
    p.op("sc", lambda s: s.activation(sqd[:, :width], yr_ap, AF.Square,
                                      accum_out=ss[:, 0:1]),
         reads=[yr_key], writes=["ln_sqd", "ln_ss"])
    p.op("ve", lambda v: v.tensor_scalar(mean[:, :], s1[:, :], 1.0 / width, None,
                                         ALU.mult),
         reads=["ln_s1"], writes=["ln_mean"])
    p.op("ve", lambda v: v.tensor_mul(ms[:, :], mean[:, :], mean[:, :]),
         reads=["ln_mean"], writes=["ln_ms"])
    p.op("ve", lambda v: v.scalar_tensor_tensor(var[:, :], ss[:, :], 1.0 / width,
                                                ms[:, :], ALU.mult, ALU.subtract),
         reads=["ln_ss", "ln_ms"], writes=["ln_var"])
    p.op("sc", lambda s: s.activation(rstd[:, :], var[:, :], AF.Ln,
                                      bias=ctx.eps_col[:, 0:1]),
         reads=["ln_var", "c_eps"], writes=["ln_rstd"])
    p.op("sc", lambda s: s.activation(rstd[:, :], rstd[:, :], AF.Exp,
                                      scale=-0.5),
         reads=["ln_rstd"], writes=["ln_rstd"])
    p.op("ve", lambda v: v.tensor_scalar(out_ap, yr_ap, mean[:, 0:1],
                                         rstd[:, 0:1], ALU.subtract, ALU.mult),
         reads=[yr_key, "ln_mean", "ln_rstd"], writes=[out_key])



def emit_l0_attn(ctx, cfg, io):
    nc, p = ctx.nc, ctx.p
    xin_key = io.get("xin_key") or (lambda s, n: None)
    xout_key = io.get("xout_key") or (lambda s, n: f"x_out@{s}_{n}")
    T, D, H, B = cfg["T"], cfg["D"], cfg["H"], cfg["B"]
    DH = 64
    assert D // H == DH
    NT = T // 128
    DC = D // 128
    half = D // 2
    QB = min(512, T)
    NQB = T // QB
    NQT = QB // 128
    KB = min(512, T)
    NK5 = T // KB
    NKC = T // 128
    G64 = D // 64
    W5 = min(512, D)
    NH5 = D // W5

    ctx.ident = ctx.sb([128, 128], BF16, "identsb")
    p.dma("sync", ctx.ident[:, :], io["ident"], writes=[("c_ident", None)])
    ctx.eps_col = ctx.sb([128, 1], F32, "eps_col")
    p.op("ve", lambda v: v.memset(ctx.eps_col[:, :], 1e-5), writes=["c_eps"])
    ones1 = ctx.sb([1, 128], BF16, "ones1")
    p.op("ve", lambda v: v.memset(ones1[:, :], 1.0), writes=["c_ones1"])

    psm = ctx.ps([128, 2048], F32, "psm")
    pso = ctx.ps([128, 512], F32, "pso")
    psb = ctx.ps([128, 512], F32, "psb")
    pst = [ctx.ps([128, 512], F32, "pst0"), ctx.ps([128, 512], F32, "pst1")]
    pst_bf = [t[:, :].bitcast(BF16) for t in pst]
    PSTK = ["pst0", "pst1"]

    def psm_keys(c0, c1):
        return [f"psm{b}" for b in range(c0 // 512, (c1 + 511) // 512)]

    arena = ctx.sb([128, 40960], U8, "arenaA")
    ar = arena[:, :]
    xbf4 = [carve(ar, 0, 4 * D, BF16), carve(ar, 8192, 4 * D, BF16)]
    wb4 = carve(ar, 16384, 4 * D, BF16)
    r4 = carve(ar, 24576, 4 * D, BF16)
    cosb = [carve(ar, 32768, half, F32), carve(ar, 32768 + 4096, half, F32)]
    sinb = [carve(ar, 32768 + 2048, half, F32),
            carve(ar, 32768 + 6144, half, F32)]
    QKV_KEYS = ["xbf4_0", "xbf4_1", "wb4", "r4", "cos0", "cos1", "sin0", "sin1"]
    xr = [carve(ar, 0, D, F32), carve(ar, 4096, D, F32)]
    yrt = carve(ar, 8192, D, F32)
    ot = [carve(ar, 12288, D, F32), carve(ar, 16384, D, F32)]
    sqd = carve(ar, 20480, D, F32)
    buB = carve(ar, 24576, D, F32)
    bu_row = carve(ar, 28672, D, F32)[0:1, :]
    bu_bf = carve(ar, 32768, D, BF16)[0:1, :]
    EPI_KEYS = ["xr0", "xr1", "yr", "ot0", "ot1", "ln_sqd", "buB", "bu_row",
                "bu_bf"]

    XT = [ctx.sb([128, T], BF16, f"XTt{c}") for c in range(DC)]
    XTK = [f"XT{c}" for c in range(DC)]
    OTb = [XT[c][:, 0:T] for c in range(DC)]
    PTt = [carve(ar, 34816, 1024, BF16), carve(ar, 36864, 1024, BF16),
           carve(ar, 0, 1024, BF16), carve(ar, 2048, 1024, BF16)]
    rec2 = [carve(ar, 38912, 512, F32)[0:64, :],
            ctx.sb([64, 512], F32, "recb")[:, :]]
    OTK = [f"OT{c}" for c in range(DC)]
    PTK = ["PT0", "PT1", "PT2", "PT3"]
    ATT_KEYS = OTK + PTK + ["rec0", "rec1"]

    nmax = ctx.sb([128, 12], F32, "nmax")
    hsum = ctx.sb([128, 16], F32, "hsum")
    qn2 = ctx.sb([128, 16], F32, "qn2")
    kn2 = ctx.sb([128, 16], F32, "kn2")
    nrmsc = ctx.sb([128, 16], F32, "nrmsc")
    negcB = ctx.sb([128, 16], F32, "negcB")
    ctx.crow = ctx.sb([1, 64], F32, "crow")
    pf1 = ctx.sb([128, D], F32, "projf0")
    rt1_1 = ctx.sb([128, half], F32, "ropet1_0")
    rt2_1 = ctx.sb([128, half], F32, "ropet2_0")
    pf = [pf1, pf1]
    rt1 = [rt1_1, rt1_1]
    rt2 = [rt2_1, rt2_1]
    scr = {k: ctx.sb([128, 1], F32, f"lns_{k}") for k in
           ("s1", "ss", "mean", "ms", "var", "rstd")}
    scr["sqd"] = sqd

    QT = [ctx.sb([128, T], BF16, f"QTt{c}") for c in range(DC)]
    KTt = [ctx.sb([128, T], BF16, f"KTt{c}") for c in range(DC)]
    V = [ctx.sb([128, D], BF16, f"Vt{n}") for n in range(NT)]
    wt = [ctx.sb([128, D], BF16, f"wtA{c}") for c in range(DC)]
    Vst = [ctx.sb([128, T], BF16, f"Vst{i}") for i in range(2)]
    for i in range(2):
        p.op("ve", (lambda v, i=i: v.memset(Vst[i][:, :], 1.0)),
             writes=[f"Vst{i}"])

    def prep_wt(wname, dram_ap):
        src_r = dram_ap.rearrange("(n p) i -> p n i", p=128)
        for g in range(0, DC, 4):
            gl = min(4, DC - g)
            p.dma("gp", wb4[:, 0:gl * D].rearrange("p (n i) -> p n i", n=gl),
                  src_r[:, g:g + gl, :], writes=["wb4"])
            for c in range(DC):
                srcs = [(wb4[:, j * D + c * 128: j * D + (c + 1) * 128], "wb4")
                        for j in range(gl)]
                par = wt[c][:, :]
                dsts = [(par, (g + j) * 128, f"wt{c}")
                        for j in range(gl)]
                emit_transposes(ctx, srcs, dsts, pst_bf, PSTK)

    for s in range(B):
        x_seq = io["x_in"][s]
        xo_seq = io["x_out"][s]
        x_r = x_seq.rearrange("(m p) d -> p m d", p=128)
        xo_r = xo_seq.rearrange("(m p) d -> p m d", p=128)

        p.fence(QKV_KEYS + EPI_KEYS + XTK + ATT_KEYS)

        for g in range(0, NT, 4):
            gl = min(4, NT - g)
            xb = xbf4[(g // 4) % 2]
            xbk = f"xbf4_{(g // 4) % 2}"
            p.dma("gp", xb[:, 0:gl * D].rearrange("p (n d) -> p n d", n=gl),
                  x_r[:, g:g + gl, :],
                  reads=[k for k in (xin_key(s, g + j) for j in range(gl))
                         if k], writes=[xbk])
            for c in range(DC):
                srcs = [(xb[:, j * D + c * 128: j * D + (c + 1) * 128], xbk)
                        for j in range(gl)]
                par = XT[c][:, :]
                dsts = [(par, (g + j) * 128, XTK[c])
                        for j in range(gl)]
                emit_transposes(ctx, srcs, dsts, pst_bf, PSTK)

        def proj_pass(wname, kind):
            dst_feat = QT if kind == "q" else KTt if kind == "k" else None
            dfk = "QT" if kind == "q" else "KT"
            for g in range(0, NT, 4):
                gl = min(4, NT - g)
                for j in range(gl):
                    n = g + j
                    if kind in ("q", "k"):
                        cb, sbn = cosb[n % 2], sinb[n % 2]
                        ck, sk = f"cos{n % 2}", f"sin{n % 2}"
                        p.dma("sync", cb[:, :],
                              io["rope_cos"].rearrange("(m p) h -> p m h",
                                                       p=128)[:, n, :],
                              writes=[ck])
                        p.dma("sync", sbn[:, :],
                              io["rope_sin"].rearrange("(m p) h -> p m h",
                                                       p=128)[:, n, :],
                              writes=[sk])
                    pbase = (n % 2) * D if 2 * D <= 2048 else 0
                    for hf in range(NH5):
                        for ic in range(DC):
                            p.op("pe",
                                 (lambda t, n=n, hf=hf, ic=ic, pb=pbase:
                                  t.matmul(psm[:, pb + hf * W5:
                                               pb + (hf + 1) * W5],
                                           XT[ic][:, n * 128:(n + 1) * 128],
                                           wt[ic][:, hf * W5:(hf + 1) * W5],
                                           start=(ic == 0),
                                           stop=(ic == DC - 1))),
                                 reads=[XTK[ic], f"wt{ic}"],
                                 writes=psm_keys(pbase + hf * W5,
                                                 pbase + (hf + 1) * W5))
                    pskeys = psm_keys(pbase, pbase + D)
                    if kind == "v":
                        p.op("ve", (lambda v, n=n, pb=pbase: v.tensor_copy(
                            V[n][:, 0:D], psm[:, pb:pb + D])),
                            reads=pskeys, writes=[f"V{n}"])
                        continue
                    pfn, pfk = pf[n % 2], "projf0"
                    r1n, r1k = rt1[n % 2], "ropet1_0"
                    r2n, r2k = rt2[n % 2], "ropet2_0"
                    if kind == "q":
                        p.op("ve", (lambda v, pfn=pfn, pb=pbase:
                                    v.tensor_scalar(
                            pfn[:, :], psm[:, pb:pb + D], 0.125, None,
                            ALU.mult)), reads=pskeys, writes=[pfk])
                    else:
                        p.op("ve", (lambda v, pfn=pfn, pb=pbase: v.tensor_copy(
                            pfn[:, :], psm[:, pb:pb + D])),
                            reads=pskeys, writes=[pfk])
                    pe_ = pfn[:, :].rearrange("p (h two) -> p two h", two=2)
                    ev, od = pe_[:, 0, :], pe_[:, 1, :]
                    dst = r4[:, j * D:(j + 1) * D].rearrange(
                        "p (h two) -> p two h", two=2)
                    dev, dod = dst[:, 0, :], dst[:, 1, :]
                    p.op("ve", (lambda v, ev=ev, cb=cb, r1n=r1n: v.tensor_mul(
                        r1n[:, :], ev, cb[:, :])),
                        reads=[pfk, ck], writes=[r1k])
                    p.op("ve", (lambda v, od=od, sbn=sbn, r2n=r2n:
                                v.tensor_mul(r2n[:, :], od, sbn[:, :])),
                        reads=[pfk, sk], writes=[r2k])
                    p.op("ve", (lambda v, dev=dev, r1n=r1n, r2n=r2n:
                                v.tensor_sub(dev, r1n[:, :], r2n[:, :])),
                        reads=[r1k, r2k], writes=["r4"])
                    p.op("ve", (lambda v, ev=ev, sbn=sbn, r1n=r1n:
                                v.tensor_mul(r1n[:, :], ev, sbn[:, :])),
                        reads=[pfk, sk], writes=[r1k])
                    p.op("ve", (lambda v, od=od, cb=cb, r2n=r2n: v.tensor_mul(
                        r2n[:, :], od, cb[:, :])),
                        reads=[pfk, ck], writes=[r2k])
                    p.op("ve", (lambda v, dod=dod, r1n=r1n, r2n=r2n:
                                v.tensor_add(dod, r1n[:, :], r2n[:, :])),
                        reads=[r1k, r2k], writes=["r4"])
                    nt2, nt2k = (qn2, "qn2") if kind == "q" else (kn2, "kn2")
                    p.op("sc", (lambda s_, j=j, pfn=pfn: s_.activation(
                        pfn[:, :], r4[:, j * D:(j + 1) * D], AF.Square)),
                        reads=["r4"], writes=[pfk])
                    p.op("ve", (lambda v, pfn=pfn: v.tensor_reduce(
                        hsum[:, 0:H], pfn[:, :].rearrange(
                            "p (h c) -> p h c", c=64), AX.X, ALU.add)),
                        reads=[pfk], writes=["hsum"])
                    p.op("ve", (lambda v, nt2=nt2: v.tensor_max(
                        nt2[:, 0:H], nt2[:, 0:H], hsum[:, 0:H])),
                        reads=["hsum", nt2k], writes=[nt2k])
                if kind in ("q", "k"):
                    for c in range(DC):
                        srcs = [(r4[:, j * D + c * 128: j * D + (c + 1) * 128],
                                 "r4") for j in range(gl)]
                        par = dst_feat[c][:, :]
                        dsts = [(par, (g + j) * 128,
                                 f"{dfk}{c}") for j in range(gl)]
                        emit_transposes(ctx, srcs, dsts, pst_bf, PSTK)

        p.op("ve", lambda v: v.memset(qn2[:, 0:H], 0.0), writes=["qn2"])
        p.op("ve", lambda v: v.memset(kn2[:, 0:H], 0.0), writes=["kn2"])
        prep_wt("wq", io["wq"])
        proj_pass("wq", "q")
        prep_wt("wk", io["wk"])
        proj_pass("wk", "k")
        prep_wt("wv", io["wv"])
        proj_pass("wv", "v")
        prep_wt("wu", io["wu"])

        hsbf = hsum[:, 0:H].bitcast(BF16)
        p.op("ve", lambda v: v.tensor_copy(hsbf[:, 0:H], qn2[:, 0:H]),
             reads=["qn2"], writes=["hsum"])
        p.op("ve", lambda v: v.tensor_copy(hsbf[:, H:2 * H], kn2[:, 0:H]),
             reads=["kn2"], writes=["hsum"])
        p.op("pe", lambda t: t.transpose(
            pst_bf[0][0:2 * H, 0:128], hsbf[:, 0:2 * H], ctx.ident[:, :]),
            reads=["hsum", "c_ident"], writes=["pst0"])
        p.op("ve", lambda v: v.tensor_reduce(
            nrmsc[0:2 * H, 0:1], pst_bf[0][0:2 * H, 0:128], AX.X, ALU.max),
            reads=["pst0"], writes=["nrmsc"])
        nrbf = nrmsc[:, 1:2].bitcast(BF16)
        p.op("ve", lambda v: v.tensor_copy(nrbf[0:2 * H, 0:1],
                                           nrmsc[0:2 * H, 0:1]),
             reads=["nrmsc"], writes=["nrmsc"])
        p.op("pe", lambda t: t.transpose(
            pst_bf[1][0:1, 0:2 * H], nrbf[0:2 * H, 0:1],
            ctx.ident[0:2 * H, 0:2 * H]),
            reads=["nrmsc", "c_ident"], writes=["pst1"])
        crow = ctx.crow
        p.op("ve", lambda v: v.tensor_copy(crow[0:1, 0:2 * H],
                                           pst_bf[1][0:1, 0:2 * H]),
             reads=["pst1"], writes=["crow"])
        p.op("ve", lambda v: v.tensor_mul(
            crow[0:1, 32:32 + H], crow[0:1, 0:H], crow[0:1, H:2 * H]),
            reads=["crow"], writes=["crow"])
        p.op("ve", lambda v: v.tensor_scalar_max(
            crow[0:1, 32:32 + H], crow[0:1, 32:32 + H], 1e-12),
            reads=["crow"], writes=["crow"])
        p.op("sc", lambda s_: s_.activation(
            crow[0:1, 32:32 + H], crow[0:1, 32:32 + H], AF.Ln),
            reads=["crow"], writes=["crow"])
        p.op("sc", lambda s_: s_.activation(
            crow[0:1, 32:32 + H], crow[0:1, 32:32 + H], AF.Exp, scale=0.5),
            reads=["crow"], writes=["crow"])
        crbf = crow[:, 48:56].bitcast(BF16)
        p.op("ve", (lambda v: v.tensor_scalar(
            crbf[0:1, 0:H], crow[0:1, 32:32 + H], -1.0, None, ALU.mult)),
            reads=["crow"], writes=["crow"])
        p.op("pe", lambda t: t.matmul(
            psb[:, 0:H], ones1[0:1, :], crbf[0:1, 0:H],
            start=True, stop=True),
            reads=["crow", "c_ones1"], writes=["psb"])
        p.op("ve", lambda v: v.tensor_copy(negcB[:, 0:H], psb[:, 0:H]),
             reads=["psb"], writes=["negcB"])

        p.fence(QKV_KEYS + EPI_KEYS + XTK + ATT_KEYS)

        p.dma("sync", bu_row[0:1, :], io["bu"].unsqueeze(0), writes=["bu_row"])
        p.op("ve", lambda v: v.tensor_copy(bu_bf[0:1, :], bu_row[0:1, :]),
             reads=["bu_row"], writes=["bu_bf"])
        for i in range(NH5):
            p.op("pe", (lambda t, i=i: t.matmul(
                psb[:, 0:W5], ones1[0:1, :], bu_bf[0:1, i * W5:(i + 1) * W5],
                start=True, stop=True)),
                reads=["bu_bf", "c_ones1"], writes=["psb"])
            p.op("ve", (lambda v, i=i: v.tensor_copy(
                buB[:, i * W5:(i + 1) * W5], psb[:, 0:W5])),
                reads=["psb"], writes=["buB"])

        for ch in range(H // 2):
            heads = [2 * ch, 2 * ch + 1]
            for h in heads:
                vs, vsk = Vst[h % 2], f"Vst{h % 2}"
                for kc in range(NKC):
                    p.op("gp", (lambda g, kc=kc, h=h, vs=vs: g.tensor_copy(
                        vs[:, kc * 128:kc * 128 + 64],
                        V[kc][:, h * 64:(h + 1) * 64])),
                        reads=[f"V{kc}"], writes=[vsk])
            for qb in range(NQB):
                pend = []
                for kc in range(NKC):
                    if QB == 512 and kc % 2 == 1:
                        continue
                    kcs = (kc, kc + 1) if QB == 512 else (kc,)
                    pp = (kc // len(kcs)) % 2
                    for h in heads:
                        hb = (h % 2) * 64
                        bb = (h % 2) * 2
                        for jj, kcj in enumerate(kcs):
                            bank = psm[:, (bb + jj) * 512:
                                       (bb + jj) * 512 + QB]
                            p.op("pe", (lambda t, kcj=kcj, ch=ch, hb=hb,
                                        qb=qb, bank=bank: t.matmul(
                                            bank,
                                            KTt[ch][hb:hb + 64,
                                                    kcj * 128:
                                                    (kcj + 1) * 128],
                                            QT[ch][hb:hb + 64,
                                                   qb * QB:(qb + 1) * QB],
                                            start=True, stop=True)),
                                reads=[f"QT{ch}", f"KT{ch}"],
                                writes=[f"psm{bb + jj}"])
                    for h in heads:
                        hb = (h % 2) * 64
                        bb = (h % 2) * 2
                        vs, vsk = Vst[h % 2], f"Vst{h % 2}"
                        pvo = pso if h % 2 == 0 else psb
                        pvk = "pso" if h % 2 == 0 else "psb"
                        pt = (h % 2) * 2 + pp
                        pb, ptk = PTt[pt], PTK[pt]
                        w = len(kcs) * QB
                        p.op("sc", (lambda s_, h=h, bb=bb, pb=pb, w=w:
                                    s_.activation(
                                        pb[:, 0:w],
                                        psm[:, bb * 512:bb * 512 + w],
                                        AF.Exp, bias=negcB[:, h:h + 1])),
                            reads=[f"psm{bb + j}" for j in
                                   range(len(kcs))] + ["negcB"],
                            writes=[ptk])
                        for j, kcj in enumerate(kcs):
                            pend.append((lambda t, kcj=kcj, j=j, pb=pb,
                                         vs=vs, pvo=pvo: t.matmul(
                                pvo[:, 0:NQT * 128],
                                vs[:, kcj * 128:(kcj + 1) * 128],
                                pb[:, j * QB:(j + 1) * QB],
                                start=(kcj == 0), stop=(kcj == NKC - 1)),
                                [vsk, ptk], [pvk]))
                    nkeep = 2 * len(kcs)
                    if len(pend) > nkeep:
                        for em, rd, wr in pend[:-nkeep]:
                            p.op("pe", em, reads=rd, writes=wr)
                        pend = pend[-nkeep:]
                for em, rd, wr in pend:
                    p.op("pe", em, reads=rd, writes=wr)
                for h in heads:
                    hb = (h % 2) * 64
                    pvo = pso if h % 2 == 0 else psb
                    pvk = "pso" if h % 2 == 0 else "psb"
                    rc, rck = rec2[h % 2], f"rec{h % 2}"
                    p.op("sc", (lambda s_, pvo=pvo, rc=rc: s_.activation(
                        rc[0:64, 0:NQT * 128], pvo[64:128, 0:NQT * 128],
                        AF.Ln)),
                        reads=[pvk], writes=[rck])
                    p.op("sc", (lambda s_, rc=rc: s_.activation(
                        rc[0:64, 0:NQT * 128], rc[0:64, 0:NQT * 128], AF.Exp,
                        scale=-1.0)),
                        reads=[rck], writes=[rck])
                    p.op("ve", (lambda v, ch=ch, hb=hb, pvo=pvo, rc=rc,
                                qb=qb: v.tensor_tensor(
                        OTb[ch][hb:hb + 64, qb * QB:(qb + 1) * QB],
                        pvo[0:64, 0:NQT * 128],
                        rc[0:64, 0:NQT * 128], ALU.mult)),
                        reads=[pvk, rck], writes=[OTK[ch]])

        p.fence(["PT2", "PT3", "xr0", "xr1"])
        if True:
            for n in range(NT):
                xk = f"xr{n % 2}"
                p.dma("sync", xr[n % 2][:, :], x_r[:, n, :],
                      reads=[k for k in [xin_key(s, n)] if k], writes=[xk])
                pbase = (n % 2) * D if 2 * D <= 2048 else 0
                for hf in range(NH5):
                    for ic in range(DC):
                        p.op("pe", (lambda t, n=n, hf=hf, ic=ic, pb=pbase:
                                    t.matmul(
                            psm[:, pb + hf * W5:pb + (hf + 1) * W5],
                            OTb[ic][:, n * 128:(n + 1) * 128],
                            wt[ic][:, hf * W5:(hf + 1) * W5],
                            start=(ic == 0), stop=(ic == DC - 1))),
                            reads=[OTK[ic], f"wt{ic}"],
                            writes=psm_keys(pbase + hf * W5,
                                            pbase + (hf + 1) * W5))
                pskeys = psm_keys(pbase, pbase + D)
                p.op("ve", (lambda v, pb=pbase: v.tensor_tensor(
                    yrt[:, :], psm[:, pb:pb + D], buB[:, :], ALU.add)),
                    reads=pskeys + ["buB"], writes=["yr"])
                p.op("ve", (lambda v, n=n: v.tensor_add(
                    yrt[:, :], yrt[:, :], xr[n % 2][:, :])),
                    reads=["yr", xk], writes=["yr"])
                ob, obk = ot[n % 2], f"ot{n % 2}"
                emit_ln(ctx, yrt[:, :], "yr", ob[:, :], obk, scr, D)
                p.dma("gp", xo_r[:, n, :], ob[:, :], reads=[obk],
                      writes=[xout_key(s, n)])



def emit_l0_ffn(ctx, cfg, io):
    nc, p = ctx.nc, ctx.p
    _NTs = cfg["T"] // 128
    xin_key = io.get("xin_key") or (lambda s, n: None)
    xout_key = io.get("xout_key") or (lambda s, n: f"x_out@{s}_{n}")
    xin_g = lambda m: xin_key(m // _NTs, m % _NTs)
    xout_g = lambda m: xout_key(m // _NTs, m % _NTs)
    T, D, B = cfg["T"], cfg["D"], cfg["B"]
    F1, F2 = cfg["F1"], cfg["F2"]
    TT = B * T
    TB = min(512, TT)
    NTB = TT // TB
    NTT = TB // 128
    DC = D // 128
    NO1 = F1 // 128
    NO2 = F2 // 128
    assert DC <= 8
    SWK = min(8, NO2)
    SW2 = (NO2 + SWK - 1) // SWK

    ctx.ident = ctx.sb([128, 128], BF16, "identsb")
    p.dma("sync", ctx.ident[:, :], io["ident"], writes=[("c_ident", None)])
    ctx.eps_col = ctx.sb([128, 1], F32, "eps_col")
    p.op("ve", lambda v: v.memset(ctx.eps_col[:, :], 1e-5), writes=["c_eps"])
    onescol = ctx.sb([128, 1], BF16, "onescol")
    p.op("ve", lambda v: v.memset(onescol[:, :], 1.0), writes=["c_onescol"])
    neghalf = ctx.sb([1, 128], BF16, "neghalf")
    p.op("ve", lambda v: v.memset(neghalf[:, :], -0.5), writes=["c_neghalf"])

    scol = {}
    acol = {}
    for nm, mul in (("s0", 2.0), ("s1", 4.0), ("s2", 4.0)):
        scol[nm] = ctx.sb([128, 1], F32, f"{nm}col")
        acol[nm] = ctx.sb([128, 1], F32, f"{nm}acol")
        p.dma("sync", scol[nm][:, :], io[nm + "c"], writes=[f"{nm}c"])
        p.op("ve", (lambda v, nm=nm, mul=mul: v.tensor_scalar(
            acol[nm][:, :], scol[nm][:, :], mul, None, ALU.mult)),
            reads=[f"{nm}c"], writes=[f"{nm}a"])

    psm = ctx.ps([128, 2048], F32, "psm")
    pso = ctx.ps([128, 512], F32, "pso")
    psb = ctx.ps([128, 512], F32, "psb")
    pst = [ctx.ps([128, 512], F32, "pst0"), ctx.ps([128, 512], F32, "pst1")]
    pst_bf = [t[:, :].bitcast(BF16) for t in pst]
    PSTK = ["pst0", "pst1"]
    banks = [(psm[:, i * 512:(i + 1) * 512], f"psm{i}") for i in range(4)] + \
            [(pso[:, :], "pso"), (psb[:, :], "psb"),
             (pst[0][:, :], "pst0"), (pst[1][:, :], "pst1")]

    K0d = [ctx.sb([128, F1], BF16, f"K0d{c}") for c in range(DC)]
    h1T = [ctx.sb([128, TB], BF16, f"h1T{c}") for c in range(NO1)]
    h2T = [ctx.sb([128, TB], BF16, f"h2T{c}") for c in range(NO2)]
    XT2 = [[ctx.sb([128, TB], BF16, f"XTt{b}_{c}") for c in range(DC)]
           for b in range(2)]
    ySB = [ctx.sb([128, TB], BF16, f"ySB{c}") for c in range(DC)]
    k1sb = [ctx.sb([128, SWK * 128], BF16, f"k1sb{i}") for i in range(2)]
    k2sb = [ctx.sb([128, D], BF16, f"k2sb{i}") for i in range(2)]
    _xb1 = ctx.sb([128, 4 * D], BF16, "xbf4_0")
    xbf4 = [_xb1, _xb1]
    xr = [ctx.sb([128, D], F32, f"xr{i}") for i in range(2)]
    yrt = ctx.sb([128, D], F32, "yrt")
    ot = [ctx.sb([128, D], F32, f"ott{i}") for i in range(2)]
    scr = {k: ctx.sb([128, 1], F32, f"lns_{k}") for k in
           ("s1", "ss", "mean", "ms", "var", "rstd")}
    scr["sqd"] = ctx.sb([128, D], F32, "lnsqd")
    sg32 = [ctx.sb([1, TB], F32, "sg32a"), ctx.sb([1, TB], F32, "sg32b")]
    sghi = [ctx.sb([1, TB], BF16, "sghia"), ctx.sb([1, TB], BF16, "sghib")]
    sgh32 = ctx.sb([1, TB], F32, "sgh32")
    sglo = [ctx.sb([1, TB], BF16, "sgloa"), ctx.sb([1, TB], BF16, "sglob")]

    if "k1bf" in ctx.pre:
        k1bf, k2bf = ctx.pre["k1bf"], ctx.pre["k2bf"]
    else:
        k1bf = nc.dram_tensor(ctx.prefix + "k1bf", [F1, F2], BF16,
                              kind="Internal").ap()
        k2bf = nc.dram_tensor(ctx.prefix + "k2bf", [F2, D], BF16,
                              kind="Internal").ap()
        for q in range(4):
            r0, r1 = q * F1 // 4, (q + 1) * F1 // 4
            p.dma("gp", k1bf[r0:r1, :], io["k1"][r0:r1, :],
                  writes=[f"k1bf{q}"])
            r0, r1 = q * F2 // 4, (q + 1) * F2 // 4
            p.dma("gp", k2bf[r0:r1, :], io["k2"][r0:r1, :],
                  writes=[f"k2bf{q}"])

    def k1bf_key(ic):
        return f"k1bf{(ic * 128) * 4 // F1}"

    def k2bf_key(ic):
        return f"k2bf{(ic * 128) * 4 // F2}"

    KW = SWK * 128
    if "k0bf" in ctx.pre:
        k0bf = ctx.pre["k0bf"]
        for ic in range(DC):
            for qi, q0 in enumerate(range(0, F1, KW)):
                qw = min(KW, F1 - q0)
                kb = qi % 2
                p.dma("sync", K0d[ic][:, q0:q0 + qw],
                      k0bf[ic * 128:(ic + 1) * 128, q0:q0 + qw],
                      writes=[f"K0d{ic}"])
                p.op("sc", (lambda e, ic=ic, q0=q0, qw=qw, kb=kb:
                            e.dma_start(
                                out=k1sb[kb][:, 0:qw],
                                in_=k0bf[D + ic * 128:D + (ic + 1) * 128,
                                         q0:q0 + qw])),
                     writes=[f"k1sb{kb}"], dma=True)
                p.op("ve", (lambda v, ic=ic, q0=q0, qw=qw, kb=kb:
                            v.tensor_sub(
                                K0d[ic][:, q0:q0 + qw],
                                K0d[ic][:, q0:q0 + qw], k1sb[kb][:, 0:qw])),
                    reads=[f"k1sb{kb}", f"K0d{ic}"], writes=[f"K0d{ic}"])
    else:
        for ic in range(DC):
            for q0 in range(0, F1, KW):
                qw = min(KW, F1 - q0)
                p.dma("gp", k1sb[0][:, 0:qw],
                      io["k0"][ic * 128:(ic + 1) * 128, q0:q0 + qw],
                      writes=["k1sb0"])
                p.dma("gp", k1sb[1][:, 0:qw],
                      io["k0"][D + ic * 128:D + (ic + 1) * 128, q0:q0 + qw],
                      writes=["k1sb1"])
                p.op("ve", (lambda v, ic=ic, q0=q0, qw=qw: v.tensor_sub(
                    K0d[ic][:, q0:q0 + qw], k1sb[0][:, 0:qw],
                    k1sb[1][:, 0:qw])),
                    reads=["k1sb0", "k1sb1"], writes=[f"K0d{ic}"])

    x_flat = io["x_in"].rearrange("b t d -> (b t) d")
    xo_flat = io["x_out"].rearrange("b t d -> (b t) d")
    x_r = x_flat.rearrange("(m p) d -> p m d", p=128)
    xo_r = xo_flat.rearrange("(m p) d -> p m d", p=128)

    def sigma_rows(src_tiles, src_keys, idx):
        n = len(src_tiles)
        for i in range(n):
            p.op("pe", (lambda t, i=i: t.matmul(
                pso[0:1, 0:TB], onescol[:, 0:1], src_tiles[i][:, 0:TB],
                start=(i == 0), stop=(i == n - 1))),
                reads=[src_keys[i], "c_onescol"], writes=["pso"])
        g32, ghi, glo = sg32[idx], sghi[idx], sglo[idx]
        kk = ("sg1", "sghi1", "sglo1") if idx == 0 else ("sg2", "sghi2", "sglo2")
        p.op("ve", lambda v: v.tensor_copy(g32[0:1, :], pso[0:1, 0:TB]),
             reads=["pso"], writes=[kk[0]])
        p.op("ve", lambda v: v.tensor_copy(ghi[0:1, :], g32[0:1, :]),
             reads=[kk[0]], writes=[kk[1]])
        p.op("ve", lambda v: v.tensor_copy(sgh32[0:1, :], ghi[0:1, :]),
             reads=[kk[1]], writes=["sgh32"])
        p.op("ve", lambda v: v.tensor_sub(glo[0:1, :], g32[0:1, :],
                                          sgh32[0:1, :]),
             reads=[kk[0], "sgh32"], writes=[kk[2]])
        return ghi, glo, kk[1], kk[2]

    for tb in range(NTB):
        t0 = tb * NTT
        XT = XT2[tb % 2]
        XK = [f"XT{tb % 2}_{c}" for c in range(DC)]
        for g in range(0, NTT, 4):
            gl = min(4, NTT - g)
            xb = xbf4[(g // 4) % 2]
            xbk = "xbf4_0"
            p.dma("gp", xb[:, 0:gl * D].rearrange("p (n d) -> p n d", n=gl),
                  x_r[:, t0 + g:t0 + g + gl, :],
                  reads=[k for k in (xin_g(t0 + g + j) for j in range(gl))
                         if k], writes=[xbk])
            for c in range(DC):
                srcs = [(xb[:, j * D + c * 128: j * D + (c + 1) * 128], xbk)
                        for j in range(gl)]
                par = XT[c][:, :]
                dsts = [(par, (g + j) * 128, XK[c])
                        for j in range(gl)]
                emit_transposes(ctx, srcs, dsts, pst_bf, PSTK)

        for oc in range(NO1):
            bank, bk = banks[4 + oc % 2]
            for ic in range(DC):
                p.op("pe", (lambda t, oc=oc, ic=ic, bank=bank, XT=XT:
                            t.matmul(
                    bank[:, 0:TB], K0d[ic][:, oc * 128:(oc + 1) * 128],
                    XT[ic][:, 0:TB], start=(ic == 0), stop=(ic == DC - 1))),
                    reads=[f"K0d{ic}", XK[ic]], writes=[bk])
            p.op("sc", (lambda s, oc=oc, bank=bank: s.activation(
                h1T[oc][:, 0:TB], bank[:, 0:TB], AF.Relu,
                scale=acol["s0"][:, 0:1])),
                reads=[bk, "s0a"], writes=[f"h1T{oc}"])

        ghi1, glo1, hk1, lk1 = sigma_rows(h1T, [f"h1T{i}" for i in range(NO1)], 0)

        for sw in range(SW2):
            oc0 = sw * SWK
            K = min(SWK, NO2 - oc0)
            grp = 0
            for ic in range(NO1):
                kb = k1sb[ic % 2]
                kbk = f"k1sb{ic % 2}"
                p.dma("sync", kb[:, 0:K * 128],
                      k1bf[ic * 128:(ic + 1) * 128,
                           oc0 * 128:(oc0 + K) * 128],
                      reads=[k1bf_key(ic)], writes=[kbk])
                for ocl in range(K):
                    bank, bk = banks[grp + ocl]
                    p.op("pe", (lambda t, ic=ic, ocl=ocl, kb=kb, bank=bank:
                                t.matmul(bank[:, 0:TB],
                                         kb[:, ocl * 128:(ocl + 1) * 128],
                                         h1T[ic][:, 0:TB],
                                         start=(ic == 0), stop=False)),
                        reads=[kbk, f"h1T{ic}"], writes=[bk])
            for ocl in range(K):
                bank, bk = banks[grp + ocl]
                p.op("pe", (lambda t, bank=bank: t.matmul(
                    bank[:, 0:TB], neghalf[0:1, :], ghi1[0:1, :],
                    start=False, stop=False)),
                    reads=["c_neghalf", hk1], writes=[bk])
                p.op("pe", (lambda t, bank=bank: t.matmul(
                    bank[:, 0:TB], neghalf[0:1, :], glo1[0:1, :],
                    start=False, stop=True)),
                    reads=["c_neghalf", lk1], writes=[bk])
                p.op("sc", (lambda s, oc=oc0 + ocl, bank=bank: s.activation(
                    h2T[oc][:, 0:TB], bank[:, 0:TB], AF.Relu,
                    scale=acol["s1"][:, 0:1])),
                    reads=[bk, "s1a"], writes=[f"h2T{oc0 + ocl}"])

        ghi2, glo2, hk2, lk2 = sigma_rows(h2T, [f"h2T{i}" for i in range(NO2)], 1)

        for ic in range(NO2):
            kb = k2sb[ic % 2]
            kbk = f"k2sb{ic % 2}"
            p.dma("sync", kb[:, 0:D],
                  k2bf[ic * 128:(ic + 1) * 128, :],
                  reads=[k2bf_key(ic)], writes=[kbk])
            for oc in range(DC):
                bank, bk = banks[oc]
                p.op("pe", (lambda t, ic=ic, oc=oc, kb=kb, bank=bank:
                            t.matmul(bank[:, 0:TB],
                                     kb[:, oc * 128:(oc + 1) * 128],
                                     h2T[ic][:, 0:TB],
                                     start=(ic == 0), stop=False)),
                    reads=[kbk, f"h2T{ic}"], writes=[bk])
        for oc in range(DC):
            bank, bk = banks[oc]
            p.op("pe", (lambda t, bank=bank: t.matmul(
                bank[:, 0:TB], neghalf[0:1, :], ghi2[0:1, :],
                start=False, stop=False)),
                reads=["c_neghalf", hk2], writes=[bk])
            p.op("pe", (lambda t, bank=bank: t.matmul(
                bank[:, 0:TB], neghalf[0:1, :], glo2[0:1, :],
                start=False, stop=True)),
                reads=["c_neghalf", lk2], writes=[bk])
            p.op("sc", (lambda s, oc=oc, bank=bank: s.activation(
                ySB[oc][:, 0:TB], bank[:, 0:TB], AF.Copy,
                scale=acol["s2"][:, 0:1])),
                reads=[bk, "s2a"], writes=[f"ySB{oc}"])

        for tt in range(NTT):
            n = t0 + tt
            xk = f"xr{tt % 2}"
            p.dma("sync", xr[tt % 2][:, :], x_r[:, n, :],
                  reads=[k for k in [xin_g(n)] if k], writes=[xk])
            pi = tt % 2
            for c in range(DC):
                p.op("pe", (lambda t, c=c, tt=tt, pi=pi: t.transpose(
                    pst_bf[pi][:, c * 128:(c + 1) * 128],
                    ySB[c][:, tt * 128:(tt + 1) * 128], ctx.ident[:, :])),
                    reads=[f"ySB{c}", "c_ident"], writes=[PSTK[pi]])
            p.op("ve", (lambda v, pi=pi, xk=xk, tt=tt: v.tensor_tensor(
                yrt[:, :], pst_bf[pi][:, 0:D], xr[tt % 2][:, :], ALU.add)),
                reads=[PSTK[pi], xk], writes=["yr"])
            ob, obk = ot[tt % 2], f"ot{tt % 2}"
            emit_ln(ctx, yrt[:, :], "yr", ob[:, :], obk, scr, D)
            p.dma("gp", xo_r[:, n, :], ob[:, :], reads=[obk],
                  writes=[xout_g(n)])


def build_l0_ffn_nc(cfg):
    T, D, B = cfg["T"], cfg["D"], cfg["B"]
    F1, F2 = cfg["F1"], cfg["F2"]
    nc = bass.Bass("TRN2", target_bir_lowering=False)
    io = {
        "x_in": nc.dram_tensor("x_in", [B, T, D], F32, kind="ExternalInput").ap(),
        "k0": nc.dram_tensor("k0", [2 * D, F1], U8, kind="ExternalInput").ap(),
        "k1": nc.dram_tensor("k1", [F1, F2], U8, kind="ExternalInput").ap(),
        "k2": nc.dram_tensor("k2", [F2, D], U8, kind="ExternalInput").ap(),
        "s0c": nc.dram_tensor("s0c", [128, 1], F32, kind="ExternalInput").ap(),
        "s1c": nc.dram_tensor("s1c", [128, 1], F32, kind="ExternalInput").ap(),
        "s2c": nc.dram_tensor("s2c", [128, 1], F32, kind="ExternalInput").ap(),
        "ident": nc.dram_tensor("ident", [128, 128], BF16,
                                kind="ExternalInput").ap(),
        "x_out": nc.dram_tensor("x_out", [B, T, D], F32,
                                kind="ExternalOutput").ap(),
    }
    stack = contextlib.ExitStack()
    with stack:
        ctx = Ctx(nc, stack)
        emit_l0_ffn(ctx, cfg, io)
        ctx.p.emit(stack, tail="sync")
    return nc



def emit_l1_attn(ctx, cfg, io):
    nc, p = ctx.nc, ctx.p
    xin_key = io.get("xin_key") or (lambda s, n: None)
    xout_key = io.get("xout_key") or (lambda s, n: f"x_out@{s}_{n}")
    T, D, H, B = cfg["T"], cfg["D"], cfg["H"], cfg["B"]
    DH = 64
    assert D // H == DH
    NT = T // 128
    NKC = NT
    DC = D // 128
    QB = min(512, T)
    NQB = T // QB
    NQT = QB // 128
    KB = min(512, T)
    OFFS = T - 128
    RW = KB + OFFS
    MBASE = max(0, OFFS - 512)
    MW = OFFS + KB - MBASE
    W5 = min(512, D)
    NH5 = D // W5

    ctx.ident = ctx.sb([128, 128], BF16, "identsb")
    p.dma("sync", ctx.ident[:, :], io["ident"], writes=[("c_ident", None)])
    ctx.eps_col = ctx.sb([128, 1], F32, "eps_col")
    p.op("ve", lambda v: v.memset(ctx.eps_col[:, :], 1e-5), writes=["c_eps"])
    ones1 = ctx.sb([1, 128], BF16, "ones1")
    p.op("ve", lambda v: v.memset(ones1[:, :], 1.0), writes=["c_ones1"])

    psm = ctx.ps([128, 2048], F32, "psm")
    pso = ctx.ps([128, 512], F32, "pso")
    psb = ctx.ps([128, 512], F32, "psb")
    pst = [ctx.ps([128, 512], F32, "pst0"), ctx.ps([128, 512], F32, "pst1")]
    pst_bf = [t[:, :].bitcast(BF16) for t in pst]
    PSTK = ["pst0", "pst1"]

    def psm_keys(c0, c1):
        return [f"psm{b}" for b in range(c0 // 512, (c1 + 511) // 512)]

    I16 = mybir.dt.int16
    rstrip = ctx.sb([128, RW], I16, "rstrip")
    mstrip = ctx.sb([128, KB], F32, "mstrip")
    p.op("gp", lambda g: g.iota(rstrip[:, :], [[-1, RW]], base=0,
                                channel_multiplier=1,
                                allow_small_or_imprecise_dtypes=True),
         writes=["rstrip"])
    p.op("ve", lambda v: v.tensor_scalar(
        mstrip[:, :], rstrip[:, 0:KB], 0.0, NEG_BIG,
        ALU.is_gt, ALU.mult),
        reads=["rstrip"], writes=["mstrip"])

    slopes = ctx.sb([128, H], F32, "slopes")
    p.dma("sync", slopes[:, :], io["slopes_bc"], writes=["slopes"])
    sqs = ctx.sb([128, 512], BF16, "sqs")
    onescol2 = ctx.sb([128, 1], BF16, "onescol2")
    p.op("ve", lambda v: v.memset(onescol2[:, :], 1.0), writes=["c_onescol"])
    nrow = ctx.sb([1, 64], F32, "nrow")
    crow = ctx.sb([1, 64], F32, "crow")
    negcB = ctx.sb([128, 16], F32, "negcB")

    XT = [ctx.sb([128, T], BF16, f"XTt{c}") for c in range(DC)]
    XTK = [f"XT{c}" for c in range(DC)]
    QT = [ctx.sb([128, T], BF16, f"QTt{c}") for c in range(DC)]
    KTt = [ctx.sb([128, T], BF16, f"KTt{c}") for c in range(DC)]
    VT = [ctx.sb([128, T], BF16, f"VTt{c}") for c in range(DC)]
    Vst = [ctx.sb([128, T], BF16, f"Vst{i}") for i in range(2)]
    for i in range(2):
        p.op("ve", (lambda v, i=i: v.memset(Vst[i][:, :], 1.0)),
             writes=[f"Vst{i}"])
    wcol = [ctx.sb([128, D], BF16, f"wcol{i}") for i in range(2)]
    bcol = ctx.sb([128, 1], F32, "bcol")
    wpb = [ctx.sb([128, D], BF16, f"wpb{c}") for c in range(DC)]

    OTb = [XT[c][:, 0:T] for c in range(DC)]
    _reca = ctx.sb([64, 512], F32, "reca")[:, :]
    rec2 = [_reca, _reca]
    OTK = [f"OT{c}" for c in range(DC)]
    PTK = ["PT0", "PT1", "PT2", "PT3"]
    ATT_KEYS = OTK + PTK + ["rec0", "rec1"]

    xbf4_1 = ctx.sb([128, 4 * D], BF16, "xbf4_0")
    xbf4 = [xbf4_1, xbf4_1]
    PTt = [ctx.sb([128, 512], BF16, "PTa")[:, :],
           ctx.sb([128, 512], BF16, "PTb")[:, :],
           xbf4_1[:, 0:512], xbf4_1[:, 512:1024]]
    xr = [ctx.sb([128, D], F32, f"xr{i}") for i in range(2)]
    yrt = ctx.sb([128, D], F32, "yrt")
    ot = [ctx.sb([128, D], F32, f"ott{i}") for i in range(2)]
    scr = {k: ctx.sb([128, 1], F32, f"lns_{k}") for k in
           ("s1", "ss", "mean", "ms", "var", "rstd")}
    scr["sqd"] = ctx.sb([128, D], F32, "lnsqd")
    buB = ctx.sb([128, D], F32, "buB")
    bu_bf = ctx.sb([1, D], BF16, "bu_bf")

    for c in range(DC):
        p.dma("gp", wpb[c][:, :],
              io["wproj"][c * 128:(c + 1) * 128, :], writes=[f"wpb{c}"])

    p.dma("gp", bu_bf[0:1, :], io["bproj"].unsqueeze(0), writes=["bu_bf"])
    for i in range(NH5):
        p.op("pe", (lambda t, i=i: t.matmul(
            psb[:, 0:W5], ones1[0:1, :], bu_bf[0:1, i * W5:(i + 1) * W5],
            start=True, stop=True)),
            reads=["bu_bf", "c_ones1"], writes=["psb"])
        p.op("ve", (lambda v, i=i: v.tensor_copy(
            buB[:, i * W5:(i + 1) * W5], psb[:, 0:W5])),
            reads=["psb"], writes=["buB"])

    for s in range(B):
        x_seq = io["x_in"][s]
        xo_seq = io["x_out"][s]
        x_r = x_seq.rearrange("(m p) d -> p m d", p=128)
        xo_r = xo_seq.rearrange("(m p) d -> p m d", p=128)

        p.fence(XTK + ATT_KEYS + ["xbf4_0"])

        for g in range(0, NT, 4):
            gl = min(4, NT - g)
            xb = xbf4[0]
            xbk = "xbf4_0"
            p.dma("gp", xb[:, 0:gl * D].rearrange("p (n d) -> p n d", n=gl),
                  x_r[:, g:g + gl, :],
                  reads=[k for k in (xin_key(s, g + j) for j in range(gl))
                         if k], writes=[xbk])
            for c in range(DC):
                srcs = [(xb[:, j * D + c * 128: j * D + (c + 1) * 128], xbk)
                        for j in range(gl)]
                par = XT[c][:, :]
                dsts = [(par, (g + j) * 128, XTK[c])
                        for j in range(gl)]
                emit_transposes(ctx, srcs, dsts, pst_bf, PSTK)

        wkqv_r = io["wkqv"].rearrange("(n p) f -> p n f", p=128)
        for oc in range(3 * DC):
            wc = wcol[oc % 2]
            wck = f"wcol{oc % 2}"
            p.dma("gp", wc[:, :].rearrange("p (n c) -> p n c", n=DC),
                  wkqv_r[:, :, oc * 128:(oc + 1) * 128], writes=[wck])
            p.dma("sync", bcol[:, :],
                  io["bkqv"][oc * 128:(oc + 1) * 128].unsqueeze(1),
                  writes=["bcol"])
            if oc < DC:
                dst, dk, scl = KTt[oc], f"KT{oc}", 1.0
            elif oc < 2 * DC:
                dst, dk, scl = QT[oc - DC], f"QT{oc - DC}", 0.125
            else:
                dst, dk, scl = VT[oc - 2 * DC], f"VT{oc - 2 * DC}", 1.0
            if scl != 1.0:
                p.op("ve", lambda v: v.tensor_scalar(
                    bcol[:, :], bcol[:, :], 0.125, None, ALU.mult),
                    reads=["bcol"], writes=["bcol"])
            for t5 in range(T // KB):
                bank = psm[:, (t5 % 2) * 512:(t5 % 2) * 512 + KB]
                bk = f"psm{t5 % 2}"
                for ic in range(DC):
                    p.op("pe", (lambda t, t5=t5, ic=ic, wc=wc, bank=bank:
                                t.matmul(bank,
                                         wc[:, ic * 128:(ic + 1) * 128],
                                         XT[ic][:, t5 * KB:(t5 + 1) * KB],
                                         start=(ic == 0), stop=(ic == DC - 1))),
                        reads=[wck, XTK[ic]], writes=[bk])
                p.op("ve", (lambda v, t5=t5, dst=dst, scl=scl, bank=bank:
                            v.tensor_scalar(dst[:, t5 * KB:(t5 + 1) * KB],
                                            bank, scl, bcol[:, 0:1],
                                            ALU.mult, ALU.add)),
                    reads=[bk, "bcol"], writes=[dk])

        p.op("ve", lambda v: v.memset(nrow[0:1, :], 0.0), writes=["nrow"])
        for kind, tiles, tk in (("q", QT, "QT"), ("k", KTt, "KT")):
            for h in range(H):
                ch, hb = h // 2, (h % 2) * 64
                col = (0 if kind == "q" else 32) + h
                for t5 in range(T // KB):
                    p.op("sc", (lambda s_, ch=ch, hb=hb, t5=t5, tiles=tiles,
                                tk=tk: s_.activation(
                                    sqs[hb:hb + 64, 0:KB],
                                    tiles[ch][hb:hb + 64,
                                              t5 * KB:(t5 + 1) * KB],
                                    AF.Square)),
                        reads=[f"{tk}{ch}"], writes=["sqs"])
                    p.op("pe", (lambda t, hb=hb: t.matmul(
                        psb[0:1, 0:KB], onescol2[hb:hb + 64, 0:1],
                        sqs[hb:hb + 64, 0:KB], start=True, stop=True)),
                        reads=["sqs", "c_onescol"], writes=["psb"])
                    p.op("ve", (lambda v: v.tensor_reduce(
                        crow[0:1, 32:33], psb[0:1, 0:KB], AX.X, ALU.max)),
                        reads=["psb"], writes=["crow"])
                    p.op("ve", (lambda v, col=col: v.tensor_max(
                        nrow[0:1, col:col + 1], nrow[0:1, col:col + 1],
                        crow[0:1, 32:33])),
                        reads=["crow", "nrow"], writes=["nrow"])
        p.op("ve", lambda v: v.tensor_mul(
            crow[0:1, 0:H], nrow[0:1, 0:H], nrow[0:1, 32:32 + H]),
            reads=["nrow"], writes=["crow"])
        p.op("ve", lambda v: v.tensor_scalar_max(
            crow[0:1, 0:H], crow[0:1, 0:H], 1e-12),
            reads=["crow"], writes=["crow"])
        p.op("sc", lambda s_: s_.activation(
            crow[0:1, 0:H], crow[0:1, 0:H], AF.Ln),
            reads=["crow"], writes=["crow"])
        p.op("sc", lambda s_: s_.activation(
            crow[0:1, 0:H], crow[0:1, 0:H], AF.Exp, scale=0.5),
            reads=["crow"], writes=["crow"])
        p.op("ve", lambda v: v.tensor_scalar(
            crow[0:1, 32:32 + H], slopes[0:1, 0:H], -(T - 1.0), None,
            ALU.mult),
            reads=["slopes"], writes=["crow"])
        p.op("ve", lambda v: v.tensor_scalar_max(
            crow[0:1, 32:32 + H], crow[0:1, 32:32 + H], 0.0),
            reads=["crow"], writes=["crow"])
        p.op("ve", lambda v: v.tensor_add(
            crow[0:1, 0:H], crow[0:1, 0:H], crow[0:1, 32:32 + H]),
            reads=["crow"], writes=["crow"])
        crbf = crow[:, 48:56].bitcast(BF16)
        p.op("ve", lambda v: v.tensor_scalar(
            crbf[0:1, 0:H], crow[0:1, 0:H], -1.0, None, ALU.mult),
            reads=["crow"], writes=["crow"])
        p.op("pe", lambda t: t.matmul(
            psb[:, 0:H], ones1[0:1, :], crbf[0:1, 0:H],
            start=True, stop=True),
            reads=["crow", "c_ones1"], writes=["psb"])
        p.op("ve", lambda v: v.tensor_copy(negcB[:, 0:H], psb[:, 0:H]),
             reads=["psb"], writes=["negcB"])

        p.fence(XTK + ATT_KEYS)

        for ch in range(H // 2):
            heads = [2 * ch, 2 * ch + 1]
            for h in heads:
                hb = (h % 2) * 64
                vs, vsk = Vst[h % 2], f"Vst{h % 2}"
                for kg in range(0, NKC, 8):
                    gl = min(8, NKC - kg)
                    pi = (kg // 8) % 2
                    for j in range(gl):
                        kc = kg + j
                        p.op("pe", (lambda t, kc=kc, j=j, pi=pi, ch=ch,
                                    hb=hb: t.transpose(
                                        pst_bf[pi][:, j * 64:(j + 1) * 64],
                                        VT[ch][hb:hb + 64,
                                               kc * 128:(kc + 1) * 128],
                                        ctx.ident[hb:hb + 64, hb:hb + 64])),
                            reads=[f"VT{ch}", "c_ident"], writes=[PSTK[pi]])
                    p.op("ve", (lambda v, kg=kg, gl=gl, pi=pi, vs=vs:
                                v.tensor_copy(
                                    vs[:, kg * 128:(kg + gl) * 128].rearrange(
                                        "p (k c) -> p k c", c=128)[:, :, 0:64],
                                    pst_bf[pi][:, 0:gl * 64].rearrange(
                                        "p (k c) -> p k c", c=64))),
                        reads=[PSTK[pi]], writes=[vsk])
            for qb in range(NQB):
                NKCp = qb * NQT + NQT
                pend = []
                for kc in range(NKCp):
                    q_off = max(0, kc - qb * NQT)
                    nq = NQT - q_off
                    qc0 = qb * QB + q_off * 128
                    o = qc0 - kc * 128
                    for h in heads:
                        hb = (h % 2) * 64
                        bb = (h % 2) * 2 + kc % 2
                        bank = psm[:, bb * 512:bb * 512 + nq * 128]
                        p.op("pe", (lambda t, kc=kc, ch=ch, hb=hb, qc0=qc0,
                                    nq=nq, bank=bank, qb=qb: t.matmul(
                                        bank,
                                        KTt[ch][hb:hb + 64,
                                                kc * 128:(kc + 1) * 128],
                                        QT[ch][hb:hb + 64,
                                               qc0:qb * QB + QB],
                                        start=True, stop=True)),
                            reads=[f"QT{ch}", f"KT{ch}"],
                            writes=[f"psm{bb}"])
                    for h in heads:
                        hb = (h % 2) * 64
                        bb = (h % 2) * 2 + kc % 2
                        bank = psm[:, bb * 512:bb * 512 + nq * 128]
                        bks = [f"psm{bb}"]
                        pt = (h % 2) * 2 + kc % 2
                        pb, ptk = PTt[pt], PTK[pt]
                        p.op("ve", (lambda v, o=o, h=h, nq=nq, bank=bank:
                                    v.scalar_tensor_tensor(
                                        bank, rstrip[:, o:o + nq * 128],
                                        slopes[:, h:h + 1], bank,
                                        ALU.mult, ALU.add)),
                            reads=["rstrip", "slopes"] + bks, writes=bks)
                        if kc >= qb * NQT:
                            p.op("ve", (lambda v, nq=nq, bank=bank:
                                        v.tensor_tensor(
                                            bank, mstrip[:, 0:nq * 128],
                                            bank, ALU.add)),
                                reads=["mstrip"] + bks, writes=bks)
                        p.op("sc", (lambda s_, h=h, bank=bank, pb=pb, nq=nq:
                                    s_.activation(pb[:, 0:nq * 128], bank,
                                                  AF.Exp,
                                                  bias=negcB[:, h:h + 1])),
                            reads=bks + ["negcB"], writes=[ptk])
                        vs, vsk = Vst[h % 2], f"Vst{h % 2}"
                        pvo = pso if h % 2 == 0 else psb
                        pvk = "pso" if h % 2 == 0 else "psb"
                        pend.append((lambda t, kc=kc, pb=pb, vs=vs,
                                     q_off=q_off, nq=nq, NKCp=NKCp, pvo=pvo:
                                     t.matmul(
                                         pvo[:, q_off * 128:NQT * 128],
                                         vs[:, kc * 128:(kc + 1) * 128],
                                         pb[:, 0:nq * 128],
                                         start=(kc == 0),
                                         stop=(kc == NKCp - 1)),
                                     [vsk, ptk], [pvk]))
                    if len(pend) > 2:
                        for em, rd, wr in pend[:-2]:
                            p.op("pe", em, reads=rd, writes=wr)
                        pend = pend[-2:]
                for em, rd, wr in pend:
                    p.op("pe", em, reads=rd, writes=wr)
                for h in heads:
                    hb = (h % 2) * 64
                    pvo = pso if h % 2 == 0 else psb
                    pvk = "pso" if h % 2 == 0 else "psb"
                    rc, rck = rec2[0], "rec0"
                    p.op("sc", (lambda s_, pvo=pvo, rc=rc: s_.activation(
                        rc[0:64, 0:NQT * 128], pvo[64:128, 0:NQT * 128],
                        AF.Ln)),
                        reads=[pvk], writes=[rck])
                    p.op("sc", (lambda s_, rc=rc: s_.activation(
                        rc[0:64, 0:NQT * 128], rc[0:64, 0:NQT * 128],
                        AF.Exp, scale=-1.0)),
                        reads=[rck], writes=[rck])
                    p.op("ve", (lambda v, ch=ch, hb=hb, pvo=pvo, rc=rc,
                                qb=qb: v.tensor_tensor(
                        OTb[ch][hb:hb + 64, qb * QB:(qb + 1) * QB],
                        pvo[0:64, 0:NQT * 128],
                        rc[0:64, 0:NQT * 128], ALU.mult)),
                        reads=[pvk, rck], writes=[OTK[ch]])

        if True:
            for n in range(NT):
                xk = f"xr{n % 2}"
                p.dma("sync", xr[n % 2][:, :], x_r[:, n, :],
                      reads=[k for k in [xin_key(s, n)] if k], writes=[xk])
                pbase = (n % 2) * D if 2 * D <= 2048 else 0
                for hf in range(NH5):
                    for ic in range(DC):
                        p.op("pe", (lambda t, n=n, hf=hf, ic=ic, pb=pbase:
                                    t.matmul(
                            psm[:, pb + hf * W5:pb + (hf + 1) * W5],
                            OTb[ic][:, n * 128:(n + 1) * 128],
                            wpb[ic][:, hf * W5:(hf + 1) * W5],
                            start=(ic == 0), stop=(ic == DC - 1))),
                            reads=[OTK[ic], f"wpb{ic}"],
                            writes=psm_keys(pbase + hf * W5,
                                            pbase + (hf + 1) * W5))
                pskeys = psm_keys(pbase, pbase + D)
                p.op("ve", (lambda v, pb=pbase: v.tensor_tensor(
                    yrt[:, :], psm[:, pb:pb + D], buB[:, :], ALU.add)),
                    reads=pskeys + ["buB"], writes=["yr"])
                p.op("ve", (lambda v, n=n: v.tensor_add(
                    yrt[:, :], yrt[:, :], xr[n % 2][:, :])),
                    reads=["yr", xk], writes=["yr"])
                ob, obk = ot[n % 2], f"ot{n % 2}"
                emit_ln(ctx, yrt[:, :], "yr", ob[:, :], obk, scr, D)
                p.dma("gp", xo_r[:, n, :], ob[:, :], reads=[obk],
                      writes=[xout_key(s, n)])


def build_l1_attn_nc(cfg):
    T, D, H, B = cfg["T"], cfg["D"], cfg["H"], cfg["B"]
    nc = bass.Bass("TRN2", target_bir_lowering=False)
    io = {
        "x_in": nc.dram_tensor("x_in", [B, T, D], F32, kind="ExternalInput").ap(),
        "wkqv": nc.dram_tensor("wkqv", [D, 3 * D], F32,
                               kind="ExternalInput").ap(),
        "bkqv": nc.dram_tensor("bkqv", [3 * D], F32, kind="ExternalInput").ap(),
        "wproj": nc.dram_tensor("wproj", [D, D], F32, kind="ExternalInput").ap(),
        "bproj": nc.dram_tensor("bproj", [D], F32, kind="ExternalInput").ap(),
        "slopes_bc": nc.dram_tensor("slopes_bc", [128, H], F32,
                                    kind="ExternalInput").ap(),
        "ident": nc.dram_tensor("ident", [128, 128], BF16,
                                kind="ExternalInput").ap(),
        "x_out": nc.dram_tensor("x_out", [B, T, D], F32,
                                kind="ExternalOutput").ap(),
    }
    stack = contextlib.ExitStack()
    with stack:
        ctx = Ctx(nc, stack)
        emit_l1_attn(ctx, cfg, io)
        ctx.p.emit(stack, tail="sync")
    return nc



class Cursor:
    def __init__(self, arena_ap, start=0):
        self.ap = arena_ap
        self.off = start

    def alloc(self, cols, dtype):
        size = mybir.dt.size(dtype)
        off = (self.off + 31) // 32 * 32
        v = carve(self.ap, off, cols, dtype)
        self.off = off + cols * size
        return v


def emit_l1_ffn(ctx, cfg, io):
    nc, p = ctx.nc, ctx.p
    xin_key = io.get("xin_key") or (lambda s, n: None)
    xout_key = io.get("xout_key") or (lambda s, n: f"x_out@{s}_{n}")
    T, D, B = cfg["T"], cfg["D"], cfg["B"]
    HF = cfg["HF"]
    WW = cfg["WW"]
    HID = cfg["HID"]
    INNER = HF * 64
    NT = T // 128
    DC = D // 128
    NIC = INNER // 128
    NW = T // WW
    NWT = WW // 128
    NJ = (HID + 127) // 128
    OFF2 = 2 * WW - 128
    RW2 = 3 * WW - 128
    assert DC % 2 == 0

    ctx.ident = ctx.sb([128, 128], BF16, "identsb")
    p.dma("sync", ctx.ident[:, :], io["ident"], writes=[("c_ident", None)])
    ctx.eps_col = ctx.sb([128, 1], F32, "eps_col")
    p.op("ve", lambda v: v.memset(ctx.eps_col[:, :], 1e-5), writes=["c_eps"])
    ones1 = ctx.sb([1, 128], BF16, "ones1")
    p.op("ve", lambda v: v.memset(ones1[:, :], 1.0), writes=["c_ones1"])
    onescol = ctx.sb([128, 1], BF16, "onescol")
    p.op("ve", lambda v: v.memset(onescol[:, :], 1.0), writes=["c_onescol"])

    I16 = mybir.dt.int16
    r2 = ctx.sb([128, RW2], I16, "r2strip")
    m2 = ctx.sb([128, RW2], F32, "m2strip")
    p.op("gp", lambda g: g.iota(r2[:, :], [[1, RW2]], base=-OFF2,
                                channel_multiplier=-1,
                                allow_small_or_imprecise_dtypes=True),
         writes=["r2"])
    p.op("ve", lambda v: v.tensor_scalar(
        m2[:, :], r2[:, :], 0.0, None, ALU.is_gt),
        reads=["r2"], writes=["m2"])
    p.op("ve", lambda v: v.scalar_tensor_tensor(
        m2[:, :], r2[:, :], float(-WW) - 0.5, m2[:, :], ALU.is_lt, ALU.add),
        reads=["r2", "m2"], writes=["m2"])
    p.op("ve", lambda v: v.tensor_scalar(
        m2[:, :], m2[:, :], NEG_BIG, None, ALU.mult),
        reads=["m2"], writes=["m2"])

    psm = ctx.ps([128, 2048], F32, "psm")
    pso = ctx.ps([128, 512], F32, "pso")
    psb = ctx.ps([128, 512], F32, "psb")
    pst = [ctx.ps([128, 512], F32, "pst0"), ctx.ps([128, 512], F32, "pst1")]
    pst_bf = [t[:, :].bitcast(BF16) for t in pst]
    PSTK = ["pst0", "pst1"]
    banks = [(psm[:, i * 512:(i + 1) * 512], f"psm{i}") for i in range(4)] + \
            [(pso[:, :], "pso"), (psb[:, :], "psb"),
             (pst[0][:, :], "pst0"), (pst[1][:, :], "pst1")]

    xr = [ctx.sb([128, D], F32, f"xr{i}") for i in range(2)]
    yrt = ctx.sb([128, D], F32, "yrt")
    ot = [ctx.sb([128, D], F32, f"ott{i}") for i in range(2)]
    scr = {k: ctx.sb([128, 1], F32, f"lns_{k}") for k in
           ("s1", "ss", "mean", "ms", "var", "rstd")}
    scr["sqd"] = ctx.sb([128, D], F32, "lnsqd")
    lgc = ctx.sb([128, DC], F32, "lgc")
    lbc = ctx.sb([128, DC], F32, "lbc")
    b2c = ctx.sb([128, DC], F32, "b2c")
    qsrep = ctx.sb([128, INNER], F32, "qsrep")
    ksrep = ctx.sb([128, INNER], F32, "ksrep")
    b1a = ctx.sb([128, NJ], F32, "b1a")
    b1g = ctx.sb([128, NJ], F32, "b1g")
    rno2 = [ctx.sb([128, HF], F32, f"rnob{i}") for i in range(2)]
    nmax = ctx.sb([128, 2], F32, "nmax")
    sgq = ctx.sb([128, 512], F32, "sgq")
    qn = ctx.sb([128, INNER], F32, "qn")
    kn = ctx.sb([128, INNER], F32, "kn")
    tbl = [ctx.sb([128, INNER], BF16, f"tbl{i}") for i in range(8)]
    TBLK = [f"tbl{i}" for i in range(8)]

    ABYTES = (ctx.avail_bytes() - 512) // 32 * 32
    arena = ctx.sb([128, ABYTES], U8, "arenaL")

    cur = Cursor(arena[:, :])
    XTs = [cur.alloc(T + 4, BF16) for _ in range(DC)]
    XTS_OFF_END = cur.off
    QF = [cur.alloc(T, BF16) for _ in range(NIC)]
    KF = [cur.alloc(T, BF16) for _ in range(NIC)]
    OF = [cur.alloc(T, BF16) for _ in range(NIC)]
    VtokA = cur.alloc(NT * INNER, BF16)
    Vtok = [VtokA[:, n * INNER:(n + 1) * INNER] for n in range(NT)]
    qr4 = cur.alloc(4 * INNER, BF16)
    kr4 = cur.alloc(4 * INNER, BF16)
    A_COMMON_END = cur.off
    cur_w = Cursor(arena[:, :], A_COMMON_END)
    wqb = [cur_w.alloc(3 * INNER, BF16) for _ in range(DC)]
    xbfA = [cur_w.alloc(4 * D, BF16) for _ in range(2)]
    sgq2 = cur_w.alloc(INNER, F32)
    cur_a = Cursor(arena[:, :], A_COMMON_END)
    Pstr = [cur_a.alloc(2 * WW, BF16) for _ in range(NWT)]
    Vst = [cur_a.alloc(2 * WW, BF16) for _ in range(2)]
    PTt = [cur_a.alloc(WW, BF16) for _ in range(2)]
    rec = cur_a.alloc(512, F32)[0:64, :]
    wob = [cur_a.alloc(D, BF16) for _ in range(NIC)]
    assert max(cur_w.off, cur_a.off) <= ABYTES

    cur_e = Cursor(arena[:, :], XTS_OFF_END)
    hn = [cur_e.alloc(T, BF16) for _ in range(DC)]
    hgt_lo = min(NJ, XTS_OFF_END // (2 * T))
    HGT = [carve(arena[:, :], j * (2 * T), T, BF16) for j in range(hgt_lo)]
    for j in range(hgt_lo, NJ):
        HGT.append(cur_e.alloc(T, BF16))
    E_HGT0 = cur_e.off
    wcolA = [cur_e.alloc(D, BF16) for _ in range(2)]
    wcolG = [cur_e.alloc(D, BF16) for _ in range(2)]
    w2sb = [cur_e.alloc(D, BF16) for _ in range(2)]
    sqf = cur_e.alloc(512, BF16)
    mrow = cur_e.alloc(512, F32)[0:1, :]
    vrow = cur_e.alloc(512, F32)[0:1, :]
    rrow = cur_e.alloc(512, F32)[0:1, :]
    mrow_bf = cur_e.alloc(512, BF16)[0:1, :]
    rrow_bf = cur_e.alloc(512, BF16)[0:1, :]
    muB = cur_e.alloc(512, F32)
    rsB = cur_e.alloc(512, F32)
    gel = cur_e.alloc(512, BF16)
    ysb8 = [cur_e.alloc(512, BF16) for _ in range(DC)]
    xbf_e = carve(arena[:, :], E_HGT0, 4 * D, BF16)
    assert cur_e.off <= ABYTES, (cur_e.off, ABYTES)

    XTK = [f"XTs{c}" for c in range(DC)]
    QFK = [f"QF{c}" for c in range(NIC)]
    KFK = [f"KF{c}" for c in range(NIC)]
    OFK = [f"OF{c}" for c in range(NIC)]
    VTK = [f"Vt{n}" for n in range(NT)]
    WQK = [f"wqb{c}" for c in range(DC)]
    PSK = [f"Pst{j}" for j in range(NWT)]
    VSK = ["Vs0", "Vs1"]
    PTK = ["PT0", "PT1"]
    WOK = [f"wob{c}" for c in range(NIC)]
    HNK = [f"hn{c}" for c in range(DC)]
    HGK = [f"HG{j}" for j in range(NJ)]
    EK = (HNK + HGK + ["wcA0", "wcA1", "wcG0", "wcG1", "w2s0", "w2s1", "sqf",
                       "mrow", "vrow", "rrow", "mrowb", "rrowb", "muB", "rsB",
                       "gel"])
    AK = (XTK + QFK + KFK + OFK + VTK + WQK + PSK + VSK + PTK + WOK +
          ["qr4", "kr4"])

    xcur = nc.dram_tensor("xcur", [B, T, D], F32, kind="Internal").ap()
    if "w1bf" in ctx.pre:
        w1bf, w2bf = ctx.pre["w1bf"], ctx.pre["w2bf"]
    else:
        w1bf = nc.dram_tensor(ctx.prefix + "w1bf", [2, D, 2 * HID], BF16,
                              kind="Internal").ap()
        w2bf = nc.dram_tensor(ctx.prefix + "w2bf", [2, HID, D], BF16,
                              kind="Internal").ap()
        for l in range(2):
            for q in range(2):
                r0, r1 = q * D // 2, (q + 1) * D // 2
                p.dma("gp", w1bf[l][r0:r1, :], io["w1"][l][r0:r1, :],
                      writes=[f"w1bf{l}_{q}"])
            p.dma("gp", w2bf[l], io["w2"][l], writes=[f"w2bf{l}"])

    def xc_key(s, n):
        return f"xc@{s}_{n}"

    wqb_pref = set()
    for s in range(B):
        x_in_r = io["x_in"][s].rearrange("(m p) d -> p m d", p=128)
        xc_r = xcur[s].rearrange("(m p) d -> p m d", p=128)
        xo_r = io["x_out"][s].rearrange("(m p) d -> p m d", p=128)

        for l in range(2):
            xsrc_r = x_in_r if l == 0 else xc_r
            xsrc_keys = [xin_key(s, n) if l == 0 else xc_key(s, n)
                         for n in range(NT)]

            p.fence(AK + EK + ["sgq2"])

            for c in range(DC):
                p.op("ve", (lambda v, c=c: v.memset(XTs[c][:, 0:1], 0.0)),
                     writes=[XTK[c]])
            for g in range(0, NT, 4):
                gl = min(4, NT - g)
                xb = xbfA[(g // 4) % 2]
                xbk = f"xbfl{(g // 4) % 2}"
                rds = [k for k in ([xsrc_keys[g + j] for j in range(gl)])
                       if k is not None]
                p.op("gp", (lambda e, xb=xb, gl=gl, g=g, xsrc_r=xsrc_r:
                            e.dma_start(
                                out=xb[:, 0:gl * D].rearrange(
                                    "p (n d) -> p n d", n=gl),
                                in_=xsrc_r[:, g:g + gl, :])),
                     reads=rds, writes=[xbk], dma=True)
                for c in range(DC):
                    srcs = [(xb[:, j * D + c * 128: j * D + (c + 1) * 128],
                             xbk) for j in range(gl)]
                    par = XTs[c][:, :]
                    dsts = [(par, 1 + (g + j) * 128, XTK[c])
                            for j in range(gl)]
                    emit_transposes(ctx, srcs, dsts, pst_bf, PSTK)

            def xts_slice(c, c0, c1):
                if c < DC // 2:
                    return XTs[c][:, 1 + c0:1 + c1]
                return XTs[c][:, c0:c1]

            if (s, l) not in wqb_pref:
                for c in range(DC):
                    p.dma("gp", wqb[c][:, :],
                          io["wqkv"][l][c * 128:(c + 1) * 128, :],
                          writes=[WQK[c]])
            p.dma("sync", qsrep[:, :], io["qs_rep"][l], writes=["qsrep"])
            p.dma("sync", ksrep[:, :], io["ks_rep"][l], writes=["ksrep"])
            for c in range(DC):
                p.dma("sync", lgc[:, c:c + 1],
                      io["ln_g"][l][c * 128:(c + 1) * 128].unsqueeze(1),
                      writes=["lgc"])
                p.dma("sync", lbc[:, c:c + 1],
                      io["ln_b"][l][c * 128:(c + 1) * 128].unsqueeze(1),
                      writes=["lbc"])
                p.dma("sync", b2c[:, c:c + 1],
                      io["b2"][l][c * 128:(c + 1) * 128].unsqueeze(1),
                      writes=["b2c"])
            for j in range(NJ):
                ca0, ca1 = j * 128, min((j + 1) * 128, HID)
                p.dma("sync", b1a[0:ca1 - ca0, j:j + 1],
                      io["b1"][l][ca0:ca1].unsqueeze(1), writes=["b1a"])
                p.dma("sync", b1g[0:ca1 - ca0, j:j + 1],
                      io["b1"][l][HID + ca0:HID + ca1].unsqueeze(1),
                      writes=["b1g"])

            for n in range(NT):
                ti = n % 2
                for w, nm in ((0, "c1"), (1, "s1m"), (2, "c2"), (3, "s2m")):
                    p.dma("gp", tbl[4 * ti + w][:, :],
                          io[nm].rearrange("(m p) f -> p m f", p=128)[:, n, :],
                          writes=[TBLK[4 * ti + w]])
                for part, dstq, dk in ((0, qn, "qn"), (1, kn, "kn")):
                    bank, bk = banks[(n % 2) * 3 + part]
                    for ic in range(DC):
                        p.op("pe", (lambda t, n=n, ic=ic, part=part, bank=bank:
                                    t.matmul(
                                        bank[:, 0:INNER],
                                        xts_slice(ic, n * 128, (n + 1) * 128),
                                        wqb[ic][:, part * INNER:
                                                (part + 1) * INNER],
                                        start=(ic == 0), stop=(ic == DC - 1))),
                            reads=[XTK[ic], WQK[ic]], writes=[bk])
                    p.op("sc", (lambda s_, bank=bank: s_.activation(
                        sgq[:, 0:INNER], bank[:, 0:INNER], AF.Square)),
                        reads=[bk], writes=["sgq"])
                    rno, rnk = rno2[part], f"rnob{part}"
                    p.op("ve", (lambda v, rno=rno: v.tensor_reduce(
                        rno[:, :], sgq[:, 0:INNER].rearrange(
                            "p (h c) -> p h c", c=64), AX.X, ALU.add)),
                        reads=["sgq"], writes=[rnk])
                    p.op("ve", (lambda v, rno=rno: v.tensor_scalar_max(
                        rno[:, :], rno[:, :], 1e-24)),
                        reads=[rnk], writes=[rnk])
                    p.op("sc", (lambda s_, rno=rno: s_.activation(
                        rno[:, :], rno[:, :], AF.Ln)),
                        reads=[rnk], writes=[rnk])
                    p.op("sc", (lambda s_, rno=rno: s_.activation(
                        rno[:, :], rno[:, :], AF.Exp, scale=-0.5)),
                        reads=[rnk], writes=[rnk])
                    rnb = rno[:, 0:HF].unsqueeze(2).to_broadcast(
                        [128, HF, 64])
                    p.op("ve", (lambda v, dstq=dstq, bank=bank, rnb=rnb:
                                v.tensor_tensor(
                                    dstq[:, 0:INNER].rearrange(
                                        "p (h c) -> p h c", c=64),
                                    bank[:, 0:INNER].rearrange(
                                        "p (h c) -> p h c", c=64),
                                    rnb, ALU.mult)),
                        reads=[bk, rnk], writes=[dk])
                    rep = qsrep if part == 0 else ksrep
                    rk = "qsrep" if part == 0 else "ksrep"
                    p.op("ve", (lambda v, dstq=dstq, rep=rep: v.tensor_mul(
                        dstq[:, 0:INNER], dstq[:, 0:INNER], rep[:, 0:INNER])),
                        reads=[dk, rk], writes=[dk])
                    ctab = tbl[4 * ti + (0 if part == 0 else 2)]
                    stab = tbl[4 * ti + (1 if part == 0 else 3)]
                    ctk = TBLK[4 * ti + (0 if part == 0 else 2)]
                    stk = TBLK[4 * ti + (1 if part == 0 else 3)]
                    dst4 = qr4 if part == 0 else kr4
                    d4k = "qr4" if part == 0 else "kr4"
                    j4 = n % 4
                    v3 = lambda a: a.rearrange("p (h two c) -> p h two c",
                                               two=2, c=32)
                    src3 = v3(dstq[:, 0:INNER])
                    ct3, st3 = v3(ctab[:, 0:INNER]), v3(stab[:, 0:INNER])
                    dst3 = v3(dst4[:, j4 * INNER:(j4 + 1) * INNER])
                    eng_r = "gp" if part == 0 else "ve"
                    sgb = sgq if part == 0 else sgq2
                    sk = "sgq" if part == 0 else "sgq2"
                    h2i = INNER // 2
                    sA, sB = sgb[:, 0:h2i], sgb[:, h2i:2 * h2i]
                    rr = lambda a: a.rearrange("p (h c) -> p h c", c=32)
                    p.op(eng_r, (lambda v, src3=src3, ct3=ct3, sA=sA:
                                 v.tensor_mul(
                        sA, src3[:, :, 0, :], ct3[:, :, 0, :])),
                        reads=[dk, ctk], writes=[sk])
                    p.op(eng_r, (lambda v, src3=src3, st3=st3, sB=sB:
                                 v.tensor_mul(
                        sB, src3[:, :, 1, :], st3[:, :, 0, :])),
                        reads=[dk, stk], writes=[sk])
                    p.op(eng_r, (lambda v, dst3=dst3, sA=sA, sB=sB, rr=rr:
                                 v.tensor_sub(
                        dst3[:, :, 0, :], rr(sA), rr(sB))),
                        reads=[sk], writes=[d4k])
                    p.op(eng_r, (lambda v, src3=src3, ct3=ct3, sA=sA:
                                 v.tensor_mul(
                        sA, src3[:, :, 1, :], ct3[:, :, 1, :])),
                        reads=[dk, ctk], writes=[sk])
                    p.op(eng_r, (lambda v, src3=src3, st3=st3, sB=sB:
                                 v.tensor_mul(
                        sB, src3[:, :, 0, :], st3[:, :, 1, :])),
                        reads=[dk, stk], writes=[sk])
                    p.op(eng_r, (lambda v, dst3=dst3, sA=sA, sB=sB, rr=rr:
                                 v.tensor_add(
                        dst3[:, :, 1, :], rr(sA), rr(sB))),
                        reads=[sk], writes=[d4k])
                bank, bk = banks[(n % 2) * 3 + 2]
                for ic in range(DC):
                    p.op("pe", (lambda t, n=n, ic=ic, bank=bank: t.matmul(
                        bank[:, 0:INNER],
                        xts_slice(ic, n * 128, (n + 1) * 128),
                        wqb[ic][:, 2 * INNER:3 * INNER],
                        start=(ic == 0), stop=(ic == DC - 1))),
                        reads=[XTK[ic], WQK[ic]], writes=[bk])
                p.op("ve", (lambda v, n=n, bank=bank: v.tensor_copy(
                    Vtok[n][:, 0:INNER], bank[:, 0:INNER])),
                    reads=[bk], writes=[VTK[n]])
                if n % 4 == 3 or n == NT - 1:
                    g0 = (n // 4) * 4
                    gl = n - g0 + 1
                    for c in range(NIC):
                        srcs = [(qr4[:, j * INNER + c * 128:
                                     j * INNER + (c + 1) * 128], "qr4")
                                for j in range(gl)]
                        par = QF[c][:, :]
                        dsts = [(par, (g0 + j) * 128,
                                 QFK[c]) for j in range(gl)]
                        emit_transposes(ctx, srcs, dsts, pst_bf, PSTK)
                        srcs = [(kr4[:, j * INNER + c * 128:
                                     j * INNER + (c + 1) * 128], "kr4")
                                for j in range(gl)]
                        par = KF[c][:, :]
                        dsts = [(par, (g0 + j) * 128,
                                 KFK[c]) for j in range(gl)]
                        emit_transposes(ctx, srcs, dsts, pst_bf, PSTK)

            p.fence(AK)

            QTPAIR = min(2 * WW, 1024)
            for w in range(NW):
                kc0 = max(0, (w - 1) * NWT)
                nkc = (w + 1) * NWT - kc0
                if w <= 1:
                    for qt in range(NWT):
                        if w == 0:
                            regs = [(qt * 128 + 128, WW)]
                        else:
                            regs = [(0, qt * 128),
                                    (qt * 128 + 128 + WW, 2 * WW)]
                        for lo, hi in regs:
                            if hi > lo:
                                p.op("ve", (lambda v, qt=qt, lo=lo, hi=hi:
                                            v.memset(
                                                Pstr[qt][:, lo:hi], 0.0)),
                                     writes=[PSK[qt]])
                for h in range(HF):
                    ch, hb = h // 2, (h % 2) * 64
                    vs = Vst[h % 2]
                    vsk = VSK[h % 2]
                    if w == 0 and h < 2:
                        p.op("ve", (lambda v, vs=vs: v.memset(vs[:, :], 1.0)),
                             writes=[vsk])
                    src3 = VtokA[:, :].rearrange(
                        "p (n c) -> p n c",
                        c=INNER)[:, kc0:kc0 + nkc, h * 64:(h + 1) * 64]
                    dst3 = vs[:, 0:nkc * 128].rearrange(
                        "p (j c) -> p j c", c=128)[:, :, 0:64]
                    p.op("gp", (lambda g, dst3=dst3, src3=src3:
                                g.tensor_copy(dst3, src3)),
                         reads=[VTK[kc0 + j] for j in range(nkc)],
                         writes=[vsk])
                    for qt in range(NWT):
                        qtg = w * NWT + qt
                        pq = qt % 2
                        qb = pq * QTPAIR
                        nkb = 2 if w > 0 else 1
                        lo = qt * 128 if w > 0 else 0
                        hi = (qt * 128 + 128 + WW) if w > 0 \
                            else (qt * 128 + 128)
                        spank = [f"psm{i}" for i in
                                 range((qb + lo) // 512,
                                       (qb + hi + 511) // 512)]
                        for kb in range(nkb):
                            wb = (w - 1 + kb) if w > 0 else w
                            if w == 0:
                                c0, c1 = 0, qt * 128 + 128
                            elif kb == 0:
                                c0, c1 = qt * 128, WW
                            else:
                                c0, c1 = 0, qt * 128 + 128
                            a0 = qb + kb * WW + c0
                            a1 = qb + kb * WW + c1
                            bks = [f"psm{i}" for i in
                                   range(a0 // 512, (a1 + 511) // 512)]
                            p.op("pe", (lambda t, qtg=qtg, wb=wb, ch=ch,
                                        hb=hb, a0=a0, a1=a1, c0=c0, c1=c1:
                                        t.matmul(
                                            psm[:, a0:a1],
                                            QF[ch][hb:hb + 64,
                                                   qtg * 128:(qtg + 1) * 128],
                                            KF[ch][hb:hb + 64,
                                                   wb * WW + c0:
                                                   wb * WW + c1],
                                            start=True, stop=True)),
                                reads=[QFK[ch], KFK[ch]], writes=bks)
                            off = (wb * WW) - qtg * 128 + OFF2
                            d0 = qb + kb * WW + qt * 128
                            dk = [f"psm{d0 // 512}"]
                            p.op("ve", (lambda v, off=off, qt=qt, d0=d0:
                                        v.tensor_tensor(
                                            psm[:, d0:d0 + 128],
                                            m2[:, off + qt * 128:
                                               off + qt * 128 + 128],
                                            psm[:, d0:d0 + 128], ALU.add)),
                                reads=["m2"] + dk, writes=dk)
                        p.op("ve", (lambda v, qb=qb, lo=lo, hi=hi, pq=pq:
                                    v.tensor_reduce(
                                        nmax[:, pq:pq + 1],
                                        psm[:, qb + lo:qb + hi], AX.X,
                                        ALU.max, negate=True)),
                            reads=spank, writes=[f"nmax{pq}"])
                        p.op("sc", (lambda s_, qt=qt, qb=qb, lo=lo, hi=hi,
                                    pq=pq: s_.activation(
                                        Pstr[qt][:, lo:hi],
                                        psm[:, qb + lo:qb + hi], AF.Exp,
                                        bias=nmax[:, pq:pq + 1])),
                            reads=spank + [f"nmax{pq}"], writes=[PSK[qt]])
                    pvpend = None
                    for j in range(nkc):
                        pb = PTt[j % 2]
                        for qt in range(NWT):
                            p.op("pe", (lambda t, j=j, qt=qt: t.transpose(
                                pst_bf[j % 2][:, qt * 128:(qt + 1) * 128],
                                Pstr[qt][:, j * 128:(j + 1) * 128],
                                ctx.ident[:, :])),
                                reads=[PSK[qt], "c_ident"],
                                writes=[PSTK[j % 2]])
                        if j % 2 == 0:
                            p.op("ve", (lambda v, j=j, pb=pb: v.tensor_copy(
                                pb[:, 0:WW], pst_bf[j % 2][:, 0:WW])),
                                reads=[PSTK[j % 2]], writes=[PTK[j % 2]])
                        else:
                            p.op("sc", (lambda s_, j=j, pb=pb: s_.activation(
                                pb[:, 0:WW], pst_bf[j % 2][:, 0:WW],
                                AF.Copy, bias=0.0)),
                                reads=[PSTK[j % 2]], writes=[PTK[j % 2]])
                        if pvpend is not None:
                            em, rd, wr = pvpend
                            p.op("pe", em, reads=rd, writes=wr)
                        pvpend = ((lambda t, j=j, pb=pb, vs=vs, nkc=nkc:
                                   t.matmul(
                                       pso[:, 0:WW],
                                       vs[:, j * 128:(j + 1) * 128],
                                       pb[:, 0:WW],
                                       start=(j == 0), stop=(j == nkc - 1))),
                                  [vsk, PTK[j % 2]], ["pso"])
                    em, rd, wr = pvpend
                    p.op("pe", em, reads=rd, writes=wr)
                    p.op("sc", lambda s_: s_.activation(
                        rec[0:64, 0:WW], pso[64:128, 0:WW], AF.Ln),
                        reads=["pso"], writes=["rec"])
                    p.op("sc", lambda s_: s_.activation(
                        rec[0:64, 0:WW], rec[0:64, 0:WW], AF.Exp,
                        scale=-1.0),
                        reads=["rec"], writes=["rec"])
                    p.op("ve", (lambda v, ch=ch, hb=hb, w=w: v.tensor_tensor(
                        OF[ch][hb:hb + 64, w * WW:(w + 1) * WW],
                        pso[0:64, 0:WW], rec[0:64, 0:WW], ALU.mult)),
                        reads=["pso", "rec"], writes=[OFK[ch]])

            for c in range(NIC):
                p.dma("gp", wob[c][:, :],
                      io["wo"][l][c * 128:(c + 1) * 128, :], writes=[WOK[c]])
            W5 = min(512, D)
            NH5 = D // W5
            for n in range(NT):
                xk = f"xr{n % 2}"
                p.op("sync", (lambda e, n=n, xsrc_r=xsrc_r: e.dma_start(
                    out=xr[n % 2][:, :], in_=xsrc_r[:, n, :])),
                    reads=[k for k in [xsrc_keys[n]] if k], writes=[xk],
                    dma=True)
                pbase = (n % 2) * D if 2 * D <= 2048 else 0
                for hf5 in range(NH5):
                    for ic in range(NIC):
                        p.op("pe", (lambda t, n=n, hf5=hf5, ic=ic, pb=pbase:
                                    t.matmul(
                            psm[:, pb + hf5 * W5:pb + (hf5 + 1) * W5],
                            OF[ic][:, n * 128:(n + 1) * 128],
                            wob[ic][:, hf5 * W5:(hf5 + 1) * W5],
                            start=(ic == 0), stop=(ic == NIC - 1))),
                            reads=[OFK[ic], WOK[ic]],
                            writes=[f"psm{(pbase + hf5 * W5) // 512}"])
                p.op("ve", (lambda v, n=n, pb=pbase: v.tensor_tensor(
                    yrt[:, :], psm[:, pb:pb + D], xr[n % 2][:, :], ALU.add)),
                    reads=[f"psm{(pbase + i * 512) // 512}"
                           for i in range((D + 511) // 512)] + [xk],
                    writes=["yr"])
                p.op("ve", (lambda v, n=n: v.tensor_copy(
                    ot[n % 2][:, :], yrt[:, :])),
                    reads=["yr"], writes=[f"ot{n % 2}"])
                p.op("gp", (lambda e, n=n, xc_r=xc_r: e.dma_start(
                    out=xc_r[:, n, :], in_=ot[n % 2][:, :])),
                    reads=[f"ot{n % 2}"], writes=[xc_key(s, n)], dma=True)

            p.fence(AK + EK + ["sgq2"])

            for c in range(DC):
                p.op("ve", (lambda v, c=c: v.memset(XTs[c][:, 0:1], 0.0)),
                     writes=[XTK[c]])
            for g in range(0, NT, 4):
                gl = min(4, NT - g)
                xb = xbf_e
                xbk = "xbfe"
                p.op("gp", (lambda e, xb=xb, gl=gl, g=g, xc_r=xc_r:
                            e.dma_start(
                                out=xb[:, 0:gl * D].rearrange(
                                    "p (n d) -> p n d", n=gl),
                                in_=xc_r[:, g:g + gl, :])),
                     reads=[xc_key(s, g + j) for j in range(gl)],
                     writes=[xbk], dma=True)
                for c in range(DC):
                    srcs = [(xb[:, j * D + c * 128: j * D + (c + 1) * 128],
                             xbk) for j in range(gl)]
                    par = XTs[c][:, :]
                    dsts = [(par, 1 + (g + j) * 128, XTK[c])
                            for j in range(gl)]
                    emit_transposes(ctx, srcs, dsts, pst_bf, PSTK)

            NB5 = T // 512 if T >= 512 else 1
            B5 = min(512, T)
            for t5 in range(NB5):
                c0, c1 = t5 * B5, (t5 + 1) * B5
                for c in range(DC):
                    p.op("pe", (lambda t, c=c, c0=c0, c1=c1: t.matmul(
                        pso[0:1, 0:B5], onescol[:, 0:1], xts_slice(c, c0, c1),
                        start=(c == 0), stop=(c == DC - 1))),
                        reads=[XTK[c], "c_onescol"], writes=["pso"])
                p.op("ve", lambda v: v.tensor_scalar(
                    mrow[0:1, 0:B5], pso[0:1, 0:B5], 1.0 / D, None, ALU.mult),
                    reads=["pso"], writes=["mrow"])
                for c in range(DC):
                    p.op("sc", (lambda s_, c=c, c0=c0, c1=c1: s_.activation(
                        sqf[:, 0:B5], xts_slice(c, c0, c1), AF.Square)),
                        reads=[XTK[c]], writes=["sqf"])
                    p.op("pe", (lambda t, c=c: t.matmul(
                        psb[0:1, 0:B5], onescol[:, 0:1], sqf[:, 0:B5],
                        start=(c == 0), stop=(c == DC - 1))),
                        reads=["sqf", "c_onescol"], writes=["psb"])
                p.op("ve", lambda v: v.tensor_scalar(
                    vrow[0:1, 0:B5], psb[0:1, 0:B5], 1.0 / D, None, ALU.mult),
                    reads=["psb"], writes=["vrow"])
                p.op("ve", lambda v: v.tensor_mul(
                    rrow[0:1, 0:B5], mrow[0:1, 0:B5], mrow[0:1, 0:B5]),
                    reads=["mrow"], writes=["rrow"])
                p.op("ve", lambda v: v.tensor_sub(
                    vrow[0:1, 0:B5], vrow[0:1, 0:B5], rrow[0:1, 0:B5]),
                    reads=["vrow", "rrow"], writes=["vrow"])
                p.op("sc", lambda s_: s_.activation(
                    rrow[0:1, 0:B5], vrow[0:1, 0:B5], AF.Ln,
                    bias=ctx.eps_col[0:1, 0:1]),
                    reads=["vrow", "c_eps"], writes=["rrow"])
                p.op("sc", lambda s_: s_.activation(
                    rrow[0:1, 0:B5], rrow[0:1, 0:B5], AF.Exp, scale=-0.5),
                    reads=["rrow"], writes=["rrow"])
                p.op("ve", lambda v: v.tensor_copy(
                    mrow_bf[0:1, 0:B5], mrow[0:1, 0:B5]),
                    reads=["mrow"], writes=["mrowb"])
                p.op("ve", lambda v: v.tensor_copy(
                    rrow_bf[0:1, 0:B5], rrow[0:1, 0:B5]),
                    reads=["rrow"], writes=["rrowb"])
                p.op("pe", lambda t: t.matmul(
                    pso[:, 0:B5], ones1[0:1, :], mrow_bf[0:1, 0:B5],
                    start=True, stop=True),
                    reads=["mrowb", "c_ones1"], writes=["pso"])
                p.op("ve", lambda v: v.tensor_copy(muB[:, 0:B5],
                                                   pso[:, 0:B5]),
                     reads=["pso"], writes=["muB"])
                p.op("pe", lambda t: t.matmul(
                    pso[:, 0:B5], ones1[0:1, :], rrow_bf[0:1, 0:B5],
                    start=True, stop=True),
                    reads=["rrowb", "c_ones1"], writes=["pso"])
                p.op("ve", lambda v: v.tensor_copy(rsB[:, 0:B5],
                                                   pso[:, 0:B5]),
                     reads=["pso"], writes=["rsB"])
                for c in range(DC):
                    p.op("ve", (lambda v, c=c, c0=c0, c1=c1: v.tensor_sub(
                        sqf[:, 0:B5], xts_slice(c, c0, c1), muB[:, 0:B5])),
                        reads=[XTK[c], "muB"], writes=["sqf"])
                    p.op("ve", lambda v: v.tensor_mul(
                        sqf[:, 0:B5], sqf[:, 0:B5], rsB[:, 0:B5]),
                        reads=["sqf", "rsB"], writes=["sqf"])
                    p.op("ve", (lambda v, c=c, c0=c0, c1=c1: v.tensor_scalar(
                        hn[c][:, c0:c1], sqf[:, 0:B5], lgc[:, c:c + 1],
                        lbc[:, c:c + 1], ALU.mult, ALU.add)),
                        reads=["sqf", "lgc", "lbc"], writes=[HNK[c]])

            p.fence(XTK + HGK + ["xbfe", "wcA0", "wcA1", "wcG0", "wcG1"])
            for j in range(NJ):
                ca0, ca1 = j * 128, min((j + 1) * 128, HID)
                cw = ca1 - ca0
                wa, wg = wcolA[j % 2], wcolG[j % 2]
                wak, wgk = f"wcA{j % 2}", f"wcG{j % 2}"
                w1r = w1bf[l].rearrange("(n p) f -> p n f", p=128)
                w1rd = [f"w1bf{l}_0", f"w1bf{l}_1"]
                p.op("sync", (lambda e, wa=wa, cw=cw, ca0=ca0, w1r=w1r:
                            e.dma_start(out=wa[:, 0:DC * cw].rearrange(
                                "p (n c) -> p n c", n=DC),
                                in_=w1r[:, :, ca0:ca0 + cw])),
                     reads=w1rd, writes=[wak], dma=True)
                p.op("sync", (lambda e, wg=wg, cw=cw, ca0=ca0, w1r=w1r:
                            e.dma_start(out=wg[:, 0:DC * cw].rearrange(
                                "p (n c) -> p n c", n=DC),
                                in_=w1r[:, :, HID + ca0:HID + ca0 + cw])),
                     reads=w1rd, writes=[wgk], dma=True)
                for t5 in range(NB5):
                    c0, c1 = t5 * B5, (t5 + 1) * B5
                    bA, bAk = banks[2 * (t5 % 2)]
                    bG, bGk = banks[2 * (t5 % 2) + 1]
                    for ic in range(DC):
                        p.op("pe", (lambda t, ic=ic, cw=cw, c0=c0, c1=c1,
                                    wa=wa, bA=bA: t.matmul(
                                        bA[0:cw, 0:B5],
                                        wa[:, ic * cw:(ic + 1) * cw],
                                        hn[ic][:, c0:c1],
                                        start=(ic == 0), stop=(ic == DC - 1))),
                            reads=[wak, HNK[ic]], writes=[bAk])
                        p.op("pe", (lambda t, ic=ic, cw=cw, c0=c0, c1=c1,
                                    wg=wg, bG=bG: t.matmul(
                                        bG[0:cw, 0:B5],
                                        wg[:, ic * cw:(ic + 1) * cw],
                                        hn[ic][:, c0:c1],
                                        start=(ic == 0), stop=(ic == DC - 1))),
                            reads=[wgk, HNK[ic]], writes=[bGk])
                    p.op("sc", (lambda s_, j=j, cw=cw, bG=bG: s_.activation(
                        gel[0:cw, 0:B5], bG[0:cw, 0:B5], AF.Gelu,
                        bias=b1g[0:cw, j:j + 1])),
                        reads=[bGk, "b1g"], writes=["gel"])
                    p.op("ve", (lambda v, j=j, cw=cw, c0=c0, c1=c1, bA=bA:
                                v.scalar_tensor_tensor(
                                    HGT[j][0:cw, c0:c1], bA[0:cw, 0:B5],
                                    b1a[0:cw, j:j + 1], gel[0:cw, 0:B5],
                                    ALU.add, ALU.mult)),
                        reads=[bAk, "b1a", "gel"], writes=[HGK[j]])

            for t5 in range(NB5):
                c0, c1 = t5 * B5, (t5 + 1) * B5
                for j in range(NJ):
                    ca0, ca1 = j * 128, min((j + 1) * 128, HID)
                    cw = ca1 - ca0
                    w2t = w2sb[j % 2]
                    w2k = f"w2s{j % 2}"
                    p.op("sync", (lambda e, w2t=w2t, ca0=ca0, ca1=ca1, l=l:
                                e.dma_start(out=w2t[0:ca1 - ca0, :],
                                            in_=w2bf[l][ca0:ca1, :])),
                         reads=[f"w2bf{l}"], writes=[w2k], dma=True)
                    for oc in range(DC):
                        bank, bk = banks[oc]
                        p.op("pe", (lambda t, j=j, oc=oc, cw=cw, c0=c0, c1=c1,
                                    w2t=w2t, bank=bank: t.matmul(
                                        bank[:, 0:B5],
                                        w2t[0:cw, oc * 128:(oc + 1) * 128],
                                        HGT[j][0:cw, c0:c1],
                                        start=(j == 0), stop=(j == NJ - 1))),
                            reads=[w2k, HGK[j]], writes=[bk])
                for oc in range(DC):
                    bank, bk = banks[oc]
                    p.op("ve", (lambda v, oc=oc, bank=bank: v.tensor_scalar(
                        ysb8[oc][:, 0:B5], bank[:, 0:B5], 1.0,
                        b2c[:, oc:oc + 1], ALU.mult, ALU.add)),
                        reads=[bk, "b2c"], writes=[f"ysb{oc}"])
                for tt in range(B5 // 128):
                    n = t5 * (B5 // 128) + tt
                    xk = f"xr{tt % 2}"
                    p.op("sync", (lambda e, n=n, tt=tt, xc_r=xc_r:
                                  e.dma_start(
                        out=xr[tt % 2][:, :], in_=xc_r[:, n, :])),
                        reads=[xc_key(s, n)], writes=[xk], dma=True)
                    pi = tt % 2
                    for c in range(DC):
                        p.op("pe", (lambda t, c=c, tt=tt, pi=pi: t.transpose(
                            pst_bf[pi][:, c * 128:(c + 1) * 128],
                            ysb8[c][:, tt * 128:(tt + 1) * 128],
                            ctx.ident[:, :])),
                            reads=[f"ysb{c}", "c_ident"], writes=[PSTK[pi]])
                    p.op("ve", (lambda v, pi=pi, tt=tt: v.tensor_tensor(
                        yrt[:, :], pst_bf[pi][:, 0:D], xr[tt % 2][:, :],
                        ALU.add)),
                        reads=[PSTK[pi], xk], writes=["yr"])
                    p.op("ve", (lambda v, tt=tt: v.tensor_copy(
                        ot[tt % 2][:, :], yrt[:, :])),
                        reads=["yr"], writes=[f"ot{tt % 2}"])
                    p.op("gp", (lambda e, n=n, tt=tt, xc_r=xc_r:
                                e.dma_start(
                        out=xc_r[:, n, :], in_=ot[tt % 2][:, :])),
                        reads=[f"ot{tt % 2}"], writes=[xc_key(s, n)],
                        dma=True)

        for n in range(NT):
            xk = f"xr{n % 2}"
            p.op("sync", (lambda e, n=n, x_in_r=x_in_r: e.dma_start(
                out=xr[n % 2][:, :], in_=x_in_r[:, n, :])),
                reads=[k for k in [xin_key(s, n)] if k],
                writes=[xk], dma=True)
            p.op("sync", (lambda e, n=n, xc_r=xc_r: e.dma_start(
                out=ot[n % 2][:, :], in_=xc_r[:, n, :])),
                reads=[xc_key(s, n)], writes=[f"ot{n % 2}"], dma=True)
            p.op("ve", (lambda v, n=n: v.tensor_add(
                yrt[:, :], xr[n % 2][:, :], ot[n % 2][:, :])),
                reads=[xk, f"ot{n % 2}"], writes=["yr"])
            ob, obk = ot[n % 2], f"ot{n % 2}"
            emit_ln(ctx, yrt[:, :], "yr", ob[:, :], obk, scr, D)
            p.op("gp", (lambda e, n=n, xo_r=xo_r: e.dma_start(
                out=xo_r[:, n, :], in_=ot[n % 2][:, :])),
                reads=[obk], writes=[xout_key(s, n)], dma=True)


def l1_xpos_tables(T, WW):
    DH = 64
    half_idx = np.arange(0, DH, 2, dtype=np.float64)
    inv_freq = 1.0 / 10000.0 ** (half_idx / DH)
    t = np.arange(T, dtype=np.float64)
    freqs = t[:, None] * inv_freq[None, :]
    freqs = np.concatenate([freqs, freqs], axis=-1)
    base = (half_idx + 0.4 * DH) / (1.4 * DH)
    power = (t - T // 2) / float(WW // 2)
    scale = base[None, :] ** power[:, None]
    scale = np.concatenate([scale, scale], axis=-1)
    cos, sin = np.cos(freqs), np.sin(freqs)
    c1 = (cos * scale * 8.0).astype(np.float32)
    s1 = (sin * scale * 8.0).astype(np.float32)
    c2 = (cos / scale).astype(np.float32)
    s2 = (sin / scale).astype(np.float32)
    return c1, s1, c2, s2


def build_l1_ffn_nc(cfg):
    T, D, B = cfg["T"], cfg["D"], cfg["B"]
    HF, WW, HID = cfg["HF"], cfg["WW"], cfg["HID"]
    INNER = HF * 64
    nc = bass.Bass("TRN2", target_bir_lowering=False)
    io = {
        "x_in": nc.dram_tensor("x_in", [B, T, D], F32, kind="ExternalInput").ap(),
        "wqkv": nc.dram_tensor("wqkv", [2, D, 3 * INNER], F32,
                               kind="ExternalInput").ap(),
        "qs_rep": nc.dram_tensor("qs_rep", [2, 128, INNER], F32,
                                 kind="ExternalInput").ap(),
        "ks_rep": nc.dram_tensor("ks_rep", [2, 128, INNER], F32,
                                 kind="ExternalInput").ap(),
        "wo": nc.dram_tensor("wo", [2, INNER, D], F32,
                             kind="ExternalInput").ap(),
        "ln_g": nc.dram_tensor("ln_g", [2, D], F32, kind="ExternalInput").ap(),
        "ln_b": nc.dram_tensor("ln_b", [2, D], F32, kind="ExternalInput").ap(),
        "w1": nc.dram_tensor("w1", [2, D, 2 * HID], F32,
                             kind="ExternalInput").ap(),
        "b1": nc.dram_tensor("b1", [2, 2 * HID], F32,
                             kind="ExternalInput").ap(),
        "w2": nc.dram_tensor("w2", [2, HID, D], F32,
                             kind="ExternalInput").ap(),
        "b2": nc.dram_tensor("b2", [2, D], F32, kind="ExternalInput").ap(),
        "c1": nc.dram_tensor("c1", [T, INNER], F32, kind="ExternalInput").ap(),
        "s1m": nc.dram_tensor("s1m", [T, INNER], F32,
                              kind="ExternalInput").ap(),
        "c2": nc.dram_tensor("c2", [T, INNER], F32, kind="ExternalInput").ap(),
        "s2m": nc.dram_tensor("s2m", [T, INNER], F32,
                              kind="ExternalInput").ap(),
        "ident": nc.dram_tensor("ident", [128, 128], BF16,
                                kind="ExternalInput").ap(),
        "x_out": nc.dram_tensor("x_out", [B, T, D], F32,
                                kind="ExternalOutput").ap(),
    }
    stack = contextlib.ExitStack()
    with stack:
        ctx = Ctx(nc, stack)
        emit_l1_ffn(ctx, cfg, io)
        ctx.p.emit(stack, tail="sync")
    return nc



def l0_rope_tables(T, D, base=10000.0):
    half = D // 2
    theta = 1.0 / base ** (np.arange(half, dtype=np.float32) / half)
    ang = np.arange(T, dtype=np.float32)[:, None] * theta[None, :]
    return np.cos(ang).astype(np.float32), np.sin(ang).astype(np.float32)


def ident_bf16():
    return np.eye(128, dtype=np.float32).astype(BF16NP)



def build_l0_attn_nc(cfg):
    T, D, B = cfg["T"], cfg["D"], cfg["B"]
    nc = bass.Bass("TRN2", target_bir_lowering=False)
    io = {
        "x_in": nc.dram_tensor("x_in", [B, T, D], F32, kind="ExternalInput").ap(),
        "wq": nc.dram_tensor("wq", [D, D], F32, kind="ExternalInput").ap(),
        "wk": nc.dram_tensor("wk", [D, D], F32, kind="ExternalInput").ap(),
        "wv": nc.dram_tensor("wv", [D, D], F32, kind="ExternalInput").ap(),
        "wu": nc.dram_tensor("wu", [D, D], F32, kind="ExternalInput").ap(),
        "bu": nc.dram_tensor("bu", [D], F32, kind="ExternalInput").ap(),
        "rope_cos": nc.dram_tensor("rope_cos", [T, D // 2], F32,
                                   kind="ExternalInput").ap(),
        "rope_sin": nc.dram_tensor("rope_sin", [T, D // 2], F32,
                                   kind="ExternalInput").ap(),
        "ident": nc.dram_tensor("ident", [128, 128], BF16,
                                kind="ExternalInput").ap(),
        "x_out": nc.dram_tensor("x_out", [B, T, D], F32,
                                kind="ExternalOutput").ap(),
    }
    stack = contextlib.ExitStack()
    with stack:
        ctx = Ctx(nc, stack)
        emit_l0_attn(ctx, cfg, io)
        ctx.p.emit(stack, tail="sync")
    return nc


def build_full_nc(fcfg):
    T, D, B, H = fcfg["T"], fcfg["D"], fcfg["B"], fcfg["H"]
    F1, F2 = fcfg["F1"], fcfg["F2"]
    HF, WW, HID = fcfg["HF"], fcfg["WW"], fcfg["HID"]
    INNER = HF * 64
    nc = bass.Bass("TRN2", target_bir_lowering=False)

    def gi(name, shape, dtype=F32):
        return nc.dram_tensor(name, shape, dtype, kind="ExternalInput").ap()

    x_in = gi("x_in", [B, T, D])
    x_out = nc.dram_tensor("x_out", [B, T, D], F32,
                           kind="ExternalOutput").ap()
    xh = [nc.dram_tensor(f"xh{i}", [B, T, D], F32, kind="Internal").ap()
          for i in range(3)]
    ident = gi("ident", [128, 128], BF16)

    def keyf(name):
        return lambda s, n: f"{name}@{s}_{n}"

    io0 = dict(x_in=x_in, x_out=xh[0], ident=ident,
               xin_key=None, xout_key=keyf("xh0"),
               wq=gi("wq", [D, D]), wk=gi("wk", [D, D]), wv=gi("wv", [D, D]),
               wu=gi("wu", [D, D]), bu=gi("bu", [D]),
               rope_cos=gi("rope_cos", [T, D // 2]),
               rope_sin=gi("rope_sin", [T, D // 2]))
    io1 = dict(x_in=xh[0], x_out=xh[1], ident=ident,
               xin_key=keyf("xh0"), xout_key=keyf("xh1"),
               k0=gi("k0", [2 * D, F1], U8), k1=gi("k1", [F1, F2], U8),
               k2=gi("k2", [F2, D], U8), s0c=gi("s0c", [128, 1]),
               s1c=gi("s1c", [128, 1]), s2c=gi("s2c", [128, 1]))
    io2 = dict(x_in=xh[1], x_out=xh[2], ident=ident,
               xin_key=keyf("xh1"), xout_key=keyf("xh2"),
               wkqv=gi("wkqv", [D, 3 * D]), bkqv=gi("bkqv", [3 * D]),
               wproj=gi("wproj", [D, D]), bproj=gi("bproj", [D]),
               slopes_bc=gi("slopes_bc", [128, H]))
    io3 = dict(x_in=xh[2], x_out=x_out, ident=ident,
               xin_key=keyf("xh2"), xout_key=keyf("x_out"),
               wqkv=gi("wqkv", [2, D, 3 * INNER]),
               qs_rep=gi("qs_rep", [2, 128, INNER]),
               ks_rep=gi("ks_rep", [2, 128, INNER]),
               wo=gi("wo", [2, INNER, D]), ln_g=gi("ln_g", [2, D]),
               ln_b=gi("ln_b", [2, D]), w1=gi("w1", [2, D, 2 * HID]),
               b1=gi("b1", [2, 2 * HID]), w2=gi("w2", [2, HID, D]),
               b2=gi("b2", [2, D]),
               c1=gi("c1", [T, INNER]), s1m=gi("s1m", [T, INNER]),
               c2=gi("c2", [T, INNER]), s2m=gi("s2m", [T, INNER]))

    cfg_a = dict(T=T, D=D, H=H, B=B)
    cfg_f = dict(T=T, D=D, B=B, F1=F1, F2=F2)
    cfg_l = dict(T=T, D=D, B=B, HF=HF, WW=WW, HID=HID)

    stack = contextlib.ExitStack()
    with stack:
        ctx = Ctx(nc, stack)
        ctx.enable_shared(stack)
        k0bf = nc.dram_tensor("F0_k0bf", [2 * D, F1], BF16,
                              kind="Internal").ap()
        for q in range(4):
            r0, r1 = q * 2 * D // 4, (q + 1) * 2 * D // 4
            ctx.p.dma("gp", k0bf[r0:r1, :], io1["k0"][r0:r1, :],
                      writes=[f"k0bf{q}"])
        k1bf = nc.dram_tensor("F0_k1bf", [F1, F2], BF16, kind="Internal").ap()
        k2bf = nc.dram_tensor("F0_k2bf", [F2, D], BF16, kind="Internal").ap()
        for q in range(4):
            r0, r1 = q * F1 // 4, (q + 1) * F1 // 4
            ctx.p.dma("gp", k1bf[r0:r1, :], io1["k1"][r0:r1, :],
                      writes=[f"k1bf{q}"])
            r0, r1 = q * F2 // 4, (q + 1) * F2 // 4
            ctx.p.dma("gp", k2bf[r0:r1, :], io1["k2"][r0:r1, :],
                      writes=[f"k2bf{q}"])
        w1bf = nc.dram_tensor("F1_w1bf", [2, D, 2 * HID], BF16,
                              kind="Internal").ap()
        w2bf = nc.dram_tensor("F1_w2bf", [2, HID, D], BF16,
                              kind="Internal").ap()
        for l in range(2):
            for q in range(2):
                r0, r1 = q * D // 2, (q + 1) * D // 2
                ctx.p.dma("gp", w1bf[l][r0:r1, :], io3["w1"][l][r0:r1, :],
                          writes=[f"w1bf{l}_{q}"])
            ctx.p.dma("gp", w2bf[l], io3["w2"][l], writes=[f"w2bf{l}"])
        ctx.pre = dict(k0bf=k0bf, k1bf=k1bf, k2bf=k2bf,
                       w1bf=w1bf, w2bf=w2bf)
        blocks = [("A0", emit_l0_attn, cfg_a, io0),
                  ("F0", emit_l0_ffn, cfg_f, io1),
                  ("A1", emit_l1_attn, cfg_a, io2),
                  ("F1", emit_l1_ffn, cfg_l, io3)]
        for name, emitter, cfg, io in blocks:
            ctx.block_reset()
            ctx.prefix = name + "_"
            emitter(ctx, cfg, io)
            ctx.p.fence_all()
        ctx.p.emit(stack, tail="sync")
    return nc


_CFG0 = dict(T=2048, D=1024, H=16, B=2)
_CFG0F = dict(T=2048, D=1024, B=2, F1=4096, F2=4096)
_CFG1 = dict(T=2048, D=1024, H=16, B=2)
_CFG1F = dict(T=2048, D=1024, B=2, HF=8, WW=512, HID=2730)
_NCORE = 8

_BUILT = {}


def _get_nc(name):
    if name not in _BUILT:
        if name == "l0a":
            _BUILT[name] = build_l0_attn_nc(_CFG0)
        elif name == "l0f":
            _BUILT[name] = build_l0_ffn_nc(_CFG0F)
        elif name == "l1a":
            _BUILT[name] = build_l1_attn_nc(_CFG1)
        elif name == "l1f":
            _BUILT[name] = build_l1_ffn_nc(_CFG1F)
    return _BUILT[name]


def _col(v):
    return np.ascontiguousarray(
        np.broadcast_to(np.float32(v).reshape(1, 1), (128, 1)))


_FCFG = dict(T=2048, D=1024, B=2, H=16, F1=4096, F2=4096, HF=8, WW=512,
             HID=2730)


def _full_common(inputs):
    T, D, H = 2048, 1024, 16
    HF, WW = 8, 512
    INNER = HF * 64
    cos, sin = l0_rope_tables(T, D)
    c1, s1, c2, s2 = l1_xpos_tables(T, WW)
    slopes = np.asarray(inputs["L1_attn_slopes"], dtype=np.float32)
    qs = np.asarray(inputs["L1_ffn_q_scale"], dtype=np.float32)
    ks = np.asarray(inputs["L1_ffn_k_scale"], dtype=np.float32)
    qs_rep = np.ascontiguousarray(np.stack(
        [np.broadcast_to(np.tile(qs[l], HF)[None, :], (128, INNER))
         for l in range(2)]))
    ks_rep = np.ascontiguousarray(np.stack(
        [np.broadcast_to(np.tile(ks[l], HF)[None, :], (128, INNER))
         for l in range(2)]))
    return dict(
        ident=ident_bf16(), rope_cos=cos, rope_sin=sin,
        wq=inputs["L0_attn_Wq"], wk=inputs["L0_attn_Wk"],
        wv=inputs["L0_attn_Wv"], wu=inputs["L0_attn_Wu"],
        bu=inputs["L0_attn_bu"],
        k0=np.ascontiguousarray(inputs["L0_ffn_k0"]).view(np.uint8),
        k1=np.ascontiguousarray(inputs["L0_ffn_k1"]).view(np.uint8),
        k2=np.ascontiguousarray(inputs["L0_ffn_k2"]).view(np.uint8),
        s0c=_col(inputs["L0_ffn_s0"]), s1c=_col(inputs["L0_ffn_s1"]),
        s2c=_col(inputs["L0_ffn_s2"]),
        wkqv=inputs["L1_attn_W_kqv"], bkqv=inputs["L1_attn_b_kqv"],
        wproj=inputs["L1_attn_W_proj"], bproj=inputs["L1_attn_b_proj"],
        slopes_bc=np.ascontiguousarray(
            np.broadcast_to(slopes[None, :], (128, H))),
        wqkv=inputs["L1_ffn_wqkv"], qs_rep=qs_rep, ks_rep=ks_rep,
        wo=inputs["L1_ffn_wo"], ln_g=inputs["L1_ffn_ln_g"],
        ln_b=inputs["L1_ffn_ln_b"], w1=inputs["L1_ffn_w1"],
        b1=inputs["L1_ffn_b1"], w2=inputs["L1_ffn_w2"],
        b2=inputs["L1_ffn_b2"],
        c1=np.ascontiguousarray(np.tile(c1, (1, HF))),
        s1m=np.ascontiguousarray(np.tile(s1, (1, HF))),
        c2=np.ascontiguousarray(np.tile(c2, (1, HF))),
        s2m=np.ascontiguousarray(np.tile(s2, (1, HF))))


def kernel(**inputs):
    from concourse.bass_utils import run_bass_kernel_spmd
    T, D = 2048, 1024
    NC = _NCORE
    x = np.ascontiguousarray(inputs["L0_attn_x"])
    shards = np.split(x, NC, axis=0)
    if "full" not in _BUILT:
        _BUILT["full"] = build_full_nc(_FCFG)
    nc = _BUILT["full"]
    common = _full_common(inputs)
    res = run_bass_kernel_spmd(
        nc, [dict(x_in=s, **common) for s in shards], core_ids=list(range(NC)))
    shards = [r["x_out"].reshape(2, T, D) for r in res.results]
    return np.concatenate(shards, axis=0).astype(np.float32)


def kernel_unfused(**inputs):
    from concourse.bass_utils import run_bass_kernel_spmd
    T, D, H = 2048, 1024, 16
    NC = _NCORE
    x = np.ascontiguousarray(inputs["L0_attn_x"])
    shards = np.split(x, NC, axis=0)
    ident = ident_bf16()
    cos, sin = l0_rope_tables(T, D)

    nc = _get_nc("l0a")
    common = dict(wq=inputs["L0_attn_Wq"], wk=inputs["L0_attn_Wk"],
                  wv=inputs["L0_attn_Wv"], wu=inputs["L0_attn_Wu"],
                  bu=inputs["L0_attn_bu"], rope_cos=cos, rope_sin=sin,
                  ident=ident)
    res = run_bass_kernel_spmd(
        nc, [dict(x_in=s, **common) for s in shards], core_ids=list(range(NC)))
    shards = [r["x_out"].reshape(2, T, D) for r in res.results]

    nc = _get_nc("l0f")
    common = dict(k0=np.ascontiguousarray(inputs["L0_ffn_k0"]).view(np.uint8),
                  k1=np.ascontiguousarray(inputs["L0_ffn_k1"]).view(np.uint8),
                  k2=np.ascontiguousarray(inputs["L0_ffn_k2"]).view(np.uint8),
                  s0c=_col(inputs["L0_ffn_s0"]), s1c=_col(inputs["L0_ffn_s1"]),
                  s2c=_col(inputs["L0_ffn_s2"]), ident=ident)
    res = run_bass_kernel_spmd(
        nc, [dict(x_in=s, **common) for s in shards], core_ids=list(range(NC)))
    shards = [r["x_out"].reshape(2, T, D) for r in res.results]

    nc = _get_nc("l1a")
    slopes = np.asarray(inputs["L1_attn_slopes"], dtype=np.float32)
    common = dict(wkqv=inputs["L1_attn_W_kqv"], bkqv=inputs["L1_attn_b_kqv"],
                  wproj=inputs["L1_attn_W_proj"],
                  bproj=inputs["L1_attn_b_proj"],
                  slopes_bc=np.ascontiguousarray(
                      np.broadcast_to(slopes[None, :], (128, H))),
                  ident=ident)
    res = run_bass_kernel_spmd(
        nc, [dict(x_in=s, **common) for s in shards], core_ids=list(range(NC)))
    shards = [r["x_out"].reshape(2, T, D) for r in res.results]

    nc = _get_nc("l1f")
    HF, WW = 8, 512
    INNER = HF * 64
    c1, s1, c2, s2 = l1_xpos_tables(T, WW)
    qs = np.asarray(inputs["L1_ffn_q_scale"], dtype=np.float32)
    ks = np.asarray(inputs["L1_ffn_k_scale"], dtype=np.float32)
    qs_rep = np.ascontiguousarray(np.stack(
        [np.broadcast_to(np.tile(qs[l], HF)[None, :], (128, INNER))
         for l in range(2)]))
    ks_rep = np.ascontiguousarray(np.stack(
        [np.broadcast_to(np.tile(ks[l], HF)[None, :], (128, INNER))
         for l in range(2)]))
    common = dict(wqkv=inputs["L1_ffn_wqkv"], qs_rep=qs_rep, ks_rep=ks_rep,
                  wo=inputs["L1_ffn_wo"], ln_g=inputs["L1_ffn_ln_g"],
                  ln_b=inputs["L1_ffn_ln_b"], w1=inputs["L1_ffn_w1"],
                  b1=inputs["L1_ffn_b1"], w2=inputs["L1_ffn_w2"],
                  b2=inputs["L1_ffn_b2"],
                  c1=np.ascontiguousarray(np.tile(c1, (1, HF))),
                  s1m=np.ascontiguousarray(np.tile(s1, (1, HF))),
                  c2=np.ascontiguousarray(np.tile(c2, (1, HF))),
                  s2m=np.ascontiguousarray(np.tile(s2, (1, HF))),
                  ident=ident)
    res = run_bass_kernel_spmd(
        nc, [dict(x_in=s, **common) for s in shards], core_ids=list(range(NC)))
    shards = [r["x_out"].reshape(2, T, D) for r in res.results]

    return np.concatenate(shards, axis=0).astype(np.float32)
